# version 05
# baseline (speedup 1.0000x reference)
import contextlib
import numpy as np
import ml_dtypes
import concourse.bass as bass
import concourse.mybir as mybir
from concourse.bass_utils import run_bass_kernel_spmd

F32 = mybir.dt.float32
BF16 = mybir.dt.bfloat16
I32 = mybir.dt.int32
AF = mybir.ActivationFunctionType
ALU = mybir.AluOpType
AX = mybir.AxisListType.X
BF = ml_dtypes.bfloat16

D = 2048
TPC = 2048
NTB = 4
TBL = 512
NDC = 16
NCORE = 8
GRP = 4

INTER0 = 5632
NIC0 = 44

BS = 512
NCH = TPC // BS

NH, NKV, HS, WIN, SINK = 16, 4, 128, 1024, 4
QT = 512
NQT = TPC // QT
KBUF = WIN + QT
NKC = KBUF // 128
SCALE_ATT = 1.0 / float(np.sqrt(np.float32(HS)))
ESHIFT = -12.0
EM12 = float(np.exp(np.float64(ESHIFT)))

NE = 8
CAP = 2048
ESTR = 2056
ZROW = NE * ESTR
CGLB = 5120
MEXP = 2816
IEXP = 1408
JSH = 5632

LN_EPS = 1e-5
RMS_EPS = float(np.finfo(np.float32).eps)


def _bfh(x):
    return np.ascontiguousarray(x).astype(BF)


def ecopy(e, out, in_):
    if hasattr(e, "tensor_copy"):
        return e.tensor_copy(out, in_)
    return e.copy(out, in_)



class Pool:
    NAMES = ("ld", "st", "mm", "dve", "act", "cc", "gp", "ch")

    def __init__(self, nc, stack):
        self.nc = nc
        self.sem = {}
        self.n = {}
        for nm in self.NAMES:
            self.sem[nm] = stack.enter_context(nc.semaphore(f"P_{nm}"))
            self.n[nm] = 0


def prog(build_fn):
    def wrapper(*a, **kw):
        nc = bass.Bass("TRN2", target_bir_lowering=False)
        stack = contextlib.ExitStack()
        P = Pool(nc, stack)
        with stack:
            build_fn(nc, P, *a, **kw)
        return nc
    return wrapper


ENG2CTR = {"vector": "dve", "scalar": "act"}


class Ch:

    def __init__(self, eng, P, nm="ch"):
        self._e = eng
        self._P = P
        self._nm = nm

    def __getattr__(self, a):
        f = getattr(self._e, a)

        def w(*args, **kw):
            inst = f(*args, **kw)
            self._P.n[self._nm] += 1
            inst.then_inc(self._P.sem[self._nm], 1)
            self._e.wait_ge(self._P.sem[self._nm], self._P.n[self._nm])
            return inst
        return w



def simple_block(nc, name, **bodies):
    with nc.Block() as blk:
        for eng, fn in bodies.items():
            getattr(blk, eng)(fn)


def load_block(nc, P, pairs, engine="gpsimd"):
    sem = P.sem["ld"]
    with nc.Block() as blk:
        def body(g):
            for (sb, dr) in pairs:
                g.dma_start(out=sb, in_=dr).then_inc(sem, 16)
                P.n["ld"] += 16
            g.wait_ge(sem, P.n["ld"])
        getattr(blk, engine)(body)


def store_block(nc, P, pairs, engine="sync"):
    sem = P.sem["st"]
    with nc.Block() as blk:
        def body(g):
            for (dr, sb) in pairs:
                g.dma_start(out=dr, in_=sb).then_inc(sem, 16)
                P.n["st"] += 16
            g.wait_ge(sem, P.n["st"])
        getattr(blk, engine)(body)


class Job:
    def __init__(self, kchunks, evac, store=None, loads=None, pw=128, ld_gate=None,
                 ow=None):
        self.kchunks = kchunks
        self.evac = evac
        self.store = store
        self.loads = loads or []
        self.pw = pw
        self.ld_gate = ld_gate
        self.ow = ow


def pipe_mm(nc, P, name, jobs, out_w, out_dt=BF16, nps=2, nout=4,
            evac_engine="scalar", ps_w=None, ld_ahead=4, persistent_out=None):
    ps_w = ps_w or out_w
    n = len(jobs)
    cumld = np.cumsum([0] + [len(j.loads) for j in jobs])
    b_ld, b_mm, b_st = P.n["ld"], P.n["mm"], P.n["st"]
    ec = ENG2CTR[evac_engine]
    b_ev = P.n[ec]
    nstores = [0] * (n + 1)
    for j, job in enumerate(jobs):
        nstores[j + 1] = nstores[j] + (1 if job.store is not None else 0)

    stack = contextlib.ExitStack()
    stage = stack.enter_context(
        nc.sbuf_tensor(f"{name}_stage", [128, max(1, nout) * out_w], out_dt))
    pss = [stack.enter_context(nc.psum_tensor(f"{name}_ps{i}", [128, ps_w], F32))
           for i in range(nps)]
    with stack, nc.Block() as blk:

        @blk.gpsimd
        def _(g):
            for j, job in enumerate(jobs):
                if not job.loads:
                    continue
                gate = job.ld_gate if job.ld_gate is not None else j - ld_ahead
                if gate > 0:
                    g.wait_ge(P.sem[ec], b_ev + gate)
                if cumld[j] > 0:
                    g.wait_ge(P.sem["ld"], b_ld + 16 * int(cumld[j]))
                for (sb, dr) in job.loads:
                    g.dma_start(out=sb, in_=dr).then_inc(P.sem["ld"], 16)
        P.n["ld"] = b_ld + 16 * int(cumld[n])

        @blk.tensor
        def _(t):
            for j, job in enumerate(jobs):
                if cumld[j + 1] > 0:
                    t.wait_ge(P.sem["ld"], b_ld + 16 * int(cumld[j + 1]))
                if j >= nps:
                    t.wait_ge(P.sem[ec], b_ev + j - nps + 1)
                ps = pss[j % nps]
                nk = len(job.kchunks)
                ow = job.ow or out_w
                last = None
                for kk, (lh, rh) in enumerate(job.kchunks):
                    last = t.matmul(ps[0:job.pw, 0:ow], lh, rh,
                                    start=(kk == 0), stop=(kk == nk - 1))
                last.then_inc(P.sem["mm"], 1)
        P.n["mm"] = b_mm + n

        def evac_body(e):
            for j, job in enumerate(jobs):
                e.wait_ge(P.sem["mm"], b_mm + j + 1)
                ow = job.ow or out_w
                if persistent_out is not None:
                    out_ap = persistent_out(j)
                else:
                    if j >= nout:
                        e.wait_ge(P.sem["st"], b_st + 16 * nstores[j - nout + 1])
                    out_ap = stage[0:job.pw,
                                   (j % nout) * out_w:(j % nout) * out_w + ow]
                job.evac(e, pss[j % nps][0:job.pw, 0:ow], out_ap) \
                    .then_inc(P.sem[ec], 1)

        if evac_engine == "scalar":
            @blk.scalar
            def _(s):
                evac_body(s)
        else:
            @blk.vector
            def _(v):
                evac_body(v)
        P.n[ec] = b_ev + n

        @blk.sync
        def _(sy):
            for j, job in enumerate(jobs):
                if job.store is None:
                    continue
                sy.wait_ge(P.sem[ec], b_ev + j + 1)
                if nstores[j] > 0:
                    sy.wait_ge(P.sem["st"], b_st + 16 * nstores[j])
                sy.dma_start(
                    out=job.store,
                    in_=stage[0:job.pw,
                              (j % nout) * out_w:(j % nout) * out_w + (job.ow or out_w)],
                ).then_inc(P.sem["st"], 16)
            sy.wait_ge(P.sem["st"], b_st + 16 * nstores[n])
            sy.wait_ge(P.sem[ec], b_ev + n)
        P.n["st"] = b_st + 16 * nstores[n]



def csrc(dram, c0, nchunk, f0, w):
    return dram.ap().rearrange("(c p) f -> p c f", p=128)[:, c0:c0 + nchunk, f0:f0 + w]


def wstrip_ap(w_dram, n_in_c, oc0, width):
    return w_dram.ap().rearrange("(c p) o -> p c o", p=128)[:, :, oc0:oc0 + width]



def emit_ln(nc, P, x_dram, outs, ln_pref="ln"):
    stack = contextlib.ExitStack()
    xt = stack.enter_context(nc.sbuf_tensor(ln_pref + "_xt", [128, NDC * TBL], BF16))
    sq = stack.enter_context(nc.sbuf_tensor(ln_pref + "_sq", [128, NDC * TBL], BF16))
    ones_col = stack.enter_context(nc.sbuf_tensor(ln_pref + "_oc", [128, 1], BF16))
    ones_row = stack.enter_context(nc.sbuf_tensor(ln_pref + "_or", [1, 128], F32))
    rows = stack.enter_context(nc.sbuf_tensor(ln_pref + "_rows", [1, 10 * TBL], F32))
    bca = stack.enter_context(nc.sbuf_tensor(ln_pref + "_bca", [128, 4 * TBL], F32))
    outt = stack.enter_context(
        nc.sbuf_tensor(ln_pref + "_outt", [128, len(outs) * NDC * TBL], BF16))
    tmp = stack.enter_context(nc.sbuf_tensor(ln_pref + "_tmp", [128, TBL], F32))

    def r(i):
        return rows[:, i * TBL:(i + 1) * TBL]
    SUM, SSQ, MEAN, VAR, RSTD, R2, A2, B1, B2, TMPR = range(10)

    with stack:
        def init(g):
            g.memset(ones_col[:, :], 1.0)
            g.memset(ones_row[:, :], 1.0)
        simple_block(nc, ln_pref + "_init", gpsimd=init)

        for b in range(NTB):
            load_block(nc, P, [
                (xt[:, :].rearrange("p (c t) -> p c t", c=NDC),
                 csrc(x_dram, 0, NDC, b * TBL, TBL))])

            def sqb(s):
                for c in range(NDC):
                    s.activation(sq[:, c * TBL:(c + 1) * TBL],
                                 xt[:, c * TBL:(c + 1) * TBL], AF.Square)
            simple_block(nc, f"{ln_pref}_sq{b}", scalar=sqb)

            with nc.psum_tensor(f"{ln_pref}_ps{b}", [1, 2 * TBL], F32) as ps:
                def mmb(t):
                    for c in range(NDC):
                        t.matmul(ps[:, 0:TBL], ones_col[:, :],
                                 xt[:, c * TBL:(c + 1) * TBL],
                                 start=(c == 0), stop=(c == NDC - 1))
                        t.matmul(ps[:, TBL:2 * TBL], ones_col[:, :],
                                 sq[:, c * TBL:(c + 1) * TBL],
                                 start=(c == 0), stop=(c == NDC - 1))
                simple_block(nc, f"{ln_pref}_mm{b}", tensor=mmb)

                def r1(v):
                    v = Ch(v, P)
                    v.tensor_scalar_mul(r(MEAN), ps[:, 0:TBL], 1.0 / D)
                    v.tensor_scalar_mul(r(SSQ), ps[:, TBL:2 * TBL], 1.0 / D)
                    v.tensor_mul(r(VAR), r(MEAN), r(MEAN))
                    v.tensor_sub(r(VAR), r(SSQ), r(VAR))
                    v.tensor_scalar_add(r(TMPR), r(VAR), LN_EPS)
                    v.reciprocal(r(TMPR), r(TMPR))
                simple_block(nc, f"{ln_pref}_r1{b}", vector=r1)

            simple_block(nc, f"{ln_pref}_r2{b}", scalar=lambda s: (
                s.activation(r(RSTD), r(TMPR), AF.Sqrt),))

            def r3(v):
                v = Ch(v, P)
                v.tensor_mul(r(TMPR), r(RSTD), r(RSTD))
                v.tensor_mul(r(TMPR), r(TMPR), r(VAR))
                v.tensor_scalar_add(r(TMPR), r(TMPR), RMS_EPS)
                v.reciprocal(r(TMPR), r(TMPR))
            simple_block(nc, f"{ln_pref}_r3{b}", vector=r3)
            simple_block(nc, f"{ln_pref}_r4{b}", scalar=lambda s: (
                s.activation(r(R2), r(TMPR), AF.Sqrt),))

            def r5(v):
                v = Ch(v, P)
                v.tensor_mul(r(B1), r(MEAN), r(RSTD))
                v.tensor_scalar_mul(r(B1), r(B1), -1.0)
                v.tensor_mul(r(A2), r(RSTD), r(R2))
                v.tensor_mul(r(B2), r(B1), r(R2))
            simple_block(nc, f"{ln_pref}_r5{b}", vector=r5)

            with nc.psum_tensor(f"{ln_pref}_pbc{b}", [128, 4 * TBL], F32) as pbc:
                def bcb(t):
                    for i, idx in enumerate([RSTD, B1, A2, B2]):
                        t.matmul(pbc[:, i * TBL:(i + 1) * TBL], ones_row[:, :],
                                 r(idx), start=True, stop=True)
                simple_block(nc, f"{ln_pref}_bc{b}", tensor=bcb)
                simple_block(nc, f"{ln_pref}_ev{b}", vector=lambda v: (
                    v.tensor_copy(bca[:, :], pbc[:, :]),))

            with nc.Block() as blk:
                b_dve = P.n["dve"]

                def nrm(v):
                    for oi, (odr, mode) in enumerate(outs):
                        a0 = 0 if mode == "ln" else 2
                        ac = bca[:, a0 * TBL:(a0 + 1) * TBL]
                        bc = bca[:, (a0 + 1) * TBL:(a0 + 2) * TBL]
                        for c in range(NDC):
                            v.tensor_tensor(tmp[:, :], xt[:, c * TBL:(c + 1) * TBL],
                                            ac, ALU.mult)
                            v.tensor_tensor(
                                outt[:, (oi * NDC + c) * TBL:(oi * NDC + c + 1) * TBL],
                                tmp[:, :], bc, ALU.add).then_inc(P.sem["dve"], 1)
                            P.n["dve"] += 1
                blk.vector(nrm)

                def nst(sy):
                    for oi, (odr, mode) in enumerate(outs):
                        sy.wait_ge(P.sem["dve"], b_dve + (oi + 1) * NDC)
                        if oi > 0:
                            sy.wait_ge(P.sem["st"], P.n["st"])
                        sy.dma_start(
                            out=csrc(odr, 0, NDC, b * TBL, TBL),
                            in_=outt[:, oi * NDC * TBL:(oi + 1) * NDC * TBL]
                            .rearrange("p (c t) -> p c t", c=NDC),
                        ).then_inc(P.sem["st"], 16)
                        P.n["st"] += 16
                    sy.wait_ge(P.sem["st"], P.n["st"])
                blk.sync(nst)


def emit_phase_ln0(nc, P, dr):
    x = dr("x", [D, TPC], F32, kind="ExternalInput")
    xn = dr("xn", [D, TPC], BF16, kind="ExternalOutput")
    xnr = dr("xnr", [D, TPC], BF16, kind="ExternalOutput")
    emit_ln(nc, P, x, [(xn, "ln"), (xnr, "lnrms")])



def emit_proj_fm(nc, P, name, xin_dram, w_dram, out_dram, act, n_in_c, n_out_c,
                 scale_dram=None, res_dram=None, out_dt=BF16, wsrc_off=0,
                 tok_cols=TPC, res_dt=F32):
    K = n_in_c
    ntb = tok_cols // TBL
    stack = contextlib.ExitStack()
    xin = stack.enter_context(nc.sbuf_tensor(f"{name}_x", [128, K * ntb * TBL], BF16))
    wst = stack.enter_context(nc.sbuf_tensor(f"{name}_w", [128, 8 * K * 128], BF16))
    scl = stack.enter_context(nc.sbuf_tensor(f"{name}_s", [128, max(1, n_out_c)], F32))
    res = stack.enter_context(
        nc.sbuf_tensor(f"{name}_r", [128, 8 * TBL], res_dt)) \
        if res_dram is not None else None
    with stack:
        pairs = [(xin[:, :].rearrange("p (c b t) -> p c b t", c=K, b=ntb)[:, :, b, :],
                  csrc(xin_dram, 0, K, b * TBL, TBL))
                 for b in range(ntb)]
        if scale_dram is not None:
            pairs += [(scl[:, oc:oc + 1], scale_dram[oc * 128:(oc + 1) * 128, 0:1])
                      for oc in range(n_out_c)]
        load_block(nc, P, pairs)

        def xi(c, b):
            return xin[:, (c * ntb + b) * TBL:(c * ntb + b + 1) * TBL]

        jobs = []
        for oc in range(n_out_c):
            wsl = oc % 8
            for b in range(ntb):
                loads = []
                if b == 0:
                    loads.append((
                        wst[:, wsl * K * 128:(wsl + 1) * K * 128]
                        .rearrange("p (c o) -> p c o", c=K),
                        wstrip_ap(w_dram, K, wsrc_off + oc * 128, 128)))
                if res_dram is not None:
                    rsl = (oc * ntb + b) % 8
                    loads.append((res[:, rsl * TBL:(rsl + 1) * TBL],
                                  res_dram[oc * 128:(oc + 1) * 128,
                                           b * TBL:(b + 1) * TBL]))

                def mk_evac(oc=oc, b=b):
                    rsl = (oc * ntb + b) % 8

                    def evac(e, ps, out):
                        if res_dram is not None:
                            sc = scl[:, oc:oc + 1] if scale_dram is not None else 1.0
                            return e.scalar_tensor_tensor(
                                out, ps, sc, res[:, rsl * TBL:(rsl + 1) * TBL],
                                ALU.mult, ALU.add)
                        if scale_dram is not None and act is None:
                            return e.tensor_scalar_mul(out, ps, scl[:, oc:oc + 1])
                        if scale_dram is not None:
                            return e.activation(out, ps, act, scale=scl[:, oc:oc + 1])
                        if act is not None:
                            return e.activation(out, ps, act)
                        return ecopy(e, out, ps)
                    return evac

                jobs.append(Job(
                    kchunks=[(wst[:, (wsl * K + c) * 128:(wsl * K + c + 1) * 128],
                              xi(c, b)) for c in range(K)],
                    evac=mk_evac(),
                    store=out_dram[oc * 128:(oc + 1) * 128, b * TBL:(b + 1) * TBL],
                    loads=loads))
        ee = "vector" if (res_dram is not None
                          or (scale_dram is not None and act is None)) else "scalar"
        pipe_mm(nc, P, name, jobs, TBL, out_dt=out_dt, nps=2, nout=4,
                evac_engine=ee, ld_ahead=4)


def emit_proj_tm(nc, P, name, xin_dram, w_dram, out_dram, act, n_in_c, out_total):
    K = n_in_c
    NOB = (out_total + 511) // 512
    stack = contextlib.ExitStack()
    xin = stack.enter_context(nc.sbuf_tensor(f"{name}_x", [128, K * NTB * TBL], BF16))
    wst = stack.enter_context(nc.sbuf_tensor(f"{name}_w", [128, K * out_total], BF16))
    with stack:
        pairs = [(xin[:, :].rearrange("p (c b t) -> p c b t", c=K, b=NTB)[:, :, b, :],
                  csrc(xin_dram, 0, K, b * TBL, TBL))
                 for b in range(NTB)]
        pairs += [(wst[:, :].rearrange("p (c o) -> p c o", c=K)[:, 4 * q:4 * (q + 1), :],
                   csrc(w_dram, 4 * q, 4, 0, out_total)) for q in range(K // 4)]
        load_block(nc, P, pairs)

        def xi(c, tc):
            b, sub = tc // 4, tc % 4
            base = (c * NTB + b) * TBL + sub * 128
            return xin[:, base:base + 128]

        def evac(e, ps, out):
            if act is not None:
                return e.activation(out, ps, act)
            return ecopy(e, out, ps)

        jobs = []
        for tc in range(TPC // 128):
            for ob in range(NOB):
                ow = min(512, out_total - ob * 512)
                jobs.append(Job(
                    kchunks=[(xi(c, tc),
                              wst[:, c * out_total + ob * 512:c * out_total + ob * 512 + ow])
                             for c in range(K)],
                    evac=evac,
                    store=out_dram[tc * 128:(tc + 1) * 128, ob * 512:ob * 512 + ow]))
        pipe_mm(nc, P, name, jobs, 512, out_dt=BF16, nps=2, nout=4,
                evac_engine="scalar")


def emit_phase_qkvg(nc, P, dr):
    xnr = dr("xnr", [D, TPC], BF16, kind="ExternalInput")
    wq = dr("wq", [D, D], F32, kind="ExternalInput")
    wk = dr("wk", [D, D], F32, kind="ExternalInput")
    wv = dr("wv", [D, D], F32, kind="ExternalInput")
    wg = dr("wg", [D, D], F32, kind="ExternalInput")
    q = dr("q", [D, TPC], BF16, kind="ExternalOutput")
    k = dr("k", [D, TPC], BF16, kind="ExternalOutput")
    g = dr("g", [D, TPC], BF16, kind="ExternalOutput")
    v = dr("v", [TPC, D], BF16, kind="ExternalOutput")
    emit_proj_fm(nc, P, "pq", xnr, wq, q, AF.Silu, NDC, NDC)
    emit_proj_fm(nc, P, "pk", xnr, wk, k, AF.Silu, NDC, NDC)
    emit_proj_fm(nc, P, "pg", xnr, wg, g, AF.Sigmoid, NDC, NDC)
    emit_proj_tm(nc, P, "pv", xnr, wv, v, AF.Silu, NDC, D)



def emit_phase_scan(nc, P, dr, mode="full"):
    q = dr("q", [D, TPC], BF16, kind="ExternalInput")
    k = dr("k", [D, TPC], BF16, kind="ExternalInput")
    v = dr("v", [TPC, D], BF16, kind="ExternalInput")
    g = dr("g", [D, TPC], BF16, kind="ExternalInput")
    rankm = dr("rankm", [128, GRP], F32, kind="ExternalInput")
    ident = dr("ident", [128, 128], BF16, kind="ExternalInput")
    trilm = dr("trilm", [4, 128, BS], BF16, kind="ExternalInput")
    og = dr("og", [D, TPC], BF16, kind="ExternalOutput")

    sloc = nc.dram_tensor("sloc", [D, D], BF16)
    ccS = nc.dram_tensor("ccS", [GRP * D, D], BF16)
    oloc = nc.dram_tensor("oloc", [D, TPC], BF16)

    EH = 1024
    stack = contextlib.ExitStack()
    kvf = stack.enter_context(nc.sbuf_tensor("sc_kvf", [128, NDC * EH], F32))
    kvb = stack.enter_context(nc.sbuf_tensor("sc_kvb", [128, NDC * EH], BF16))
    qt = stack.enter_context(nc.sbuf_tensor("sc_qt", [128, NDC * BS], BF16))
    kt = stack.enter_context(nc.sbuf_tensor("sc_kt", [128, NDC * BS], BF16))
    ktm = stack.enter_context(nc.sbuf_tensor("sc_ktm", [128, 4 * D], BF16))
    vt = stack.enter_context(nc.sbuf_tensor("sc_vt", [128, 4 * EH], BF16))
    stb = stack.enter_context(nc.sbuf_tensor("sc_stb", [128, 4 * BS], BF16))
    idn = stack.enter_context(nc.sbuf_tensor("sc_idn", [128, 128], BF16))
    trm = stack.enter_context(nc.sbuf_tensor("sc_trm", [128, 4 * BS], BF16))
    rmc = stack.enter_context(nc.sbuf_tensor("sc_rmc", [128, GRP], F32))

    def kvf_s(dc, e0, w):
        return kvf[:, dc * EH + e0:dc * EH + e0 + w]

    def kvb_s(dc, e0, w):
        return kvb[:, dc * EH + e0:dc * EH + e0 + w]

    with stack:
        load_block(nc, P, [(idn[:, :], ident[:, :]), (rmc[:, :], rankm[:, :])]
                   + [(trm[:, kc * BS:(kc + 1) * BS], trilm[kc, :, :])
                      for kc in range(4)])

        for h in range(2):
            e0g = h * EH
            simple_block(nc, f"sc_z{h}", gpsimd=lambda gg: (
                gg.memset(kvf[:, :], 0.0), gg.memset(kvb[:, :], 0.0)))
            for c in range(NCH):
                t0 = c * BS
                load_block(nc, P, [
                    (qt[:, :].rearrange("p (c t) -> p c t", c=NDC),
                     csrc(q, 0, NDC, t0, BS)),
                    (kt[:, :].rearrange("p (c t) -> p c t", c=NDC),
                     csrc(k, 0, NDC, t0, BS)),
                    (vt[:, :].rearrange("p (s e) -> p s e", s=4),
                     csrc(v, t0 // 128, 4, e0g, EH)),
                ])

                with (
                    nc.psum_tensor(f"sc_pta{h}{c}", [128, 1024], BF16) as pta,
                    nc.psum_tensor(f"sc_ptb{h}{c}", [128, 1024], BF16) as ptb,
                    nc.Block() as blk,
                ):
                    psts = [pta, ptb]
                    b_mm, b_dve = P.n["mm"], P.n["dve"]

                    def tb(t):
                        i = 0
                        for dc in range(NDC):
                            for s in range(4):
                                if i >= 2:
                                    t.wait_ge(P.sem["dve"], b_dve + i - 1)
                                t.transpose(
                                    psts[i % 2][:, 0:128],
                                    kt[:, dc * BS + s * 128:dc * BS + (s + 1) * 128],
                                    idn[:, :]).then_inc(P.sem["mm"], 1)
                                i += 1
                    blk.tensor(tb)
                    P.n["mm"] = b_mm + 64

                    def te(v_):
                        i = 0
                        for dc in range(NDC):
                            for s in range(4):
                                v_.wait_ge(P.sem["mm"], b_mm + i + 1)
                                v_.tensor_copy(
                                    ktm[:, s * D + dc * 128:s * D + (dc + 1) * 128],
                                    psts[i % 2][:, 0:128]).then_inc(P.sem["dve"], 1)
                                i += 1
                    blk.vector(te)
                    P.n["dve"] = b_dve + 64

                def sc_evac(kc):
                    def evac(e, ps, out):
                        return e.scalar_tensor_tensor(
                            out, ps, 1.0, trm[:, kc * BS:(kc + 1) * BS],
                            ALU.mult, ALU.mult)
                    return evac
                jobs = [Job(
                    kchunks=[(kt[:, dc * BS + kc * 128:dc * BS + (kc + 1) * 128],
                              qt[:, dc * BS:(dc + 1) * BS]) for dc in range(NDC)],
                    evac=sc_evac(kc), store=None) for kc in range(4)]
                pipe_mm(nc, P, f"sc_s{h}{c}", jobs, BS, out_dt=BF16, nps=2,
                        evac_engine="vector",
                        persistent_out=lambda j: stb[:, j * BS:(j + 1) * BS])

                def o_evac(e, ps, out):
                    return ecopy(e, out, ps)
                jobs = []
                for ec in range(EH // 128):
                    kch = [(vt[:, s * EH + ec * 128:s * EH + (ec + 1) * 128],
                            stb[:, s * BS:(s + 1) * BS]) for s in range(4)]
                    if c > 0:
                        kch += [(kvb_s(dc, ec * 128, 128),
                                 qt[:, dc * BS:(dc + 1) * BS]) for dc in range(NDC)]
                    jobs.append(Job(
                        kchunks=kch, evac=o_evac,
                        store=oloc[e0g + ec * 128:e0g + (ec + 1) * 128, t0:t0 + BS]))
                pipe_mm(nc, P, f"sc_o{h}{c}", jobs, BS, out_dt=BF16, nps=2, nout=4,
                        evac_engine="scalar")

                with (
                    nc.psum_tensor(f"sc_pua{h}{c}", [128, 512], F32) as pua,
                    nc.psum_tensor(f"sc_pub{h}{c}", [128, 512], F32) as pub,
                    nc.Block() as blk,
                ):
                    pus = [pua, pub]
                    b_mm, b_dve = P.n["mm"], P.n["dve"]

                    def ub_t(t):
                        i = 0
                        for dc in range(NDC):
                            for eb in range(EH // 512):
                                if i >= 2:
                                    t.wait_ge(P.sem["dve"], b_dve + i - 1)
                                last = None
                                for s in range(4):
                                    last = t.matmul(
                                        pus[i % 2][:, :],
                                        ktm[:, s * D + dc * 128:s * D + (dc + 1) * 128],
                                        vt[:, s * EH + eb * 512:s * EH + (eb + 1) * 512],
                                        start=(s == 0), stop=(s == 3))
                                last.then_inc(P.sem["mm"], 1)
                                i += 1
                    blk.tensor(ub_t)
                    P.n["mm"] = b_mm + 32

                    def ub_v(v_):
                        i = 0
                        for dc in range(NDC):
                            for eb in range(EH // 512):
                                v_.wait_ge(P.sem["mm"], b_mm + i + 1)
                                v_.tensor_add(kvf_s(dc, eb * 512, 512),
                                              kvf_s(dc, eb * 512, 512),
                                              pus[i % 2][:, :])
                                v_.tensor_copy(kvb_s(dc, eb * 512, 512),
                                               kvf_s(dc, eb * 512, 512)) \
                                    .then_inc(P.sem["dve"], 1)
                                i += 1
                    blk.vector(ub_v)
                    P.n["dve"] = b_dve + 32

            store_block(nc, P, [
                (sloc[dc * 128:(dc + 1) * 128, e0g:e0g + EH],
                 kvb[:, dc * EH:(dc + 1) * EH]) for dc in range(NDC)])

    if mode == "full":
        with nc.Block() as blk:
            def ccb(gg):
                gg.collective_compute(
                    "AllGather", ALU.bypass,
                    replica_groups=[[0, 1, 2, 3], [4, 5, 6, 7]],
                    ins=[sloc.ap().opt()], outs=[ccS.ap().opt()]) \
                    .then_inc(P.sem["cc"])
                P.n["cc"] += 1
                gg.wait_ge(P.sem["cc"], P.n["cc"])
            blk.gpsimd(ccb)

    stack2 = contextlib.ExitStack()
    Pb = stack2.enter_context(nc.sbuf_tensor("sc_P", [128, NDC * D], BF16))
    Pr = stack2.enter_context(nc.sbuf_tensor("sc_Pr", [128, GRP * D], BF16))
    Pf = stack2.enter_context(nc.sbuf_tensor("sc_Pf", [128, D], F32))
    qa = stack2.enter_context(nc.sbuf_tensor("sc_qa", [128, 2 * NDC * TBL], BF16))
    otg = stack2.enter_context(nc.sbuf_tensor("sc_otg", [128, 16 * TBL], BF16))
    tmpf = stack2.enter_context(nc.sbuf_tensor("sc_tf2", [128, TBL], F32))
    with stack2:
        if mode == "nop":
            simple_block(nc, "sc_pz", gpsimd=lambda gg: (gg.memset(Pb[:, :], 0.0),))
        else:
            for dc in range(NDC):
                load_block(nc, P, [
                    (Pr[:, :].rearrange("p (r e) -> p r e", r=GRP),
                     ccS.ap().rearrange("(r c p) e -> p r c e", r=GRP, p=128)[:, :, dc, :])])

                def pacc(v_, dc=dc):
                    v_.tensor_scalar_mul(Pf[:, :], Pr[:, 0:D], rmc[:, 0:1])
                    for rr in range(1, GRP):
                        v_.scalar_tensor_tensor(Pf[:, :], Pr[:, rr * D:(rr + 1) * D],
                                                rmc[:, rr:rr + 1], Pf[:, :],
                                                ALU.mult, ALU.add)
                    v_.tensor_copy(Pb[:, dc * D:(dc + 1) * D], Pf[:, :])
                simple_block(nc, f"sc_pa{dc}", vector=pacc)

        jobs = []
        for b in range(NTB):
            for ec in range(NDC):
                loads = []
                if ec == 0:
                    loads += [(qa[:, ((b % 2) * NDC + dc) * TBL:
                                  ((b % 2) * NDC + dc + 1) * TBL],
                               q[dc * 128:(dc + 1) * 128, b * TBL:(b + 1) * TBL])
                              for dc in range(NDC)]
                j = b * NDC + ec
                sl = j % 8
                loads += [(otg[:, sl * TBL:(sl + 1) * TBL],
                           oloc[ec * 128:(ec + 1) * 128, b * TBL:(b + 1) * TBL]),
                          (otg[:, (8 + sl) * TBL:(8 + sl + 1) * TBL],
                           g[ec * 128:(ec + 1) * 128, b * TBL:(b + 1) * TBL])]

                def mk_evac(sl=sl):
                    def evac(e, ps, out):
                        e.tensor_tensor(tmpf[:, :], ps,
                                        otg[:, sl * TBL:(sl + 1) * TBL], ALU.add)
                        return e.tensor_tensor(
                            out, tmpf[:, :],
                            otg[:, (8 + sl) * TBL:(8 + sl + 1) * TBL], ALU.mult)
                    return evac
                jobs.append(Job(
                    kchunks=[(Pb[:, dc * D + ec * 128:dc * D + (ec + 1) * 128],
                              qa[:, ((b % 2) * NDC + dc) * TBL:
                                 ((b % 2) * NDC + dc + 1) * TBL])
                             for dc in range(NDC)],
                    evac=mk_evac(),
                    store=og[ec * 128:(ec + 1) * 128, b * TBL:(b + 1) * TBL],
                    loads=loads))
        pipe_mm(nc, P, "sc_x", jobs, TBL, out_dt=BF16, nps=2, nout=4,
                evac_engine="vector", ld_ahead=6)




def emit_glu(nc, P, name, xin_dram, wg_dram, wu_dram, h_dram, n_in_c, n_i_c,
             gs_dram=None, us_dram=None, wg_off=0, wu_off=0):
    K = n_in_c
    stack = contextlib.ExitStack()
    xin = stack.enter_context(nc.sbuf_tensor(f"{name}_x", [128, K * NTB * TBL], BF16))
    wst = stack.enter_context(nc.sbuf_tensor(f"{name}_w", [128, 4 * 2 * K * 128], BF16))
    scl = stack.enter_context(nc.sbuf_tensor(f"{name}_s", [128, 2 * max(1, n_i_c)], F32))
    hg = stack.enter_context(nc.sbuf_tensor(f"{name}_hg", [128, 3 * 2 * TBL], BF16))
    stage = stack.enter_context(nc.sbuf_tensor(f"{name}_st", [128, 4 * TBL], BF16))
    psa = stack.enter_context(nc.psum_tensor(f"{name}_pa", [128, 2 * TBL], F32))
    psb = stack.enter_context(nc.psum_tensor(f"{name}_pb", [128, 2 * TBL], F32))
    with stack:
        pairs = [(xin[:, :].rearrange("p (c b t) -> p c b t", c=K, b=NTB)[:, :, b, :],
                  csrc(xin_dram, 0, K, b * TBL, TBL))
                 for b in range(NTB)]
        if gs_dram is not None:
            pairs += [(scl[:, ic:ic + 1], gs_dram[ic * 128:(ic + 1) * 128, 0:1])
                      for ic in range(n_i_c)]
            pairs += [(scl[:, n_i_c + ic:n_i_c + ic + 1],
                       us_dram[ic * 128:(ic + 1) * 128, 0:1]) for ic in range(n_i_c)]
        load_block(nc, P, pairs)

        def xi(c, b):
            return xin[:, (c * NTB + b) * TBL:(c * NTB + b + 1) * TBL]

        pss = [psa, psb]
        njob = n_i_c * NTB
        b_ld, b_mm, b_act, b_dve, b_st = (P.n["ld"], P.n["mm"], P.n["act"],
                                          P.n["dve"], P.n["st"])
        with nc.Block() as blk:
            @blk.gpsimd
            def _(g):
                nld = 0
                for ic in range(n_i_c):
                    wsl = ic % 4
                    j0 = ic * NTB
                    if j0 > 2:
                        g.wait_ge(P.sem["dve"], b_dve + j0 - 2)
                    if ic > 0:
                        g.wait_ge(P.sem["ld"], b_ld + 32 * ic)
                    g.dma_start(
                        out=wst[:, (2 * wsl) * K * 128:(2 * wsl + 1) * K * 128]
                            .rearrange("p (c o) -> p c o", c=K),
                        in_=wstrip_ap(wg_dram, K, wg_off + ic * 128, 128),
                    ).then_inc(P.sem["ld"], 16)
                    g.dma_start(
                        out=wst[:, (2 * wsl + 1) * K * 128:(2 * wsl + 2) * K * 128]
                            .rearrange("p (c o) -> p c o", c=K),
                        in_=wstrip_ap(wu_dram, K, wu_off + ic * 128, 128),
                    ).then_inc(P.sem["ld"], 16)
                    nld += 32
            P.n["ld"] = b_ld + 32 * n_i_c

            @blk.tensor
            def _(t):
                for ic in range(n_i_c):
                    wsl = ic % 4
                    t.wait_ge(P.sem["ld"], b_ld + 32 * (ic + 1))
                    for b in range(NTB):
                        j = ic * NTB + b
                        if j >= 2:
                            t.wait_ge(P.sem["act"], b_act + j - 1)
                        for c in range(K):
                            t.matmul(pss[j % 2][:, 0:TBL],
                                     wst[:, ((2 * wsl) * K + c) * 128:
                                         ((2 * wsl) * K + c + 1) * 128],
                                     xi(c, b), start=(c == 0), stop=(c == K - 1))
                        last = None
                        for c in range(K):
                            last = t.matmul(
                                pss[j % 2][:, TBL:2 * TBL],
                                wst[:, ((2 * wsl + 1) * K + c) * 128:
                                    ((2 * wsl + 1) * K + c + 1) * 128],
                                xi(c, b), start=(c == 0), stop=(c == K - 1))
                        last.then_inc(P.sem["mm"], 1)
            P.n["mm"] = b_mm + njob

            @blk.scalar
            def _(s):
                for ic in range(n_i_c):
                    for b in range(NTB):
                        j = ic * NTB + b
                        s.wait_ge(P.sem["mm"], b_mm + j + 1)
                        if j >= 3:
                            s.wait_ge(P.sem["dve"], b_dve + j - 2)
                        sl = j % 3
                        if gs_dram is not None:
                            s.activation(hg[:, (2 * sl) * TBL:(2 * sl + 1) * TBL],
                                         pss[j % 2][:, 0:TBL], AF.Silu,
                                         scale=scl[:, ic:ic + 1])
                            s.activation(hg[:, (2 * sl + 1) * TBL:(2 * sl + 2) * TBL],
                                         pss[j % 2][:, TBL:2 * TBL], AF.Copy,
                                         scale=scl[:, n_i_c + ic:n_i_c + ic + 1]) \
                                .then_inc(P.sem["act"], 1)
                        else:
                            s.activation(hg[:, (2 * sl) * TBL:(2 * sl + 1) * TBL],
                                         pss[j % 2][:, 0:TBL], AF.Silu)
                            s.activation(hg[:, (2 * sl + 1) * TBL:(2 * sl + 2) * TBL],
                                         pss[j % 2][:, TBL:2 * TBL], AF.Copy) \
                                .then_inc(P.sem["act"], 1)
            P.n["act"] = b_act + njob

            @blk.vector
            def _(v):
                for ic in range(n_i_c):
                    for b in range(NTB):
                        j = ic * NTB + b
                        v.wait_ge(P.sem["act"], b_act + j + 1)
                        if j >= 4:
                            v.wait_ge(P.sem["st"], b_st + 16 * (j - 3))
                        sl = j % 3
                        v.tensor_mul(stage[:, (j % 4) * TBL:(j % 4 + 1) * TBL],
                                     hg[:, (2 * sl) * TBL:(2 * sl + 1) * TBL],
                                     hg[:, (2 * sl + 1) * TBL:(2 * sl + 2) * TBL]) \
                            .then_inc(P.sem["dve"], 1)
            P.n["dve"] = b_dve + njob

            @blk.sync
            def _(sy):
                for ic in range(n_i_c):
                    for b in range(NTB):
                        j = ic * NTB + b
                        sy.wait_ge(P.sem["dve"], b_dve + j + 1)
                        sy.dma_start(
                            out=h_dram[ic * 128:(ic + 1) * 128, b * TBL:(b + 1) * TBL],
                            in_=stage[:, (j % 4) * TBL:(j % 4 + 1) * TBL],
                        ).then_inc(P.sem["st"], 16)
                sy.wait_ge(P.sem["st"], b_st + 16 * njob)
            P.n["st"] = b_st + 16 * njob


def emit_down(nc, P, name, h_dram, w_dram, out_dram, res_dram, n_i_c, n_out_c,
              ds_dram=None, out_dt=F32, nhalf=2):
    K = n_i_c
    oc_half = n_out_c // nhalf
    stack = contextlib.ExitStack()
    hst = stack.enter_context(nc.sbuf_tensor(f"{name}_h", [128, K * TBL], BF16))
    wst = stack.enter_context(
        nc.sbuf_tensor(f"{name}_w", [128, K * oc_half * 128], BF16))
    scl = stack.enter_context(nc.sbuf_tensor(f"{name}_s", [128, max(1, n_out_c)], F32))
    res = stack.enter_context(nc.sbuf_tensor(f"{name}_r", [128, 8 * TBL], F32))
    with stack:
        if ds_dram is not None:
            load_block(nc, P, [
                (scl[:, oc:oc + 1], ds_dram[oc * 128:(oc + 1) * 128, 0:1])
                for oc in range(n_out_c)])

        jobs = []
        for oh in range(nhalf):
            for b in range(NTB):
                for oc in range(oc_half):
                    j = len(jobs)
                    loads = []
                    gate = None
                    if b == 0 and oc == 0:
                        loads += [(wst[:, :].rearrange("p (c o) -> p c o", c=K)
                                   [:, 4 * qq:4 * (qq + 1), :],
                                   csrc(w_dram, 4 * qq, 4, oh * oc_half * 128,
                                        oc_half * 128))
                                  for qq in range(K // 4)]
                        gate = j - 1
                    if oc == 0:
                        loads += [(hst[:, :].rearrange("p (c t) -> p c t", c=K),
                                   csrc(h_dram, 0, K, b * TBL, TBL))]
                        if gate is None:
                            gate = j - 1
                    ocg = oh * oc_half + oc
                    rsl = j % 8
                    loads += [(res[:, rsl * TBL:(rsl + 1) * TBL],
                               res_dram[ocg * 128:(ocg + 1) * 128,
                                        b * TBL:(b + 1) * TBL])]

                    def mk_evac(ocg=ocg, rsl=rsl):
                        def evac(e, ps, out):
                            sc = scl[:, ocg:ocg + 1] if ds_dram is not None else 1.0
                            return e.scalar_tensor_tensor(
                                out, ps, sc, res[:, rsl * TBL:(rsl + 1) * TBL],
                                ALU.mult, ALU.add)
                        return evac

                    jobs.append(Job(
                        kchunks=[(wst[:, c * oc_half * 128 + oc * 128:
                                      c * oc_half * 128 + (oc + 1) * 128],
                                  hst[:, c * TBL:(c + 1) * TBL]) for c in range(K)],
                        evac=mk_evac(),
                        store=out_dram[ocg * 128:(ocg + 1) * 128,
                                       b * TBL:(b + 1) * TBL],
                        loads=loads, ld_gate=gate))
        pipe_mm(nc, P, name, jobs, TBL, out_dt=out_dt, nps=2, nout=4,
                evac_engine="vector", ld_ahead=6)


def emit_phase_wp_res(nc, P, dr):
    og = dr("og", [D, TPC], BF16, kind="ExternalInput")
    wp = dr("wp", [D, D], F32, kind="ExternalInput")
    x = dr("x", [D, TPC], F32, kind="ExternalInput")
    xmid = dr("xmid", [D, TPC], F32, kind="ExternalOutput")
    emit_proj_fm(nc, P, "wp", og, wp, xmid, None, NDC, NDC, res_dram=x, out_dt=F32)


def emit_phase_ffn0(nc, P, dr):
    xn = dr("xn", [D, TPC], BF16, kind="ExternalInput")
    wgate = dr("wgate", [D, INTER0], F32, kind="ExternalInput")
    wup = dr("wup", [D, INTER0], F32, kind="ExternalInput")
    gs = dr("gs", [INTER0, 1], F32, kind="ExternalInput")
    us = dr("us", [INTER0, 1], F32, kind="ExternalInput")
    h = dr("h", [INTER0, TPC], BF16, kind="ExternalOutput")
    emit_glu(nc, P, "f0", xn, wgate, wup, h, NDC, NIC0, gs_dram=gs, us_dram=us)


def emit_phase_down0(nc, P, dr):
    h = dr("h", [INTER0, TPC], BF16, kind="ExternalInput")
    wdown = dr("wdown", [INTER0, D], F32, kind="ExternalInput")
    ds = dr("ds", [D, 1], F32, kind="ExternalInput")
    xmid = dr("xmid", [D, TPC], F32, kind="ExternalInput")
    x1 = dr("x1", [D, TPC], F32, kind="ExternalOutput")
    emit_down(nc, P, "d0", h, wdown, x1, xmid, NIC0, NDC, ds_dram=ds)




def emit_fm2tm(nc, P, name, in_dram, out_dram, n_r_c=NDC, n_t_c=TPC // 128):
    stack = contextlib.ExitStack()
    xall = stack.enter_context(
        nc.sbuf_tensor(f"{name}_x", [128, n_r_c * n_t_c * 128], BF16))
    rowb = stack.enter_context(nc.sbuf_tensor(f"{name}_rb", [128, 2 * n_r_c * 128], BF16))
    pta = stack.enter_context(nc.psum_tensor(f"{name}_pa", [128, 1024], BF16))
    ptb = stack.enter_context(nc.psum_tensor(f"{name}_pb", [128, 1024], BF16))
    idn = stack.enter_context(nc.sbuf_tensor(f"{name}_id", [128, 128], BF16))
    with stack:
        simple_block(nc, f"{name}_id", gpsimd=lambda g: (
            g.iota(idn[:, :], [[1, 128]], channel_multiplier=-1,
                   allow_small_or_imprecise_dtypes=True),))
        simple_block(nc, f"{name}_id2", vector=lambda v: (
            v.tensor_single_scalar(idn[:, :], idn[:, :], 0.0, ALU.is_equal),))
        load_block(nc, P, [
            (xall[:, (rc * n_t_c + tc) * 128:(rc * n_t_c + tc + 1) * 128],
             in_dram[rc * 128:(rc + 1) * 128, tc * 128:(tc + 1) * 128])
            for rc in range(n_r_c) for tc in range(n_t_c)])

        pss = [pta, ptb]
        b_mm, b_dve, b_st = P.n["mm"], P.n["dve"], P.n["st"]
        with nc.Block() as blk:
            @blk.tensor
            def _(t):
                i = 0
                for tc in range(n_t_c):
                    for rc in range(n_r_c):
                        if i >= 2:
                            t.wait_ge(P.sem["dve"], b_dve + i - 1)
                        t.transpose(
                            pss[i % 2][:, 0:128],
                            xall[:, (rc * n_t_c + tc) * 128:(rc * n_t_c + tc + 1) * 128],
                            idn[:, :]).then_inc(P.sem["mm"], 1)
                        i += 1
            P.n["mm"] = b_mm + n_t_c * n_r_c

            @blk.vector
            def _(v):
                i = 0
                for tc in range(n_t_c):
                    for rc in range(n_r_c):
                        v.wait_ge(P.sem["mm"], b_mm + i + 1)
                        if tc >= 2 and rc == 0:
                            v.wait_ge(P.sem["st"], b_st + 16 * (tc - 1))
                        v.tensor_copy(
                            rowb[:, ((tc % 2) * n_r_c + rc) * 128:
                                 ((tc % 2) * n_r_c + rc + 1) * 128],
                            pss[i % 2][:, 0:128]).then_inc(P.sem["dve"], 1)
                        i += 1
            P.n["dve"] = b_dve + n_t_c * n_r_c

            @blk.sync
            def _(sy):
                for tc in range(n_t_c):
                    sy.wait_ge(P.sem["dve"], b_dve + (tc + 1) * n_r_c)
                    if tc > 0:
                        sy.wait_ge(P.sem["st"], b_st + 16 * tc)
                    sy.dma_start(
                        out=out_dram[tc * 128:(tc + 1) * 128, :],
                        in_=rowb[:, (tc % 2) * n_r_c * 128:
                                 ((tc % 2) + 1) * n_r_c * 128],
                    ).then_inc(P.sem["st"], 16)
                sy.wait_ge(P.sem["st"], b_st + 16 * n_t_c)
            P.n["st"] = b_st + 16 * n_t_c


def emit_phase_ln1(nc, P, dr):
    x1 = dr("x1", [D, TPC], F32, kind="ExternalInput")
    xn1 = dr("xn1", [D, TPC], BF16, kind="ExternalOutput")
    xn1t = dr("xn1t", [TPC, D], BF16, kind="ExternalOutput")
    emit_ln(nc, P, x1, [(xn1, "ln")], ln_pref="l1")
    emit_fm2tm(nc, P, "tq", xn1, xn1t)


def emit_phase_qkv1(nc, P, dr):
    xn1 = dr("xn1", [D, TPC], BF16, kind="ExternalInput")
    wq = dr("wq", [D, D], F32, kind="ExternalInput")
    wk = dr("wk", [D, NKV * HS], F32, kind="ExternalInput")
    wv = dr("wv", [D, NKV * HS], F32, kind="ExternalInput")
    q1 = dr("q1", [D, TPC], BF16, kind="ExternalOutput")
    k1 = dr("k1", [TPC, NKV * HS], BF16, kind="ExternalOutput")
    v1 = dr("v1", [TPC, NKV * HS], BF16, kind="ExternalOutput")
    emit_proj_fm(nc, P, "q1", xn1, wq, q1, None, NDC, NDC)
    emit_proj_tm(nc, P, "k1", xn1, wk, k1, None, NDC, NKV * HS)
    emit_proj_tm(nc, P, "v1", xn1, wv, v1, None, NDC, NKV * HS)


def emit_phase_wo_res(nc, P, dr):
    y = dr("y", [D, TPC], BF16, kind="ExternalInput")
    wo = dr("wo", [D, D], F32, kind="ExternalInput")
    res = dr("x1", [D, TPC], F32, kind="ExternalInput")
    out = dr("xa", [D, TPC], F32, kind="ExternalOutput")
    emit_proj_fm(nc, P, "wo", y, wo, out, None, NDC, NDC, res_dram=res, out_dt=F32)


def emit_phase_shared(nc, P, dr):
    xn1 = dr("xn1", [D, TPC], BF16, kind="ExternalInput")
    wug = dr("wug", [D, 2 * JSH], F32, kind="ExternalInput")
    h = dr("h", [JSH, TPC], BF16, kind="ExternalOutput")
    emit_glu(nc, P, "sh", xn1, wug, wug, h, NDC, NIC0, wg_off=0, wu_off=JSH)


def emit_phase_shdown(nc, P, dr):
    h = dr("h", [JSH, TPC], BF16, kind="ExternalInput")
    wdown = dr("wdown", [JSH, D], F32, kind="ExternalInput")
    xa = dr("xa", [D, TPC], F32, kind="ExternalInput")
    xa2 = dr("xa2", [D, TPC], F32, kind="ExternalOutput")
    emit_down(nc, P, "sd", h, wdown, xa2, xa, NIC0, NDC, ds_dram=None)




def emit_phase_attn1(nc, P, dr):
    q1 = dr("q1", [D, TPC], BF16, kind="ExternalInput")
    k1 = dr("k1", [TPC, NKV * HS], BF16, kind="ExternalInput")
    v1 = dr("v1", [TPC, NKV * HS], BF16, kind="ExternalInput")
    khalo = dr("khalo", [1152, NKV * HS], BF16, kind="ExternalInput")
    vhalo = dr("vhalo", [1152, NKV * HS], BF16, kind="ExternalInput")
    bandml = dr("bandml", [4, 128, QT], BF16, kind="ExternalInput")
    bandmr = dr("bandmr", [4, 128, QT], BF16, kind="ExternalInput")
    sinkm = dr("sinkm", [NQT, 4, QT], BF16, kind="ExternalInput")
    ninv = dr("ninv", [NQT, QT], F32, kind="ExternalInput")
    y = dr("y", [D, TPC], BF16, kind="ExternalOutput")

    NT_H = 9
    NT_A = 25

    stack = contextlib.ExitStack()
    kfm = stack.enter_context(nc.sbuf_tensor("a1_kfm", [128, NKV * 3200], BF16))
    vbuf = stack.enter_context(nc.sbuf_tensor("a1_vb", [128, NT_A * NKV * HS], BF16))
    qh = stack.enter_context(nc.sbuf_tensor("a1_qh", [128, 2 * QT], BF16))
    pbuf = stack.enter_context(nc.sbuf_tensor("a1_pb", [128, 13 * QT], BF16))
    mls = stack.enter_context(nc.sbuf_tensor("a1_ml", [128, 8 * QT], BF16))
    snk = stack.enter_context(nc.sbuf_tensor("a1_sm", [128, NQT * QT], BF16))
    niv = stack.enter_context(nc.sbuf_tensor("a1_ni", [1, NQT * QT], F32))
    rows = stack.enter_context(nc.sbuf_tensor("a1_rw", [1, 2 * QT], F32))
    denb = stack.enter_context(nc.sbuf_tensor("a1_db", [128, QT], F32))
    ybuf = stack.enter_context(nc.sbuf_tensor("a1_yb", [128, 4 * QT], BF16))
    ones_c = stack.enter_context(nc.sbuf_tensor("a1_oc", [128, 1], BF16))
    ones_r = stack.enter_context(nc.sbuf_tensor("a1_or", [1, 128], F32))
    idn = stack.enter_context(nc.sbuf_tensor("a1_id", [128, 128], BF16))
    bsh = stack.enter_context(nc.sbuf_tensor("a1_bs", [128, 1], F32))
    em12 = stack.enter_context(nc.sbuf_tensor("a1_em", [1, 2], F32))
    em12b = stack.enter_context(nc.sbuf_tensor("a1_emb", [1, 1], BF16))

    with stack:
        simple_block(nc, "a1_i0", gpsimd=lambda g: (
            g.iota(idn[:, :], [[1, 128]], channel_multiplier=-1,
                   allow_small_or_imprecise_dtypes=True),
            g.memset(ones_c[:, :], 1.0),
            g.memset(ones_r[:, :], 1.0),
            g.memset(em12[:, 0:1], 0.0),
            g.memset(bsh[:, :], ESHIFT)))
        simple_block(nc, "a1_em1", scalar=lambda s: (
            s.activation(em12b[:, :], em12[:, 0:1], AF.Exp,
                         bias=bsh[0:1, :], scale=SCALE_ATT),))
        simple_block(nc, "a1_em2", vector=lambda v: (
            v.tensor_scalar_mul(em12[:, 1:2], em12b[:, :], -1.0),))
        simple_block(nc, "a1_i1", vector=lambda v: (
            v.tensor_single_scalar(idn[:, :], idn[:, :], 0.0, ALU.is_equal),))

        load_block(nc, P, [
            (vbuf[:, 0:NT_H * NKV * HS].rearrange("p (b f) -> p b f", b=NT_H),
             csrc(vhalo, 0, NT_H, 0, NKV * HS)),
            (vbuf[:, NT_H * NKV * HS:].rearrange("p (b f) -> p b f", b=16),
             csrc(v1, 0, 16, 0, NKV * HS)),
        ] + [
            (mls[:, i * QT:(i + 1) * QT], bandml[i, :, :]) for i in range(4)
        ] + [
            (mls[:, (4 + i) * QT:(5 + i) * QT], bandmr[i, :, :]) for i in range(4)
        ] + [
            (snk[0:4, qt * QT:(qt + 1) * QT], sinkm[qt, :, :]) for qt in range(NQT)
        ] + [
            (niv[:, qt * QT:(qt + 1) * QT], ninv[qt:qt + 1, :]) for qt in range(NQT)
        ])

        with (
            nc.psum_tensor("a1_pta", [128, 1024], BF16) as pta,
            nc.psum_tensor("a1_ptb", [128, 1024], BF16) as ptb,
            nc.sbuf_tensor("a1_krow", [128, 2 * NKV * HS], BF16) as krow,
            nc.Block() as blk,
        ):
            pss = [pta, ptb]
            b_mm, b_dve, b_ld = P.n["mm"], P.n["dve"], P.n["ld"]
            NTT = 25 * NKV

            @blk.gpsimd
            def _(g):
                for bt in range(25):
                    if bt >= 2:
                        g.wait_ge(P.sem["mm"], b_mm + (bt - 1) * NKV)
                    if bt > 0:
                        g.wait_ge(P.sem["ld"], b_ld + 16 * bt)
                    src = (khalo[bt * 128:(bt + 1) * 128, :] if bt < 9 else
                           k1[(bt - 9) * 128:(bt - 9 + 1) * 128, :])
                    g.dma_start(
                        out=krow[:, (bt % 2) * NKV * HS:((bt % 2) + 1) * NKV * HS],
                        in_=src).then_inc(P.sem["ld"], 16)
            P.n["ld"] = b_ld + 25 * 16

            @blk.tensor
            def _(t):
                i = 0
                for bt in range(25):
                    t.wait_ge(P.sem["ld"], b_ld + 16 * (bt + 1))
                    for kv in range(NKV):
                        if i >= 2:
                            t.wait_ge(P.sem["dve"], b_dve + i - 1)
                        t.transpose(
                            pss[i % 2][:, 0:128],
                            krow[:, ((bt % 2) * NKV + kv) * HS:
                                 ((bt % 2) * NKV + kv + 1) * HS],
                            idn[:, :]).then_inc(P.sem["mm"], 1)
                        i += 1
            P.n["mm"] = b_mm + NTT

            @blk.vector
            def _(v_):
                i = 0
                for bt in range(25):
                    for kv in range(NKV):
                        v_.wait_ge(P.sem["mm"], b_mm + i + 1)
                        if bt < 8:
                            dst = kfm[:, kv * 3200 + bt * 128:kv * 3200 + (bt + 1) * 128]
                            src = pss[i % 2][:, 0:128]
                        elif bt == 8:
                            dst = kfm[:, kv * 3200 + 3072:kv * 3200 + 3076]
                            src = pss[i % 2][:, 0:4]
                        else:
                            c0 = 1024 + (bt - 9) * 128
                            dst = kfm[:, kv * 3200 + c0:kv * 3200 + c0 + 128]
                            src = pss[i % 2][:, 0:128]
                        v_.tensor_copy(dst, src).then_inc(P.sem["dve"], 1)
                        i += 1
            P.n["dve"] = b_dve + NTT

        MASKED = list(range(4)) + list(range(8, 12))
        b_ld, b_mm = P.n["ld"], P.n["mm"]
        b_act, b_dve, b_st = P.n["act"], P.n["dve"], P.n["st"]
        NPAIR = NH * NQT
        PMM, PACT, PDVE = 40, 13, 11
        with (
            nc.psum_tensor("a1_s0", [128, QT], F32) as S0,
            nc.psum_tensor("a1_s1", [128, QT], F32) as S1,
            nc.psum_tensor("a1_s2", [128, QT], F32) as S2,
            nc.psum_tensor("a1_sk", [4, QT], F32) as SK,
            nc.psum_tensor("a1_y0", [128, QT], F32) as Y0,
            nc.psum_tensor("a1_y1", [128, QT], F32) as Y1,
            nc.psum_tensor("a1_dn", [1, QT], F32) as DEN,
            nc.psum_tensor("a1_bc", [128, QT], F32) as BCp,
            nc.Block() as blk,
        ):
            Ss = [S0, S1, S2]
            Ys = [Y0, Y1]

            @blk.gpsimd
            def _(g):
                for p in range(NPAIR):
                    h, qt = p // NQT, p % NQT
                    if p >= 2:
                        g.wait_ge(P.sem["mm"], b_mm + (p - 1) * PMM)
                    if p > 0:
                        g.wait_ge(P.sem["ld"], b_ld + 16 * p)
                    g.dma_start(out=qh[:, (p % 2) * QT:(p % 2 + 1) * QT],
                                in_=q1[h * HS:(h + 1) * HS, qt * QT:(qt + 1) * QT]) \
                        .then_inc(P.sem["ld"], 16)
            P.n["ld"] = b_ld + NPAIR * 16

            @blk.tensor
            def _(t):
                for p in range(NPAIR):
                    h, qt = p // NQT, p % NQT
                    kv = h // (NH // NKV)
                    M0 = b_mm + p * PMM
                    A0 = b_act + p * PACT
                    D0 = b_dve + p * PDVE
                    t.wait_ge(P.sem["ld"], b_ld + 16 * (p + 1))
                    for kc in range(12):
                        if p > 0 and kc < 3:
                            t.wait_ge(P.sem["act"], A0)
                        elif kc >= 3:
                            t.wait_ge(P.sem["act"], A0 + kc - 2)
                        c0 = kv * 3200 + qt * QT + kc * 128
                        t.matmul(Ss[kc % 3][:, :], kfm[:, c0:c0 + 128],
                                 qh[:, (p % 2) * QT:(p % 2 + 1) * QT],
                                 start=True, stop=True).then_inc(P.sem["mm"], 1)
                    t.matmul(SK[:, :], kfm[:, kv * 3200 + 3072:kv * 3200 + 3076],
                             qh[:, (p % 2) * QT:(p % 2 + 1) * QT],
                             start=True, stop=True).then_inc(P.sem["mm"], 1)
                    t.wait_ge(P.sem["act"], A0 + 13)
                    t.wait_ge(P.sem["dve"], D0 + 9)
                    for kc in range(12):
                        bt = qt * 4 + kc
                        vt_i = bt if bt < 8 else bt + 1
                        vcol = (vt_i * NKV + kv) * HS
                        t.matmul(Ys[p % 2][:, :], vbuf[:, vcol:vcol + HS],
                                 pbuf[:, kc * QT:(kc + 1) * QT],
                                 start=(kc == 0), stop=False).then_inc(P.sem["mm"], 1)
                    t.matmul(Ys[p % 2][:, :],
                             vbuf[0:4, (8 * NKV + kv) * HS:(8 * NKV + kv + 1) * HS],
                             pbuf[0:4, 12 * QT:13 * QT],
                             start=False, stop=True).then_inc(P.sem["mm"], 1)
                    for kc in range(12):
                        t.matmul(DEN[:, :], ones_c[:, :],
                                 pbuf[:, kc * QT:(kc + 1) * QT],
                                 start=(kc == 0), stop=False).then_inc(P.sem["mm"], 1)
                    t.matmul(DEN[:, :], ones_c[0:4, :], pbuf[0:4, 12 * QT:13 * QT],
                             start=False, stop=True).then_inc(P.sem["mm"], 1)
                    t.wait_ge(P.sem["dve"], D0 + 10)
                    t.matmul(BCp[:, :], ones_r[:, :], rows[:, QT:2 * QT],
                             start=True, stop=True).then_inc(P.sem["mm"], 1)
            P.n["mm"] = b_mm + NPAIR * PMM

            @blk.scalar
            def _(s):
                for p in range(NPAIR):
                    M0 = b_mm + p * PMM
                    for kc in range(12):
                        s.wait_ge(P.sem["mm"], M0 + kc + 1)
                        s.activation(pbuf[:, kc * QT:(kc + 1) * QT],
                                     Ss[kc % 3][:, :], AF.Exp,
                                     bias=bsh[:, :], scale=SCALE_ATT) \
                            .then_inc(P.sem["act"], 1)
                    s.wait_ge(P.sem["mm"], M0 + 13)
                    s.activation(pbuf[0:4, 12 * QT:13 * QT], SK[:, :], AF.Exp,
                                 bias=bsh[0:4, :], scale=SCALE_ATT) \
                        .then_inc(P.sem["act"], 1)
            P.n["act"] = b_act + NPAIR * PACT

            @blk.vector
            def _(v_):
                for p in range(NPAIR):
                    h, qt = p // NQT, p % NQT
                    A0 = b_act + p * PACT
                    M0 = b_mm + p * PMM
                    for kc in MASKED:
                        v_.wait_ge(P.sem["act"], A0 + kc + 1)
                        mcol = kc if kc < 4 else (kc - 8) + 4
                        v_.tensor_mul(pbuf[:, kc * QT:(kc + 1) * QT],
                                      pbuf[:, kc * QT:(kc + 1) * QT],
                                      mls[:, mcol * QT:(mcol + 1) * QT]) \
                            .then_inc(P.sem["dve"], 1)
                    v_.wait_ge(P.sem["act"], A0 + 13)
                    v_.tensor_mul(pbuf[0:4, 12 * QT:13 * QT],
                                  pbuf[0:4, 12 * QT:13 * QT],
                                  snk[0:4, qt * QT:(qt + 1) * QT]) \
                        .then_inc(P.sem["dve"], 1)
                    v_.wait_ge(P.sem["mm"], M0 + 39)
                    cv = Ch(v_, P)
                    cv.scalar_tensor_tensor(rows[:, 0:QT],
                                            niv[:, qt * QT:(qt + 1) * QT],
                                            em12[:, 1:2], DEN[:, :],
                                            ALU.mult, ALU.add)
                    v_.reciprocal(rows[:, QT:2 * QT], rows[:, 0:QT]) \
                        .then_inc(P.sem["dve"], 1)
                    v_.wait_ge(P.sem["mm"], M0 + 40)
                    cv.tensor_copy(denb[:, :], BCp[:, :])
                    if p >= 4:
                        v_.wait_ge(P.sem["st"], b_st + 16 * (p - 3))
                    v_.tensor_mul(ybuf[:, (p % 4) * QT:(p % 4 + 1) * QT],
                                  Ys[p % 2][:, :], denb[:, :]).then_inc(P.sem["dve"], 1)
            P.n["dve"] = b_dve + NPAIR * PDVE

            @blk.sync
            def _(sy):
                for p in range(NPAIR):
                    h, qt = p // NQT, p % NQT
                    sy.wait_ge(P.sem["dve"], b_dve + p * PDVE + 11)
                    if p > 0:
                        sy.wait_ge(P.sem["st"], b_st + 16 * p)
                    sy.dma_start(out=y[h * HS:(h + 1) * HS, qt * QT:(qt + 1) * QT],
                                 in_=ybuf[:, (p % 4) * QT:(p % 4 + 1) * QT]) \
                        .then_inc(P.sem["st"], 16)
                sy.wait_ge(P.sem["st"], b_st + 16 * NPAIR)
            P.n["st"] = b_st + NPAIR * 16



def emit_phase_route(nc, P, dr, debug=False):
    xn1 = dr("xn1", [D, TPC], BF16, kind="ExternalInput")
    wr = dr("wr", [D, NE], F32, kind="ExternalInput")
    cmask = dr("cmask", [8, 1], F32, kind="ExternalInput")
    sdst0 = dr("sdst0", [TPC, 1], I32, kind="ExternalOutput")
    sdst1 = dr("sdst1", [TPC, 1], I32, kind="ExternalOutput")
    gidx0 = dr("gidx0", [TPC, 1], I32, kind="ExternalOutput")
    gidx1 = dr("gidx1", [TPC, 1], I32, kind="ExternalOutput")
    cw0 = dr("cw0", [TPC, 1], F32, kind="ExternalOutput")
    cw1 = dr("cw1", [TPC, 1], F32, kind="ExternalOutput")
    cnt_in = nc.dram_tensor("cnt_in", [1, NE], F32)
    cnt_out = nc.dram_tensor("cnt_out", [NCORE, NE], F32)
    if debug:
        dbg_lg = dr("dbg_lg", [128, 16 * NE], F32, kind="ExternalOutput")
        dbg_oh = dr("dbg_oh", [128, 16 * NE], F32, kind="ExternalOutput")
        dbg_cum = dr("dbg_cum", [128, 16 * NE], F32, kind="ExternalOutput")
        dbg_oh2 = dr("dbg_oh2", [128, 16 * NE], F32, kind="ExternalOutput")
        dbg_cols = dr("dbg_cols", [128, 24 * 16], F32, kind="ExternalOutput")

    NT = TPC // 128
    stack = contextlib.ExitStack()
    xr = stack.enter_context(nc.sbuf_tensor("rt_x", [128, NDC * 128], BF16))
    wrs = stack.enter_context(nc.sbuf_tensor("rt_w", [128, NDC * NE], BF16))
    lg = stack.enter_context(nc.sbuf_tensor("rt_lg", [128, NT * NE], F32))
    oh1 = stack.enter_context(nc.sbuf_tensor("rt_o1", [128, NT * NE], F32))
    oh2 = stack.enter_context(nc.sbuf_tensor("rt_o2", [128, NT * NE], F32))
    bothb = stack.enter_context(nc.sbuf_tensor("rt_bb", [128, NT * NE], BF16))
    cum = stack.enter_context(nc.sbuf_tensor("rt_cm", [128, NT * NE], F32))
    cols = stack.enter_context(nc.sbuf_tensor("rt_cl", [128, 24 * NT], F32))
    t8 = stack.enter_context(nc.sbuf_tensor("rt_t8", [128, 2 * NE], F32))
    iota8 = stack.enter_context(nc.sbuf_tensor("rt_i8", [128, NE], F32))
    im9 = stack.enter_context(nc.sbuf_tensor("rt_i9", [128, NE], F32))
    zrow = stack.enter_context(nc.sbuf_tensor("rt_zr", [128, 1], F32))
    triu = stack.enter_context(nc.sbuf_tensor("rt_tr", [128, 128], BF16))
    ones_c = stack.enter_context(nc.sbuf_tensor("rt_oc", [128, 1], BF16))
    ones_rf = stack.enter_context(nc.sbuf_tensor("rt_of", [1, 128], F32))
    carry = stack.enter_context(nc.sbuf_tensor("rt_cr", [1, NE], F32))
    cnts = stack.enter_context(nc.sbuf_tensor("rt_ct", [8, NE], F32))
    cmk = stack.enter_context(nc.sbuf_tensor("rt_ck", [8, 1], F32))
    base_bc = stack.enter_context(nc.sbuf_tensor("rt_bc", [128, NE], F32))
    icol = stack.enter_context(nc.sbuf_tensor("rt_ic", [128, 4], I32))

    CW1, CW2, POS0, POS1, DM, E0, E1, GP0, GP1, V0, V1, P0C, P1C, SD0, SD1, \
        OK0, OK1, GI0, GI1, T0, T1, T2, T3, T4 = range(24)

    def cl(tc, i):
        return cols[:, (i * NT + tc):(i * NT + tc) + 1]

    with stack:
        def init(g):
            g.iota(iota8[:, :], [[1, NE]], channel_multiplier=0,
                   allow_small_or_imprecise_dtypes=True)
            g.iota(im9[:, :], [[1, NE]], base=-9, channel_multiplier=0,
                   allow_small_or_imprecise_dtypes=True)
            g.memset(zrow[:, :], float(ZROW))
            g.iota(triu[:, :], [[1, 128]], channel_multiplier=-1,
                   allow_small_or_imprecise_dtypes=True)
            g.memset(ones_c[:, :], 1.0)
            g.memset(ones_rf[:, :], 1.0)
            g.memset(carry[:, :], 0.0)
        simple_block(nc, "rt_init", gpsimd=init)
        simple_block(nc, "rt_init2", vector=lambda v: (
            v.tensor_scalar(triu[:, :], triu[:, :], 1.0, None, ALU.is_ge),))

        load_block(nc, P, [(wrs[:, c * NE:(c + 1) * NE], wr[c * 128:(c + 1) * 128, :])
                           for c in range(NDC)]
                   + [(cmk[:, :], cmask[:, :])])
        jobs = []
        for tc in range(NT):
            loads = [(xr[:, :].rearrange("p (c t) -> p c t", c=NDC),
                      csrc(xn1, 0, NDC, tc * 128, 128))]
            jobs.append(Job(
                kchunks=[(xr[:, c * 128:(c + 1) * 128], wrs[:, c * NE:(c + 1) * NE])
                         for c in range(NDC)],
                evac=lambda e, ps, out: e.tensor_copy(out, ps),
                store=None, loads=loads, ld_gate=tc - 1, ow=NE))
        pipe_mm(nc, P, "rt_lgp", jobs, NE, out_dt=F32, nps=2, evac_engine="vector",
                persistent_out=lambda j: lg[:, j * NE:(j + 1) * NE])

        def tops(v):
            v = Ch(v, P)
            for tc in range(NT):
                L = lg[:, tc * NE:(tc + 1) * NE]
                v.reduce_max(cl(tc, T0), L, AX)
                v.tensor_scalar(t8[:, 0:NE], L, cl(tc, T0), None, ALU.is_equal)
                v.tensor_mul(t8[:, NE:2 * NE], t8[:, 0:NE], im9[:, :])
                v.tensor_scalar_add(t8[:, NE:2 * NE], t8[:, NE:2 * NE], 9.0)
                v.tensor_reduce(cl(tc, T1), t8[:, NE:2 * NE], AX, ALU.min)
                v.tensor_scalar(oh1[:, tc * NE:(tc + 1) * NE], iota8[:, :],
                                cl(tc, T1), None, ALU.is_equal)
                v.scalar_tensor_tensor(t8[:, 0:NE], oh1[:, tc * NE:(tc + 1) * NE],
                                       -1e38, L, ALU.mult, ALU.add)
                v.reduce_max(cl(tc, T2), t8[:, 0:NE], AX)
                v.tensor_scalar(t8[:, NE:2 * NE], t8[:, 0:NE],
                                cl(tc, T2), None, ALU.is_equal)
                v.tensor_mul(t8[:, 0:NE], t8[:, NE:2 * NE], im9[:, :])
                v.tensor_scalar_add(t8[:, 0:NE], t8[:, 0:NE], 9.0)
                v.tensor_reduce(cl(tc, T3), t8[:, 0:NE], AX, ALU.min)
                v.tensor_scalar(oh2[:, tc * NE:(tc + 1) * NE], iota8[:, :],
                                cl(tc, T3), None, ALU.is_equal)
                v.tensor_sub(cl(tc, DM), cl(tc, T0), cl(tc, T2))
                v.tensor_add(t8[:, 0:NE], oh1[:, tc * NE:(tc + 1) * NE],
                             oh2[:, tc * NE:(tc + 1) * NE])
                v.tensor_copy(bothb[:, tc * NE:(tc + 1) * NE], t8[:, 0:NE])
                v.tensor_mul(t8[:, 0:NE], oh1[:, tc * NE:(tc + 1) * NE], iota8[:, :])
                v.reduce_sum(cl(tc, E0), t8[:, 0:NE], AX)
                v.tensor_mul(t8[:, 0:NE], oh2[:, tc * NE:(tc + 1) * NE], iota8[:, :])
                v.reduce_sum(cl(tc, E1), t8[:, 0:NE], AX)
        simple_block(nc, "rt_tops", vector=tops)
        simple_block(nc, "rt_sig", scalar=lambda s: [
            s.activation(cl(tc, CW1), cl(tc, DM), AF.Sigmoid) for tc in range(NT)])
        simple_block(nc, "rt_cw2", vector=lambda v: [
            Ch(v, P).tensor_scalar(cl(tc, CW2), cl(tc, CW1), -1.0, 1.0,
                                   ALU.mult, ALU.add)
            for tc in range(NT)])

        for tc in range(NT):
            with (
                nc.psum_tensor(f"rt_p1{tc}", [128, NE], F32) as p1,
                nc.psum_tensor(f"rt_p2{tc}", [128, NE], F32) as p2,
                nc.psum_tensor(f"rt_p3{tc}", [1, NE], F32) as p3,
            ):
                def mm1(t, tc=tc, p1=p1, p2=p2, p3=p3):
                    t.matmul(p2[:, :], ones_rf[:, :], carry[:, :],
                             start=True, stop=True)
                    t.matmul(p1[:, :], triu[:, :], bothb[:, tc * NE:(tc + 1) * NE],
                             start=True, stop=True)
                    t.matmul(p3[:, :], ones_c[:, :], bothb[:, tc * NE:(tc + 1) * NE],
                             start=True, stop=True)
                simple_block(nc, f"rt_cu{tc}", tensor=mm1)

                def ev1(v, tc=tc, p1=p1, p2=p2, p3=p3):
                    v = Ch(v, P)
                    v.tensor_copy(t8[:, 0:NE], p2[:, :])
                    v.scalar_tensor_tensor(cum[:, tc * NE:(tc + 1) * NE],
                                           p1[:, :], 1.0, t8[:, 0:NE],
                                           ALU.mult, ALU.add)
                    v.tensor_add(carry[:, :], carry[:, :], p3[:, :])
                simple_block(nc, f"rt_cv{tc}", vector=ev1)

        def posb(v):
            v = Ch(v, P)
            for tc in range(NT):
                O1 = oh1[:, tc * NE:(tc + 1) * NE]
                O2 = oh2[:, tc * NE:(tc + 1) * NE]
                CU = cum[:, tc * NE:(tc + 1) * NE]
                v.tensor_mul(t8[:, 0:NE], O1, CU)
                v.reduce_sum(cl(tc, POS0), t8[:, 0:NE], AX)
                v.tensor_add(t8[:, 0:NE], CU, O1)
                v.tensor_mul(t8[:, 0:NE], t8[:, 0:NE], O2)
                v.reduce_sum(cl(tc, POS1), t8[:, 0:NE], AX)
        simple_block(nc, "rt_pos", vector=posb)

        store_block(nc, P, [(cnt_in[:, :], carry[:, :])])
        with nc.Block() as blk:
            def ccb(gg):
                gg.collective_compute(
                    "AllGather", ALU.bypass, replica_groups=[list(range(NCORE))],
                    ins=[cnt_in.ap().opt()], outs=[cnt_out.ap().opt()]) \
                    .then_inc(P.sem["cc"])
                P.n["cc"] += 1
                gg.wait_ge(P.sem["cc"], P.n["cc"])
            blk.gpsimd(ccb)
        load_block(nc, P, [(cnts[:, :], cnt_out[:, :])])
        with (
            nc.psum_tensor("rt_pb", [1, NE], F32) as pb,
            nc.psum_tensor("rt_pb2", [128, NE], F32) as pb2,
        ):
            simple_block(nc, "rt_bs", tensor=lambda t: (
                t.matmul(pb[:, :], cmk[:, :], cnts[:, :], start=True, stop=True),))
            simple_block(nc, "rt_bse", vector=lambda v: (
                v.tensor_copy(carry[:, :], pb[:, :]),))
            simple_block(nc, "rt_bs2", tensor=lambda t: (
                t.matmul(pb2[:, :], ones_rf[:, :], carry[:, :], start=True, stop=True),))
            simple_block(nc, "rt_bse2", vector=lambda v: (
                v.tensor_copy(base_bc[:, :], pb2[:, :]),))

        def finb(v):
            v = Ch(v, P)
            for tc in range(NT):
                O1 = oh1[:, tc * NE:(tc + 1) * NE]
                O2 = oh2[:, tc * NE:(tc + 1) * NE]
                v.tensor_mul(t8[:, 0:NE], O1, base_bc[:, :])
                v.reduce_sum(cl(tc, T0), t8[:, 0:NE], AX)
                v.tensor_add(cl(tc, GP0), cl(tc, T0), cl(tc, POS0))
                v.tensor_mul(t8[:, 0:NE], O2, base_bc[:, :])
                v.reduce_sum(cl(tc, T1), t8[:, 0:NE], AX)
                v.tensor_add(cl(tc, GP1), cl(tc, T1), cl(tc, POS1))
                v.tensor_single_scalar(cl(tc, V0), cl(tc, GP0), float(CGLB), ALU.is_lt)
                v.tensor_single_scalar(cl(tc, V1), cl(tc, GP1), float(CGLB), ALU.is_lt)
                v.tensor_mul(cl(tc, CW1), cl(tc, CW1), cl(tc, V0))
                v.tensor_mul(cl(tc, CW2), cl(tc, CW2), cl(tc, V1))
                v.tensor_scalar_min(cl(tc, P0C), cl(tc, POS0), float(ESTR - 1))
                v.tensor_scalar_min(cl(tc, P1C), cl(tc, POS1), float(ESTR - 1))
                v.scalar_tensor_tensor(cl(tc, SD0), cl(tc, E0), float(ESTR),
                                       cl(tc, P0C), ALU.mult, ALU.add)
                v.scalar_tensor_tensor(cl(tc, SD1), cl(tc, E1), float(ESTR),
                                       cl(tc, P1C), ALU.mult, ALU.add)
                v.tensor_single_scalar(cl(tc, T2), cl(tc, POS0), float(CAP), ALU.is_lt)
                v.tensor_mul(cl(tc, OK0), cl(tc, T2), cl(tc, V0))
                v.tensor_single_scalar(cl(tc, T3), cl(tc, POS1), float(CAP), ALU.is_lt)
                v.tensor_mul(cl(tc, OK1), cl(tc, T3), cl(tc, V1))
                v.tensor_scalar_add(cl(tc, T4), cl(tc, SD0), -float(ZROW))
                v.tensor_mul(cl(tc, T4), cl(tc, T4), cl(tc, OK0))
                v.tensor_scalar_add(cl(tc, GI0), cl(tc, T4), float(ZROW))
                v.tensor_scalar_add(cl(tc, T4), cl(tc, SD1), -float(ZROW))
                v.tensor_mul(cl(tc, T4), cl(tc, T4), cl(tc, OK1))
                v.tensor_scalar_add(cl(tc, GI1), cl(tc, T4), float(ZROW))
        simple_block(nc, "rt_fin", vector=finb)

        if debug:
            store_block(nc, P, [(dbg_lg[:, :], lg[:, :]), (dbg_oh[:, :], oh1[:, :]),
                                (dbg_cum[:, :], cum[:, :]), (dbg_cols[:, :], cols[:, :]),
                                (dbg_oh2[:, :], oh2[:, :])])
        b_dve = P.n["dve"]
        b_st0 = P.n["st"]
        with nc.Block() as blk:
            def conv(v):
                for tc in range(NT):
                    if tc > 0:
                        v.wait_ge(P.sem["st"], b_st0 + 6 * 16 * tc)
                    v.tensor_copy(icol[:, 0:1], cl(tc, SD0))
                    v.tensor_copy(icol[:, 1:2], cl(tc, SD1))
                    v.tensor_copy(icol[:, 2:3], cl(tc, GI0))
                    v.tensor_copy(icol[:, 3:4], cl(tc, GI1)).then_inc(P.sem["dve"], 1)
            blk.vector(conv)
            P.n["dve"] = b_dve + NT

            def stb(sy):
                for tc in range(NT):
                    sy.wait_ge(P.sem["dve"], b_dve + tc + 1)
                    if tc > 0:
                        sy.wait_ge(P.sem["st"], P.n["st"] + 6 * 16 * tc)
                    r = slice(tc * 128, (tc + 1) * 128)
                    sy.dma_start(out=sdst0[r, :], in_=icol[:, 0:1]).then_inc(P.sem["st"], 16)
                    sy.dma_start(out=sdst1[r, :], in_=icol[:, 1:2]).then_inc(P.sem["st"], 16)
                    sy.dma_start(out=gidx0[r, :], in_=icol[:, 2:3]).then_inc(P.sem["st"], 16)
                    sy.dma_start(out=gidx1[r, :], in_=icol[:, 3:4]).then_inc(P.sem["st"], 16)
                    sy.dma_start(out=cw0[r, :], in_=cl(tc, CW1)).then_inc(P.sem["st"], 16)
                    sy.dma_start(out=cw1[r, :], in_=cl(tc, CW2)).then_inc(P.sem["st"], 16)
                sy.wait_ge(P.sem["st"], P.n["st"] + 6 * 16 * NT)
            blk.sync(stb)
            P.n["st"] += 6 * 16 * NT



def emit_phase_moe(nc, P, dr, debug=False):
    xn1t = dr("xn1t", [TPC, D], BF16, kind="ExternalInput")
    sdst0 = dr("sdst0", [TPC, 1], I32, kind="ExternalInput")
    sdst1 = dr("sdst1", [TPC, 1], I32, kind="ExternalInput")
    gidx0 = dr("gidx0", [TPC, 1], I32, kind="ExternalInput")
    gidx1 = dr("gidx1", [TPC, 1], I32, kind="ExternalInput")
    cw0 = dr("cw0", [TPC, 1], F32, kind="ExternalInput")
    cw1 = dr("cw1", [TPC, 1], F32, kind="ExternalInput")
    wug = dr("wug", [NE, D, MEXP], F32, kind="ExternalInput")
    wdn = dr("wdn", [NE, IEXP, D], F32, kind="ExternalInput")
    xa2 = dr("xa2", [D, TPC], F32, kind="ExternalInput")
    out = dr("out", [D, TPC], F32, kind="ExternalOutput")

    buf = nc.dram_tensor("moe_buf", [ZROW + 1, D], BF16)
    obuf = nc.dram_tensor("moe_obuf", [ZROW + 1, D], BF16)
    if debug:
        dbuf = dr("dbuf", [ZROW + 1, D], BF16, kind="ExternalOutput")
        dobuf = dr("dobuf", [ZROW + 1, D], BF16, kind="ExternalOutput")

    NT = TPC // 128

    stack = contextlib.ExitStack()
    xrow = stack.enter_context(nc.sbuf_tensor("mo_x", [128, 2 * D], BF16))
    idxc = stack.enter_context(nc.sbuf_tensor("mo_ix", [128, 4], I32))
    cwc = stack.enter_context(nc.sbuf_tensor("mo_cw", [128, 2], F32))
    zr = stack.enter_context(nc.sbuf_tensor("mo_zr", [1, D], BF16))
    idn = stack.enter_context(nc.sbuf_tensor("mo_id", [128, 128], BF16))

    with stack:
        simple_block(nc, "mo_i0", gpsimd=lambda g: (
            g.iota(idn[:, :], [[1, 128]], channel_multiplier=-1,
                   allow_small_or_imprecise_dtypes=True),
            g.memset(zr[:, :], 0.0)))
        simple_block(nc, "mo_i1", vector=lambda v: (
            v.tensor_single_scalar(idn[:, :], idn[:, :], 0.0, ALU.is_equal),))
        store_block(nc, P, [(obuf[ZROW:ZROW + 1, :], zr[:, :])])

        with nc.Block() as blk:
            b_ld = P.n["ld"]

            def disp(g):
                n = b_ld
                for tc in range(NT):
                    sl = tc % 2
                    g.wait_ge(P.sem["ld"], n)
                    g.dma_start(out=xrow[:, sl * D:(sl + 1) * D],
                                in_=xn1t[tc * 128:(tc + 1) * 128, :]) \
                        .then_inc(P.sem["ld"], 16)
                    g.dma_start(out=idxc[:, 0:1],
                                in_=sdst0[tc * 128:(tc + 1) * 128, :]) \
                        .then_inc(P.sem["ld"], 16)
                    g.dma_start(out=idxc[:, 1:2],
                                in_=sdst1[tc * 128:(tc + 1) * 128, :]) \
                        .then_inc(P.sem["ld"], 16)
                    n += 48
                    g.wait_ge(P.sem["ld"], n)
                    g.indirect_dma_start(
                        out=buf[:, :],
                        out_offset=bass.IndirectOffsetOnAxis(idxc[:, 0:1], 0),
                        in_=xrow[:, sl * D:(sl + 1) * D], in_offset=None) \
                        .then_inc(P.sem["ld"], 16)
                    n += 16
                    g.wait_ge(P.sem["ld"], n)
                    g.indirect_dma_start(
                        out=buf[:, :],
                        out_offset=bass.IndirectOffsetOnAxis(idxc[:, 1:2], 0),
                        in_=xrow[:, sl * D:(sl + 1) * D], in_offset=None) \
                        .then_inc(P.sem["ld"], 16)
                    n += 16
                g.wait_ge(P.sem["ld"], n)
            blk.gpsimd(disp)
            P.n["ld"] += NT * 5 * 16

        NSG = CAP // 512
        NMC = MEXP // 128
        NIC = IEXP // 128
        for e in range(NE):
            estk = contextlib.ExitStack()
            wugs = estk.enter_context(nc.sbuf_tensor(f"mo_wu{e}", [128, NDC * MEXP], BF16))
            wdns = estk.enter_context(nc.sbuf_tensor(f"mo_wd{e}", [128, NIC * D], BF16))
            bfm = estk.enter_context(nc.sbuf_tensor(f"mo_bf{e}", [128, NDC * 512], BF16))
            hfm = estk.enter_context(nc.sbuf_tensor(f"mo_hf{e}", [128, NIC * 512], BF16))
            hup = estk.enter_context(nc.sbuf_tensor(f"mo_hu{e}", [128, NMC * 512], BF16))
            with estk:
                load_block(nc, P, [
                    (wugs[:, dc * MEXP:(dc + 1) * MEXP],
                     wug[e, dc * 128:(dc + 1) * 128, :]) for dc in range(NDC)
                ] + [
                    (wdns[:, ic * D:(ic + 1) * D],
                     wdn[e, ic * 128:(ic + 1) * 128, :]) for ic in range(NIC)])
                for sg in range(NSG):
                    with nc.Block() as blk:
                        b_st2 = P.n["st"]

                        def tload(sy, e=e, sg=sg, b_st2=b_st2):
                            n = b_st2
                            for dc in range(NDC):
                                if dc > 0:
                                    sy.wait_ge(P.sem["st"], n)
                                sy.dma_start(
                                    out=bfm[:, dc * 512:(dc + 1) * 512],
                                    in_=buf[e * ESTR + sg * 512:e * ESTR + (sg + 1) * 512,
                                            dc * 128:(dc + 1) * 128],
                                    transpose=True).then_inc(P.sem["st"], 16)
                                n += 16
                            sy.wait_ge(P.sem["st"], n)
                        blk.sync(tload)
                        P.n["st"] += NDC * 16

                    jobs = []
                    for mc in range(NMC):
                        def mk_ev(mc=mc):
                            def ev(eng, ps, o):
                                if mc < NIC:
                                    return eng.activation(
                                        hup[:, mc * 512:(mc + 1) * 512], ps, AF.Silu)
                                return eng.activation(
                                    hup[:, mc * 512:(mc + 1) * 512], ps, AF.Copy)
                            return ev
                        jobs.append(Job(
                            kchunks=[(wugs[:, dc * MEXP + mc * 128:
                                          dc * MEXP + (mc + 1) * 128],
                                      bfm[:, dc * 512:(dc + 1) * 512])
                                     for dc in range(NDC)],
                            evac=mk_ev(), store=None))
                    pipe_mm(nc, P, f"mo_ug{e}_{sg}", jobs, 512, out_dt=BF16, nps=2,
                            evac_engine="scalar", persistent_out=lambda j: None)
                    simple_block(nc, f"mo_cb{e}_{sg}", vector=lambda v: [
                        v.tensor_mul(hfm[:, ic * 512:(ic + 1) * 512],
                                     hup[:, ic * 512:(ic + 1) * 512],
                                     hup[:, (NIC + ic) * 512:(NIC + ic + 1) * 512])
                        for ic in range(NIC)])

                    jobs = []
                    for sc in range(4):
                        for ob in range(4):
                            jobs.append(Job(
                                kchunks=[(hfm[:, ic * 512 + sc * 128:
                                              ic * 512 + (sc + 1) * 128],
                                          wdns[:, ic * D + ob * 512:
                                               ic * D + (ob + 1) * 512])
                                         for ic in range(NIC)],
                                evac=lambda eng, ps, o: ecopy(eng, o, ps),
                                store=obuf[e * ESTR + sg * 512 + sc * 128:
                                           e * ESTR + sg * 512 + (sc + 1) * 128,
                                           ob * 512:(ob + 1) * 512]))
                    pipe_mm(nc, P, f"mo_dn{e}_{sg}", jobs, 512, out_dt=BF16,
                            nps=2, nout=4, evac_engine="scalar")

        if debug:
            with nc.Block() as blk:
                def dmp(g):
                    n0 = P.n["ld"]
                    for rr in range(0, ZROW + 1, 2048):
                        rw = min(2048, ZROW + 1 - rr)
                        g.wait_ge(P.sem["ld"], P.n["ld"])
                        g.dma_start(out=dbuf[rr:rr + rw, :], in_=buf[rr:rr + rw, :]) \
                            .then_inc(P.sem["ld"], 16)
                        P.n["ld"] += 16
                        g.wait_ge(P.sem["ld"], P.n["ld"])
                        g.dma_start(out=dobuf[rr:rr + rw, :], in_=obuf[rr:rr + rw, :]) \
                            .then_inc(P.sem["ld"], 16)
                        P.n["ld"] += 16
                    g.wait_ge(P.sem["ld"], P.n["ld"])
                blk.gpsimd(dmp)
        cstk = contextlib.ExitStack()
        gath = cstk.enter_context(nc.sbuf_tensor("mo_g", [128, 2 * D], BF16))
        ytm = cstk.enter_context(nc.sbuf_tensor("mo_y", [128, D], BF16))
        fstg = cstk.enter_context(nc.sbuf_tensor("mo_fs", [128, D], F32))
        xat = cstk.enter_context(nc.sbuf_tensor("mo_xa", [128, 2 * D], F32))
        cstk.__enter__() if False else None
        for tc in range(NT):
            load_block(nc, P, [
                (idxc[:, 2:3], gidx0[tc * 128:(tc + 1) * 128, :]),
                (idxc[:, 3:4], gidx1[tc * 128:(tc + 1) * 128, :]),
                (cwc[:, 0:1], cw0[tc * 128:(tc + 1) * 128, :]),
                (cwc[:, 1:2], cw1[tc * 128:(tc + 1) * 128, :]),
            ] + [(xat[:, 0:NDC * 128].rearrange("p (c t) -> p c t", c=NDC),
                  csrc(xa2, 0, NDC, tc * 128, 128))])
            with nc.Block() as blk:
                b_ld2 = P.n["ld"]

                def gat(g):
                    g.indirect_dma_start(
                        out=gath[:, 0:D], out_offset=None, in_=obuf[:, :],
                        in_offset=bass.IndirectOffsetOnAxis(idxc[:, 2:3], 0)) \
                        .then_inc(P.sem["ld"], 16)
                    g.wait_ge(P.sem["ld"], b_ld2 + 16)
                    g.indirect_dma_start(
                        out=gath[:, D:2 * D], out_offset=None, in_=obuf[:, :],
                        in_offset=bass.IndirectOffsetOnAxis(idxc[:, 3:4], 0)) \
                        .then_inc(P.sem["ld"], 16)
                    g.wait_ge(P.sem["ld"], b_ld2 + 32)
                blk.gpsimd(gat)
                P.n["ld"] = b_ld2 + 32
            def _cmb(v, tc=tc):
                v = Ch(v, P)
                v.tensor_scalar_mul(fstg[:, 0:D], gath[:, 0:D], cwc[:, 0:1])
                v.scalar_tensor_tensor(fstg[:, 0:D], gath[:, D:2 * D], cwc[:, 1:2],
                                       fstg[:, 0:D], ALU.mult, ALU.add)
                v.tensor_copy(ytm[:, :], fstg[:, 0:D])
            simple_block(nc, f"mo_y{tc}", vector=_cmb)
            with (
                nc.psum_tensor(f"mo_yp{tc}a", [128, 1024], BF16) as pta,
                nc.psum_tensor(f"mo_yp{tc}b", [128, 1024], BF16) as ptb,
                nc.Block() as blk,
            ):
                pss = [pta, ptb]
                b_mm, b_dve, b_st3 = P.n["mm"], P.n["dve"], P.n["st"]

                def tb2(t):
                    for dc in range(NDC):
                        if dc >= 2:
                            t.wait_ge(P.sem["dve"], b_dve + dc - 1)
                        t.transpose(pss[dc % 2][:, 0:128],
                                    ytm[:, dc * 128:(dc + 1) * 128],
                                    idn[:, :]).then_inc(P.sem["mm"], 1)
                blk.tensor(tb2)
                P.n["mm"] = b_mm + NDC

                def te2(v_):
                    for dc in range(NDC):
                        v_.wait_ge(P.sem["mm"], b_mm + dc + 1)
                        v_.tensor_tensor(fstg[:, dc * 128:(dc + 1) * 128],
                                         pss[dc % 2][:, 0:128],
                                         xat[:, dc * 128:(dc + 1) * 128],
                                         ALU.add).then_inc(P.sem["dve"], 1)
                blk.vector(te2)
                P.n["dve"] = b_dve + NDC

                def st2(sy):
                    for dc in range(NDC):
                        sy.wait_ge(P.sem["dve"], b_dve + dc + 1)
                        if dc > 0:
                            sy.wait_ge(P.sem["st"], b_st3 + 16 * dc)
                        sy.dma_start(
                            out=out[dc * 128:(dc + 1) * 128, tc * 128:(tc + 1) * 128],
                            in_=fstg[:, dc * 128:(dc + 1) * 128]) \
                            .then_inc(P.sem["st"], 16)
                    sy.wait_ge(P.sem["st"], b_st3 + 16 * NDC)
                blk.sync(st2)
                P.n["st"] = b_st3 + 16 * NDC
        cstk.close()




def _passthrough_dr(nc):
    def dr(name, shape, dt, kind=None):
        if kind is None:
            return nc.dram_tensor(name, shape, dt)
        return nc.dram_tensor(name, shape, dt, kind=kind)
    return dr


def _mk_build(emit):
    @prog
    def b(nc, P, **kw):
        emit(nc, P, _passthrough_dr(nc), **kw)
    return b


build_wo_res = _mk_build(emit_phase_wo_res)
build_ln0 = _mk_build(emit_phase_ln0)
build_qkvg = _mk_build(emit_phase_qkvg)
build_scan = _mk_build(emit_phase_scan)
build_wp_res = _mk_build(emit_phase_wp_res)
build_ffn0 = _mk_build(emit_phase_ffn0)
build_down0 = _mk_build(emit_phase_down0)
build_ln1 = _mk_build(emit_phase_ln1)
build_qkv1 = _mk_build(emit_phase_qkv1)
build_attn1 = _mk_build(emit_phase_attn1)
build_shared = _mk_build(emit_phase_shared)
build_shdown = _mk_build(emit_phase_shdown)
build_route = _mk_build(emit_phase_route)
build_moe = _mk_build(emit_phase_moe)



def emit_phase_halo(nc, P, dr):
    k1 = dr("k1", [TPC, NKV * HS], BF16, kind="ExternalInput")
    v1 = dr("v1", [TPC, NKV * HS], BF16, kind="ExternalInput")
    kidx = dr("kidx", [1152, 1], I32, kind="ExternalInput")
    vidx = dr("vidx", [1152, 1], I32, kind="ExternalInput")
    khalo = dr("khalo", [1152, NKV * HS], BF16, kind="ExternalOutput")
    vhalo = dr("vhalo", [1152, NKV * HS], BF16, kind="ExternalOutput")
    contrib = nc.dram_tensor("hx_contrib", [2064, 512], BF16)
    agh = nc.dram_tensor("hx_agh", [GRP * 2064, 512], BF16)

    stack = contextlib.ExitStack()
    zt = stack.enter_context(nc.sbuf_tensor("hx_z", [128, 512], BF16))
    idxs = stack.enter_context(nc.sbuf_tensor("hx_i", [128, 2], I32))
    row = stack.enter_context(nc.sbuf_tensor("hx_r", [128, 512], BF16))
    with stack:
        with nc.Block() as blk:
            def cb(g):
                g.memset(zt[:, :], 0.0)
                for dst0, src, rws in [(0, k1, (1024, 2048)), (1024, k1, (0, 4)),
                                       (1028, v1, (0, 4)), (1032, v1, (1024, 2048))]:
                    g.wait_ge(P.sem["ld"], P.n["ld"])
                    g.dma_start(out=contrib[dst0:dst0 + (rws[1] - rws[0]), :],
                                in_=src[rws[0]:rws[1], :]).then_inc(P.sem["ld"], 16)
                    P.n["ld"] += 16
                g.wait_ge(P.sem["ld"], P.n["ld"])
                g.dma_start(out=contrib[2056:2064, :], in_=zt[0:8, :]) \
                    .then_inc(P.sem["ld"], 16)
                P.n["ld"] += 16
                g.wait_ge(P.sem["ld"], P.n["ld"])
                g.collective_compute(
                    "AllGather", ALU.bypass,
                    replica_groups=[[0, 1, 2, 3], [4, 5, 6, 7]],
                    ins=[contrib.ap().opt()], outs=[agh.ap().opt()]) \
                    .then_inc(P.sem["cc"])
                P.n["cc"] += 1
                g.wait_ge(P.sem["cc"], P.n["cc"])
                for j in range(9):
                    for which, (idxd, outd, col) in enumerate(
                            [(kidx, khalo, 0), (vidx, vhalo, 1)]):
                        g.wait_ge(P.sem["ld"], P.n["ld"])
                        g.dma_start(out=idxs[:, col:col + 1],
                                    in_=idxd[j * 128:(j + 1) * 128, :]) \
                            .then_inc(P.sem["ld"], 16)
                        P.n["ld"] += 16
                        g.wait_ge(P.sem["ld"], P.n["ld"])
                        g.indirect_dma_start(
                            out=row[:, :], out_offset=None, in_=agh[:, :],
                            in_offset=bass.IndirectOffsetOnAxis(idxs[:, col:col + 1], 0)) \
                            .then_inc(P.sem["ld"], 16)
                        P.n["ld"] += 16
                        g.wait_ge(P.sem["ld"], P.n["ld"])
                        g.dma_start(out=outd[j * 128:(j + 1) * 128, :], in_=row[:, :]) \
                            .then_inc(P.sem["ld"], 16)
                        P.n["ld"] += 16
                g.wait_ge(P.sem["ld"], P.n["ld"])
            blk.gpsimd(cb)


_EI, _EO = "ExternalInput", "ExternalOutput"
FUSED_ALIASES = {
    "ln0": {"x": ("x", _EI), "xn": ("xn", None), "xnr": ("xnr", None)},
    "qkvg": {"xnr": ("xnr", None), "wq": ("l0wq", _EI), "wk": ("l0wk", _EI),
             "wv": ("l0wv", _EI), "wg": ("l0wg", _EI), "q": ("l0q", None),
             "k": ("l0k", None), "g": ("l0g", None), "v": ("l0v", None)},
    "scan": {"q": ("l0q", None), "k": ("l0k", None), "v": ("l0v", None),
             "g": ("l0g", None), "rankm": ("rankm", _EI), "ident": ("ident", _EI),
             "trilm": ("trilm", _EI), "og": ("og", None)},
    "wp_res": {"og": ("og", None), "wp": ("l0wp", _EI), "x": ("x", _EI),
               "xmid": ("xmid", None)},
    "ffn0": {"xn": ("xn", None), "wgate": ("l0wgate", _EI), "wup": ("l0wup", _EI),
             "gs": ("l0gs", _EI), "us": ("l0us", _EI), "h": ("h0", None)},
    "down0": {"h": ("h0", None), "wdown": ("l0wdown", _EI), "ds": ("l0ds", _EI),
              "xmid": ("xmid", None), "x1": ("x1", None)},
    "ln1": {"x1": ("x1", None), "xn1": ("xn1", None), "xn1t": ("xn1t", None)},
    "qkv1": {"xn1": ("xn1", None), "wq": ("l1wq", _EI), "wk": ("l1wk", _EI),
             "wv": ("l1wv", _EI), "q1": ("q1", None), "k1": ("k1", None),
             "v1": ("v1", None)},
    "halo": {"k1": ("k1", None), "v1": ("v1", None), "kidx": ("kidx", _EI),
             "vidx": ("vidx", _EI), "khalo": ("khalo", None), "vhalo": ("vhalo", None)},
    "attn1": {"q1": ("q1", None), "k1": ("k1", None), "v1": ("v1", None),
              "khalo": ("khalo", None), "vhalo": ("vhalo", None),
              "bandml": ("bandml", _EI), "bandmr": ("bandmr", _EI),
              "sinkm": ("sinkm", _EI), "ninv": ("ninv", _EI), "y": ("y1", None)},
    "wo_res": {"y": ("y1", None), "wo": ("l1wo", _EI), "x1": ("x1", None),
               "xa": ("xa", None)},
    "shared": {"xn1": ("xn1", None), "wug": ("l1wsug", _EI), "h": ("hsh", None)},
    "shdown": {"h": ("hsh", None), "wdown": ("l1wsdn", _EI), "xa": ("xa", None),
               "xa2": ("xa2", None)},
    "route": {"xn1": ("xn1", None), "wr": ("l1wr", _EI), "cmask": ("cmask", _EI),
              "sdst0": ("sdst0", None), "sdst1": ("sdst1", None),
              "gidx0": ("gidx0", None), "gidx1": ("gidx1", None),
              "cw0": ("cw0", None), "cw1": ("cw1", None)},
    "moe": {"xn1t": ("xn1t", None), "sdst0": ("sdst0", None), "sdst1": ("sdst1", None),
            "gidx0": ("gidx0", None), "gidx1": ("gidx1", None), "cw0": ("cw0", None),
            "cw1": ("cw1", None), "wug": ("l1wug", _EI), "wdn": ("l1wdn", _EI),
            "xa2": ("xa2", None), "out": ("out", _EO)},
}

FUSED_ORDER = ["ln0", "qkvg", "scan", "wp_res", "ffn0", "down0", "ln1", "qkv1",
               "halo", "attn1", "wo_res", "shared", "shdown", "route", "moe"]


@prog
def build_fused(nc, P):
    cache = {}

    def mk_dr(amap):
        def dr(name, shape, dt, kind=None):
            cname, ckind = amap[name]
            if cname in cache:
                return cache[cname]
            if ckind is None:
                t = nc.dram_tensor(cname, list(shape), dt)
            else:
                t = nc.dram_tensor(cname, list(shape), dt, kind=ckind)
            cache[cname] = t
            return t
        return dr

    for ph in FUSED_ORDER:
        emit = globals()[f"emit_phase_{ph}"]
        emit(nc, P, mk_dr(FUSED_ALIASES[ph]))


def make_halo_idx():
    kidxs, vidxs = [], []
    for c in range(NCORE):
        r = c % GRP
        Z = 2056
        ki = np.full(1152, Z, np.int32)
        vi = np.full(1152, Z, np.int32)
        if r > 0:
            ki[0:1024] = (r - 1) * 2064 + np.arange(1024)
            vi[0:1024] = (r - 1) * 2064 + 1032 + np.arange(1024)
        ki[1024:1028] = 1024 + np.arange(4)
        vi[1024:1028] = 1028 + np.arange(4)
        kidxs.append(np.ascontiguousarray(ki.reshape(-1, 1)))
        vidxs.append(np.ascontiguousarray(vi.reshape(-1, 1)))
    return kidxs, vidxs


_FUSED_NC = None


def kernel_fused(L0_attn_x, L0_attn_Wq, L0_attn_Wk, L0_attn_Wv, L0_attn_Wg,
                 L0_attn_Wp, L0_ffn_gate_w, L0_ffn_up_w, L0_ffn_down_w,
                 L0_ffn_gate_s, L0_ffn_up_s, L0_ffn_down_s, L1_attn_Wq,
                 L1_attn_Wk, L1_attn_Wv, L1_attn_Wo, L1_ffn_router,
                 L1_ffn_w_up_gate, L1_ffn_w_down, L1_ffn_shared_w_up_gate,
                 L1_ffn_shared_w_down):
    def T(w):
        return np.ascontiguousarray(np.asarray(w).T)

    def C(w):
        return np.ascontiguousarray(np.asarray(w))

    global _FUSED_NC
    xs = shard_x(np.asarray(L0_attn_x))
    rankm, ident, trilm = host_scan_inputs()
    bandml, bandmr, sinkms, ninvs = make_attn_inputs()
    kidxs, vidxs = make_halo_idx()
    shared_w = {
        "l0wq": T(L0_attn_Wq), "l0wk": T(L0_attn_Wk), "l0wv": T(L0_attn_Wv),
        "l0wg": T(L0_attn_Wg), "l0wp": T(L0_attn_Wp),
        "l0wgate": T(L0_ffn_gate_w), "l0wup": T(L0_ffn_up_w),
        "l0wdown": T(L0_ffn_down_w),
        "l0gs": C(np.asarray(L0_ffn_gate_s).reshape(-1, 1)),
        "l0us": C(np.asarray(L0_ffn_up_s).reshape(-1, 1)),
        "l0ds": C(np.asarray(L0_ffn_down_s).reshape(-1, 1)),
        "l1wq": T(L1_attn_Wq), "l1wk": T(L1_attn_Wk), "l1wv": T(L1_attn_Wv),
        "l1wo": T(L1_attn_Wo), "l1wr": C(L1_ffn_router),
        "l1wug": C(L1_ffn_w_up_gate), "l1wdn": C(L1_ffn_w_down),
        "l1wsug": C(L1_ffn_shared_w_up_gate), "l1wsdn": C(L1_ffn_shared_w_down),
        "ident": ident, "trilm": trilm, "bandml": bandml, "bandmr": bandmr,
    }
    in_maps = []
    for c in range(NCORE):
        cm = np.zeros((NCORE, 1), np.float32)
        cm[:c, 0] = 1.0
        in_maps.append({
            "x": xs[c], "rankm": rankm[c], "sinkm": sinkms[c], "ninv": ninvs[c],
            "cmask": cm, "kidx": kidxs[c], "vidx": vidxs[c], **shared_w})
    if _FUSED_NC is None:
        _FUSED_NC = build_fused()
    res = _run(_FUSED_NC, in_maps)
    out = np.empty((2, 8192, D), np.float32)
    for c in range(NCORE):
        b, qr = c // GRP, c % GRP
        out[b, qr * TPC:(qr + 1) * TPC, :] = res[c]["out"].T
    return out



def _run(nc, in_maps):
    return run_bass_kernel_spmd(nc, in_maps, core_ids=list(range(NCORE))).results


def shard_x(x):
    out = []
    for c in range(NCORE):
        b, qr = c // GRP, c % GRP
        out.append(np.ascontiguousarray(x[b, qr * TPC:(qr + 1) * TPC, :].T))
    return out



def make_attn_inputs():
    i = np.arange(QT)
    bandml = np.zeros((4, 128, QT), np.float32)
    bandmr = np.zeros((4, 128, QT), np.float32)
    for kc in range(4):
        for rr in range(128):
            bandml[kc, rr, :] = (kc * 128 + rr >= i + 1)
    for j in range(4):
        for rr in range(128):
            bandmr[j, rr, :] = ((8 + j) * 128 + rr <= 1024 + i)
    bandml = bandml.astype(BF)
    bandmr = bandmr.astype(BF)
    sinkms, ninvs = [], []
    for c in range(NCORE):
        first = (c % GRP == 0)
        sm = np.ones((NQT, 4, QT), np.float32)
        nv = np.zeros((NQT, QT), np.float32)
        if first:
            for qt in range(NQT):
                q0 = qt * QT
                qv = q0 + i
                for js in range(4):
                    sm[qt, js, :] = np.where(
                        qv < 1024, 0.0,
                        np.where(qv <= 1027, (qv - 1023 > js).astype(np.float32), 1.0))
                lim = 1023 - q0
                nv[qt, :] = np.maximum(0, np.minimum(1024 + i, lim) - i)
        sinkms.append(sm.astype(BF))
        ninvs.append(nv)
    return bandml, bandmr, sinkms, ninvs


def make_halo(k_all, v_all):
    kh, vh = [], []
    for c in range(NCORE):
        b, qr = c // GRP, c % GRP
        k = np.zeros((1152, NKV * HS), BF)
        v = np.zeros((1152, NKV * HS), BF)
        if qr > 0:
            k[0:1024] = k_all[c - 1][TPC - 1024:]
            v[0:1024] = v_all[c - 1][TPC - 1024:]
        k[1024:1028] = k_all[b * GRP][0:4]
        v[1024:1028] = v_all[b * GRP][0:4]
        kh.append(k)
        vh.append(v)
    return kh, vh


def host_scan_inputs():
    rankm = []
    for c in range(NCORE):
        row = np.zeros((128, GRP), np.float32)
        row[:, :c % GRP] = 1.0
        rankm.append(row)
    ident = np.eye(128, dtype=np.float32).astype(BF)
    trilm = np.zeros((4, 128, BS), np.float32)
    for kc in range(4):
        for rr in range(128):
            trilm[kc, rr, kc * 128 + rr:] = 1.0
    return rankm, ident, trilm.astype(BF)


def kernel_unfused(L0_attn_x, L0_attn_Wq, L0_attn_Wk, L0_attn_Wv, L0_attn_Wg, L0_attn_Wp,
           L0_ffn_gate_w, L0_ffn_up_w, L0_ffn_down_w, L0_ffn_gate_s, L0_ffn_up_s,
           L0_ffn_down_s, L1_attn_Wq, L1_attn_Wk, L1_attn_Wv, L1_attn_Wo,
           L1_ffn_router, L1_ffn_w_up_gate, L1_ffn_w_down,
           L1_ffn_shared_w_up_gate, L1_ffn_shared_w_down):
    def T(w):
        return np.ascontiguousarray(np.asarray(w).T)

    def C(w):
        return np.ascontiguousarray(np.asarray(w))

    R = range(NCORE)
    xs = shard_x(np.asarray(L0_attn_x))

    r1 = _run(build_ln0(), [{"x": xs[c]} for c in R])
    wq, wk, wv, wg = T(L0_attn_Wq), T(L0_attn_Wk), T(L0_attn_Wv), T(L0_attn_Wg)
    r2 = _run(build_qkvg(), [
        {"xnr": r1[c]["xnr"], "wq": wq, "wk": wk, "wv": wv, "wg": wg} for c in R])
    rankm, ident, trilm = host_scan_inputs()
    r3 = _run(build_scan(), [
        {"q": r2[c]["q"], "k": r2[c]["k"], "v": r2[c]["v"], "g": r2[c]["g"],
         "rankm": rankm[c], "ident": ident, "trilm": trilm} for c in R])
    wp = T(L0_attn_Wp)
    r4 = _run(build_wp_res(), [
        {"og": r3[c]["og"], "wp": wp, "x": xs[c]} for c in R])
    wgate, wup = T(L0_ffn_gate_w), T(L0_ffn_up_w)
    gs = C(np.asarray(L0_ffn_gate_s).reshape(-1, 1))
    us = C(np.asarray(L0_ffn_up_s).reshape(-1, 1))
    r5 = _run(build_ffn0(), [
        {"xn": r1[c]["xn"], "wgate": wgate, "wup": wup, "gs": gs, "us": us}
        for c in R])
    wdown0 = T(L0_ffn_down_w)
    ds = C(np.asarray(L0_ffn_down_s).reshape(-1, 1))
    r6 = _run(build_down0(), [
        {"h": r5[c]["h"], "wdown": wdown0, "ds": ds, "xmid": r4[c]["xmid"]}
        for c in R])
    r7 = _run(build_ln1(), [{"x1": r6[c]["x1"]} for c in R])
    wq1, wk1, wv1 = T(L1_attn_Wq), T(L1_attn_Wk), T(L1_attn_Wv)
    r8 = _run(build_qkv1(), [
        {"xn1": r7[c]["xn1"], "wq": wq1, "wk": wk1, "wv": wv1} for c in R])
    kh, vh = make_halo([r8[c]["k1"] for c in R], [r8[c]["v1"] for c in R])
    bandml, bandmr, sinkms, ninvs = make_attn_inputs()
    r9 = _run(build_attn1(), [
        {"q1": r8[c]["q1"], "k1": r8[c]["k1"], "v1": r8[c]["v1"],
         "khalo": kh[c], "vhalo": vh[c], "bandml": bandml, "bandmr": bandmr,
         "sinkm": sinkms[c], "ninv": ninvs[c]} for c in R])
    wo = T(L1_attn_Wo)
    r10 = _run(build_wo_res(), [
        {"y": r9[c]["y"], "wo": wo, "x1": r6[c]["x1"]} for c in R])
    wsug = C(L1_ffn_shared_w_up_gate)
    r11 = _run(build_shared(), [{"xn1": r7[c]["xn1"], "wug": wsug} for c in R])
    wsdn = C(L1_ffn_shared_w_down)
    r12 = _run(build_shdown(), [
        {"h": r11[c]["h"], "wdown": wsdn, "xa": r10[c]["xa"]} for c in R])
    wr = C(L1_ffn_router)
    cmasks = []
    for c in R:
        cm = np.zeros((NCORE, 1), np.float32)
        cm[:c, 0] = 1.0
        cmasks.append(cm)
    r13 = _run(build_route(), [
        {"xn1": r7[c]["xn1"], "wr": wr, "cmask": cmasks[c]} for c in R])
    wug_a = C(L1_ffn_w_up_gate)
    wdn_a = C(L1_ffn_w_down)
    r14 = _run(build_moe(), [
        {"xn1t": r7[c]["xn1t"], "sdst0": r13[c]["sdst0"], "sdst1": r13[c]["sdst1"],
         "gidx0": r13[c]["gidx0"], "gidx1": r13[c]["gidx1"],
         "cw0": r13[c]["cw0"], "cw1": r13[c]["cw1"],
         "wug": wug_a, "wdn": wdn_a, "xa2": r12[c]["xa2"]} for c in R])

    out = np.empty((2, 2 * GRP * 0 + 8192, D), np.float32)
    for c in R:
        b, qr = c // GRP, c % GRP
        out[b, qr * TPC:(qr + 1) * TPC, :] = r14[c]["out"].T
    return out


kernel = kernel_fused


# version 06 (final)
# speedup vs baseline: 1.0249x; 1.0249x over previous
import contextlib
import numpy as np
import ml_dtypes
import concourse.bass as bass
import concourse.mybir as mybir
from concourse.bass_utils import run_bass_kernel_spmd

F32 = mybir.dt.float32
BF16 = mybir.dt.bfloat16
I32 = mybir.dt.int32
AF = mybir.ActivationFunctionType
ALU = mybir.AluOpType
AX = mybir.AxisListType.X
BF = ml_dtypes.bfloat16

D = 2048
TPC = 2048
NTB = 4
TBL = 512
NDC = 16
NCORE = 8
GRP = 4

INTER0 = 5632
NIC0 = 44

BS = 512
NCH = TPC // BS

NH, NKV, HS, WIN, SINK = 16, 4, 128, 1024, 4
QT = 512
NQT = TPC // QT
KBUF = WIN + QT
NKC = KBUF // 128
SCALE_ATT = 1.0 / float(np.sqrt(np.float32(HS)))
ESHIFT = -12.0
EM12 = float(np.exp(np.float64(ESHIFT)))

NE = 8
CAP = 2048
ESTR = 2056
ZROW = NE * ESTR
CGLB = 5120
MEXP = 2816
IEXP = 1408
JSH = 5632

LN_EPS = 1e-5
RMS_EPS = float(np.finfo(np.float32).eps)


def _bfh(x):
    return np.ascontiguousarray(x).astype(BF)


def ecopy(e, out, in_):
    if hasattr(e, "tensor_copy"):
        return e.tensor_copy(out, in_)
    return e.copy(out, in_)



class Pool:
    NAMES = ("ld", "st", "mm", "dve", "act", "cc", "gp", "ch")

    def __init__(self, nc, stack):
        self.nc = nc
        self.sem = {}
        self.n = {}
        for nm in self.NAMES:
            self.sem[nm] = stack.enter_context(nc.semaphore(f"P_{nm}"))
            self.n[nm] = 0


def prog(build_fn):
    def wrapper(*a, **kw):
        nc = bass.Bass("TRN2", target_bir_lowering=False)
        stack = contextlib.ExitStack()
        P = Pool(nc, stack)
        with stack:
            build_fn(nc, P, *a, **kw)
        return nc
    return wrapper


ENG2CTR = {"vector": "dve", "scalar": "act"}


class Ch:

    def __init__(self, eng, P, nm="ch"):
        self._e = eng
        self._P = P
        self._nm = nm

    def __getattr__(self, a):
        f = getattr(self._e, a)

        def w(*args, **kw):
            inst = f(*args, **kw)
            self._P.n[self._nm] += 1
            inst.then_inc(self._P.sem[self._nm], 1)
            self._e.wait_ge(self._P.sem[self._nm], self._P.n[self._nm])
            return inst
        return w



def simple_block(nc, name, **bodies):
    with nc.Block() as blk:
        for eng, fn in bodies.items():
            getattr(blk, eng)(fn)


def load_block(nc, P, pairs, engine="gpsimd"):
    sem = P.sem["ld"]
    with nc.Block() as blk:
        def body(g):
            for (sb, dr) in pairs:
                g.dma_start(out=sb, in_=dr).then_inc(sem, 16)
                P.n["ld"] += 16
            g.wait_ge(sem, P.n["ld"])
        getattr(blk, engine)(body)


def store_block(nc, P, pairs, engine="sync"):
    sem = P.sem["st"]
    with nc.Block() as blk:
        def body(g):
            for (dr, sb) in pairs:
                g.dma_start(out=dr, in_=sb).then_inc(sem, 16)
                P.n["st"] += 16
            g.wait_ge(sem, P.n["st"])
        getattr(blk, engine)(body)


class Job:
    def __init__(self, kchunks, evac, store=None, loads=None, pw=128, ld_gate=None,
                 ow=None):
        self.kchunks = kchunks
        self.evac = evac
        self.store = store
        self.loads = loads or []
        self.pw = pw
        self.ld_gate = ld_gate
        self.ow = ow


def pipe_mm(nc, P, name, jobs, out_w, out_dt=BF16, nps=2, nout=4,
            evac_engine="scalar", ps_w=None, ld_ahead=4, persistent_out=None):
    ps_w = ps_w or out_w
    n = len(jobs)
    cumld = np.cumsum([0] + [len(j.loads) for j in jobs])
    b_ld, b_mm, b_st = P.n["ld"], P.n["mm"], P.n["st"]
    ec = ENG2CTR[evac_engine]
    b_ev = P.n[ec]
    nstores = [0] * (n + 1)
    for j, job in enumerate(jobs):
        nstores[j + 1] = nstores[j] + (1 if job.store is not None else 0)

    stack = contextlib.ExitStack()
    stage = stack.enter_context(
        nc.sbuf_tensor(f"{name}_stage", [128, max(1, nout) * out_w], out_dt))
    pss = [stack.enter_context(nc.psum_tensor(f"{name}_ps{i}", [128, ps_w], F32))
           for i in range(nps)]
    with stack, nc.Block() as blk:

        @blk.gpsimd
        def _(g):
            for j, job in enumerate(jobs):
                if not job.loads:
                    continue
                gate = job.ld_gate if job.ld_gate is not None else j - ld_ahead
                if gate > 0:
                    g.wait_ge(P.sem[ec], b_ev + gate)
                if cumld[j] > 0:
                    g.wait_ge(P.sem["ld"], b_ld + 16 * int(cumld[j]))
                for (sb, dr) in job.loads:
                    g.dma_start(out=sb, in_=dr).then_inc(P.sem["ld"], 16)
        P.n["ld"] = b_ld + 16 * int(cumld[n])

        @blk.tensor
        def _(t):
            for j, job in enumerate(jobs):
                if cumld[j + 1] > 0:
                    t.wait_ge(P.sem["ld"], b_ld + 16 * int(cumld[j + 1]))
                if j >= nps:
                    t.wait_ge(P.sem[ec], b_ev + j - nps + 1)
                ps = pss[j % nps]
                nk = len(job.kchunks)
                ow = job.ow or out_w
                last = None
                for kk, (lh, rh) in enumerate(job.kchunks):
                    last = t.matmul(ps[0:job.pw, 0:ow], lh, rh,
                                    start=(kk == 0), stop=(kk == nk - 1))
                last.then_inc(P.sem["mm"], 1)
        P.n["mm"] = b_mm + n

        def evac_body(e):
            for j, job in enumerate(jobs):
                e.wait_ge(P.sem["mm"], b_mm + j + 1)
                ow = job.ow or out_w
                if persistent_out is not None:
                    out_ap = persistent_out(j)
                else:
                    if j >= nout:
                        e.wait_ge(P.sem["st"], b_st + 16 * nstores[j - nout + 1])
                    out_ap = stage[0:job.pw,
                                   (j % nout) * out_w:(j % nout) * out_w + ow]
                job.evac(e, pss[j % nps][0:job.pw, 0:ow], out_ap) \
                    .then_inc(P.sem[ec], 1)

        if evac_engine == "scalar":
            @blk.scalar
            def _(s):
                evac_body(s)
        else:
            @blk.vector
            def _(v):
                evac_body(v)
        P.n[ec] = b_ev + n

        @blk.sync
        def _(sy):
            for j, job in enumerate(jobs):
                if job.store is None:
                    continue
                sy.wait_ge(P.sem[ec], b_ev + j + 1)
                if nstores[j] > 0:
                    sy.wait_ge(P.sem["st"], b_st + 16 * nstores[j])
                sy.dma_start(
                    out=job.store,
                    in_=stage[0:job.pw,
                              (j % nout) * out_w:(j % nout) * out_w + (job.ow or out_w)],
                ).then_inc(P.sem["st"], 16)
            sy.wait_ge(P.sem["st"], b_st + 16 * nstores[n])
            sy.wait_ge(P.sem[ec], b_ev + n)
        P.n["st"] = b_st + 16 * nstores[n]



def csrc(dram, c0, nchunk, f0, w):
    return dram.ap().rearrange("(c p) f -> p c f", p=128)[:, c0:c0 + nchunk, f0:f0 + w]


def wstrip_ap(w_dram, n_in_c, oc0, width):
    return w_dram.ap().rearrange("(c p) o -> p c o", p=128)[:, :, oc0:oc0 + width]



def emit_ln(nc, P, x_dram, outs, ln_pref="ln"):
    stack = contextlib.ExitStack()
    xt = stack.enter_context(nc.sbuf_tensor(ln_pref + "_xt", [128, NDC * TBL], BF16))
    sq = stack.enter_context(nc.sbuf_tensor(ln_pref + "_sq", [128, NDC * TBL], BF16))
    ones_col = stack.enter_context(nc.sbuf_tensor(ln_pref + "_oc", [128, 1], BF16))
    ones_row = stack.enter_context(nc.sbuf_tensor(ln_pref + "_or", [1, 128], F32))
    rows = stack.enter_context(nc.sbuf_tensor(ln_pref + "_rows", [1, 10 * TBL], F32))
    bca = stack.enter_context(nc.sbuf_tensor(ln_pref + "_bca", [128, 4 * TBL], F32))
    outt = stack.enter_context(
        nc.sbuf_tensor(ln_pref + "_outt", [128, len(outs) * NDC * TBL], BF16))
    tmp = stack.enter_context(nc.sbuf_tensor(ln_pref + "_tmp", [128, TBL], F32))

    def r(i):
        return rows[:, i * TBL:(i + 1) * TBL]
    SUM, SSQ, MEAN, VAR, RSTD, R2, A2, B1, B2, TMPR = range(10)

    with stack:
        def init(g):
            g.memset(ones_col[:, :], 1.0)
            g.memset(ones_row[:, :], 1.0)
        simple_block(nc, ln_pref + "_init", gpsimd=init)

        for b in range(NTB):
            load_block(nc, P, [
                (xt[:, :].rearrange("p (c t) -> p c t", c=NDC),
                 csrc(x_dram, 0, NDC, b * TBL, TBL))])

            def sqb(s):
                for c in range(NDC):
                    s.activation(sq[:, c * TBL:(c + 1) * TBL],
                                 xt[:, c * TBL:(c + 1) * TBL], AF.Square)
            simple_block(nc, f"{ln_pref}_sq{b}", scalar=sqb)

            with nc.psum_tensor(f"{ln_pref}_ps{b}", [1, 2 * TBL], F32) as ps:
                def mmb(t):
                    for c in range(NDC):
                        t.matmul(ps[:, 0:TBL], ones_col[:, :],
                                 xt[:, c * TBL:(c + 1) * TBL],
                                 start=(c == 0), stop=(c == NDC - 1))
                        t.matmul(ps[:, TBL:2 * TBL], ones_col[:, :],
                                 sq[:, c * TBL:(c + 1) * TBL],
                                 start=(c == 0), stop=(c == NDC - 1))
                simple_block(nc, f"{ln_pref}_mm{b}", tensor=mmb)

                def r1(v):
                    v = Ch(v, P)
                    v.tensor_scalar_mul(r(MEAN), ps[:, 0:TBL], 1.0 / D)
                    v.tensor_scalar_mul(r(SSQ), ps[:, TBL:2 * TBL], 1.0 / D)
                    v.tensor_mul(r(VAR), r(MEAN), r(MEAN))
                    v.tensor_sub(r(VAR), r(SSQ), r(VAR))
                    v.tensor_scalar_add(r(TMPR), r(VAR), LN_EPS)
                    v.reciprocal(r(TMPR), r(TMPR))
                simple_block(nc, f"{ln_pref}_r1{b}", vector=r1)

            simple_block(nc, f"{ln_pref}_r2{b}", scalar=lambda s: (
                s.activation(r(RSTD), r(TMPR), AF.Sqrt),))

            def r3(v):
                v = Ch(v, P)
                v.tensor_mul(r(TMPR), r(RSTD), r(RSTD))
                v.tensor_mul(r(TMPR), r(TMPR), r(VAR))
                v.tensor_scalar_add(r(TMPR), r(TMPR), RMS_EPS)
                v.reciprocal(r(TMPR), r(TMPR))
            simple_block(nc, f"{ln_pref}_r3{b}", vector=r3)
            simple_block(nc, f"{ln_pref}_r4{b}", scalar=lambda s: (
                s.activation(r(R2), r(TMPR), AF.Sqrt),))

            def r5(v):
                v = Ch(v, P)
                v.tensor_mul(r(B1), r(MEAN), r(RSTD))
                v.tensor_scalar_mul(r(B1), r(B1), -1.0)
                v.tensor_mul(r(A2), r(RSTD), r(R2))
                v.tensor_mul(r(B2), r(B1), r(R2))
            simple_block(nc, f"{ln_pref}_r5{b}", vector=r5)

            with nc.psum_tensor(f"{ln_pref}_pbc{b}", [128, 4 * TBL], F32) as pbc:
                def bcb(t):
                    for i, idx in enumerate([RSTD, B1, A2, B2]):
                        t.matmul(pbc[:, i * TBL:(i + 1) * TBL], ones_row[:, :],
                                 r(idx), start=True, stop=True)
                simple_block(nc, f"{ln_pref}_bc{b}", tensor=bcb)
                simple_block(nc, f"{ln_pref}_ev{b}", vector=lambda v: (
                    v.tensor_copy(bca[:, :], pbc[:, :]),))

            with nc.Block() as blk:
                b_dve = P.n["dve"]

                def nrm(v):
                    for oi, (odr, mode) in enumerate(outs):
                        a0 = 0 if mode == "ln" else 2
                        ac = bca[:, a0 * TBL:(a0 + 1) * TBL]
                        bc = bca[:, (a0 + 1) * TBL:(a0 + 2) * TBL]
                        for c in range(NDC):
                            v.tensor_tensor(tmp[:, :], xt[:, c * TBL:(c + 1) * TBL],
                                            ac, ALU.mult)
                            v.tensor_tensor(
                                outt[:, (oi * NDC + c) * TBL:(oi * NDC + c + 1) * TBL],
                                tmp[:, :], bc, ALU.add).then_inc(P.sem["dve"], 1)
                            P.n["dve"] += 1
                blk.vector(nrm)

                def nst(sy):
                    for oi, (odr, mode) in enumerate(outs):
                        sy.wait_ge(P.sem["dve"], b_dve + (oi + 1) * NDC)
                        if oi > 0:
                            sy.wait_ge(P.sem["st"], P.n["st"])
                        sy.dma_start(
                            out=csrc(odr, 0, NDC, b * TBL, TBL),
                            in_=outt[:, oi * NDC * TBL:(oi + 1) * NDC * TBL]
                            .rearrange("p (c t) -> p c t", c=NDC),
                        ).then_inc(P.sem["st"], 16)
                        P.n["st"] += 16
                    sy.wait_ge(P.sem["st"], P.n["st"])
                blk.sync(nst)


def emit_phase_ln0(nc, P, dr):
    x = dr("x", [D, TPC], F32, kind="ExternalInput")
    xn = dr("xn", [D, TPC], BF16, kind="ExternalOutput")
    xnr = dr("xnr", [D, TPC], BF16, kind="ExternalOutput")
    emit_ln(nc, P, x, [(xn, "ln"), (xnr, "lnrms")])



def emit_proj_fm(nc, P, name, xin_dram, w_dram, out_dram, act, n_in_c, n_out_c,
                 scale_dram=None, res_dram=None, out_dt=BF16, wsrc_off=0,
                 tok_cols=TPC, res_dt=F32):
    K = n_in_c
    ntb = tok_cols // TBL
    stack = contextlib.ExitStack()
    xin = stack.enter_context(nc.sbuf_tensor(f"{name}_x", [128, K * ntb * TBL], BF16))
    wst = stack.enter_context(nc.sbuf_tensor(f"{name}_w", [128, 8 * K * 128], BF16))
    scl = stack.enter_context(nc.sbuf_tensor(f"{name}_s", [128, max(1, n_out_c)], F32))
    res = stack.enter_context(
        nc.sbuf_tensor(f"{name}_r", [128, 8 * TBL], res_dt)) \
        if res_dram is not None else None
    with stack:
        pairs = [(xin[:, :].rearrange("p (c b t) -> p c b t", c=K, b=ntb)[:, :, b, :],
                  csrc(xin_dram, 0, K, b * TBL, TBL))
                 for b in range(ntb)]
        if scale_dram is not None:
            pairs += [(scl[:, oc:oc + 1], scale_dram[oc * 128:(oc + 1) * 128, 0:1])
                      for oc in range(n_out_c)]
        load_block(nc, P, pairs)

        def xi(c, b):
            return xin[:, (c * ntb + b) * TBL:(c * ntb + b + 1) * TBL]

        jobs = []
        for oc in range(n_out_c):
            wsl = oc % 8
            for b in range(ntb):
                loads = []
                if b == 0:
                    loads.append((
                        wst[:, wsl * K * 128:(wsl + 1) * K * 128]
                        .rearrange("p (c o) -> p c o", c=K),
                        wstrip_ap(w_dram, K, wsrc_off + oc * 128, 128)))
                if res_dram is not None:
                    rsl = (oc * ntb + b) % 8
                    loads.append((res[:, rsl * TBL:(rsl + 1) * TBL],
                                  res_dram[oc * 128:(oc + 1) * 128,
                                           b * TBL:(b + 1) * TBL]))

                def mk_evac(oc=oc, b=b):
                    rsl = (oc * ntb + b) % 8

                    def evac(e, ps, out):
                        if res_dram is not None:
                            sc = scl[:, oc:oc + 1] if scale_dram is not None else 1.0
                            return e.scalar_tensor_tensor(
                                out, ps, sc, res[:, rsl * TBL:(rsl + 1) * TBL],
                                ALU.mult, ALU.add)
                        if scale_dram is not None and act is None:
                            return e.tensor_scalar_mul(out, ps, scl[:, oc:oc + 1])
                        if scale_dram is not None:
                            return e.activation(out, ps, act, scale=scl[:, oc:oc + 1])
                        if act is not None:
                            return e.activation(out, ps, act)
                        return ecopy(e, out, ps)
                    return evac

                jobs.append(Job(
                    kchunks=[(wst[:, (wsl * K + c) * 128:(wsl * K + c + 1) * 128],
                              xi(c, b)) for c in range(K)],
                    evac=mk_evac(),
                    store=out_dram[oc * 128:(oc + 1) * 128, b * TBL:(b + 1) * TBL],
                    loads=loads))
        ee = "vector" if (res_dram is not None
                          or (scale_dram is not None and act is None)) else "scalar"
        pipe_mm(nc, P, name, jobs, TBL, out_dt=out_dt, nps=2, nout=4,
                evac_engine=ee, ld_ahead=4)


def emit_proj_tm(nc, P, name, xin_dram, w_dram, out_dram, act, n_in_c, out_total):
    K = n_in_c
    NOB = (out_total + 511) // 512
    stack = contextlib.ExitStack()
    xin = stack.enter_context(nc.sbuf_tensor(f"{name}_x", [128, K * NTB * TBL], BF16))
    wst = stack.enter_context(nc.sbuf_tensor(f"{name}_w", [128, K * out_total], BF16))
    with stack:
        pairs = [(xin[:, :].rearrange("p (c b t) -> p c b t", c=K, b=NTB)[:, :, b, :],
                  csrc(xin_dram, 0, K, b * TBL, TBL))
                 for b in range(NTB)]
        pairs += [(wst[:, :].rearrange("p (c o) -> p c o", c=K)[:, 4 * q:4 * (q + 1), :],
                   csrc(w_dram, 4 * q, 4, 0, out_total)) for q in range(K // 4)]
        load_block(nc, P, pairs)

        def xi(c, tc):
            b, sub = tc // 4, tc % 4
            base = (c * NTB + b) * TBL + sub * 128
            return xin[:, base:base + 128]

        def evac(e, ps, out):
            if act is not None:
                return e.activation(out, ps, act)
            return ecopy(e, out, ps)

        jobs = []
        for tc in range(TPC // 128):
            for ob in range(NOB):
                ow = min(512, out_total - ob * 512)
                jobs.append(Job(
                    kchunks=[(xi(c, tc),
                              wst[:, c * out_total + ob * 512:c * out_total + ob * 512 + ow])
                             for c in range(K)],
                    evac=evac,
                    store=out_dram[tc * 128:(tc + 1) * 128, ob * 512:ob * 512 + ow]))
        pipe_mm(nc, P, name, jobs, 512, out_dt=BF16, nps=2, nout=4,
                evac_engine="scalar")


def emit_phase_qkvg(nc, P, dr):
    xnr = dr("xnr", [D, TPC], BF16, kind="ExternalInput")
    wq = dr("wq", [D, D], F32, kind="ExternalInput")
    wk = dr("wk", [D, D], F32, kind="ExternalInput")
    wv = dr("wv", [D, D], F32, kind="ExternalInput")
    wg = dr("wg", [D, D], F32, kind="ExternalInput")
    q = dr("q", [D, TPC], BF16, kind="ExternalOutput")
    k = dr("k", [D, TPC], BF16, kind="ExternalOutput")
    g = dr("g", [D, TPC], BF16, kind="ExternalOutput")
    v = dr("v", [TPC, D], BF16, kind="ExternalOutput")
    emit_proj_fm(nc, P, "pq", xnr, wq, q, AF.Silu, NDC, NDC)
    emit_proj_fm(nc, P, "pk", xnr, wk, k, AF.Silu, NDC, NDC)
    emit_proj_fm(nc, P, "pg", xnr, wg, g, AF.Sigmoid, NDC, NDC)
    emit_proj_tm(nc, P, "pv", xnr, wv, v, AF.Silu, NDC, D)



def emit_phase_scan(nc, P, dr, mode="full"):
    q = dr("q", [D, TPC], BF16, kind="ExternalInput")
    k = dr("k", [D, TPC], BF16, kind="ExternalInput")
    v = dr("v", [TPC, D], BF16, kind="ExternalInput")
    g = dr("g", [D, TPC], BF16, kind="ExternalInput")
    rankm = dr("rankm", [128, GRP], F32, kind="ExternalInput")
    ident = dr("ident", [128, 128], BF16, kind="ExternalInput")
    trilm = dr("trilm", [4, 128, BS], BF16, kind="ExternalInput")
    og = dr("og", [D, TPC], BF16, kind="ExternalOutput")

    sloc = nc.dram_tensor("sloc", [D, D], BF16)
    ccS = nc.dram_tensor("ccS", [GRP * D, D], BF16)
    oloc = nc.dram_tensor("oloc", [D, TPC], BF16)

    EH = 1024
    stack = contextlib.ExitStack()
    kvf = stack.enter_context(nc.sbuf_tensor("sc_kvf", [128, NDC * EH], F32))
    kvb = stack.enter_context(nc.sbuf_tensor("sc_kvb", [128, NDC * EH], BF16))
    qt = stack.enter_context(nc.sbuf_tensor("sc_qt", [128, NDC * BS], BF16))
    kt = stack.enter_context(nc.sbuf_tensor("sc_kt", [128, NDC * BS], BF16))
    ktm = stack.enter_context(nc.sbuf_tensor("sc_ktm", [128, 4 * D], BF16))
    vt = stack.enter_context(nc.sbuf_tensor("sc_vt", [128, 4 * EH], BF16))
    stb = stack.enter_context(nc.sbuf_tensor("sc_stb", [128, 4 * BS], BF16))
    idn = stack.enter_context(nc.sbuf_tensor("sc_idn", [128, 128], BF16))
    trm = stack.enter_context(nc.sbuf_tensor("sc_trm", [128, 4 * BS], BF16))
    rmc = stack.enter_context(nc.sbuf_tensor("sc_rmc", [128, GRP], F32))

    def kvf_s(dc, e0, w):
        return kvf[:, dc * EH + e0:dc * EH + e0 + w]

    def kvb_s(dc, e0, w):
        return kvb[:, dc * EH + e0:dc * EH + e0 + w]

    with stack:
        load_block(nc, P, [(idn[:, :], ident[:, :]), (rmc[:, :], rankm[:, :])]
                   + [(trm[:, kc * BS:(kc + 1) * BS], trilm[kc, :, :])
                      for kc in range(4)])

        for h in range(2):
            e0g = h * EH
            simple_block(nc, f"sc_z{h}", gpsimd=lambda gg: (
                gg.memset(kvf[:, :], 0.0), gg.memset(kvb[:, :], 0.0)))
            for c in range(NCH):
                t0 = c * BS
                load_block(nc, P, [
                    (qt[:, :].rearrange("p (c t) -> p c t", c=NDC),
                     csrc(q, 0, NDC, t0, BS)),
                    (kt[:, :].rearrange("p (c t) -> p c t", c=NDC),
                     csrc(k, 0, NDC, t0, BS)),
                    (vt[:, :].rearrange("p (s e) -> p s e", s=4),
                     csrc(v, t0 // 128, 4, e0g, EH)),
                ])

                with (
                    nc.psum_tensor(f"sc_pta{h}{c}", [128, 1024], BF16) as pta,
                    nc.psum_tensor(f"sc_ptb{h}{c}", [128, 1024], BF16) as ptb,
                    nc.Block() as blk,
                ):
                    psts = [pta, ptb]
                    b_mm, b_dve = P.n["mm"], P.n["dve"]

                    def tb(t):
                        i = 0
                        for dc in range(NDC):
                            for s in range(4):
                                if i >= 2:
                                    t.wait_ge(P.sem["dve"], b_dve + i - 1)
                                t.transpose(
                                    psts[i % 2][:, 0:128],
                                    kt[:, dc * BS + s * 128:dc * BS + (s + 1) * 128],
                                    idn[:, :]).then_inc(P.sem["mm"], 1)
                                i += 1
                    blk.tensor(tb)
                    P.n["mm"] = b_mm + 64

                    def te(v_):
                        i = 0
                        for dc in range(NDC):
                            for s in range(4):
                                v_.wait_ge(P.sem["mm"], b_mm + i + 1)
                                v_.tensor_copy(
                                    ktm[:, s * D + dc * 128:s * D + (dc + 1) * 128],
                                    psts[i % 2][:, 0:128]).then_inc(P.sem["dve"], 1)
                                i += 1
                    blk.vector(te)
                    P.n["dve"] = b_dve + 64

                def sc_evac(kc):
                    def evac(e, ps, out):
                        return e.scalar_tensor_tensor(
                            out, ps, 1.0, trm[:, kc * BS:(kc + 1) * BS],
                            ALU.mult, ALU.mult)
                    return evac
                jobs = [Job(
                    kchunks=[(kt[:, dc * BS + kc * 128:dc * BS + (kc + 1) * 128],
                              qt[:, dc * BS:(dc + 1) * BS]) for dc in range(NDC)],
                    evac=sc_evac(kc), store=None) for kc in range(4)]
                pipe_mm(nc, P, f"sc_s{h}{c}", jobs, BS, out_dt=BF16, nps=2,
                        evac_engine="vector",
                        persistent_out=lambda j: stb[:, j * BS:(j + 1) * BS])

                def o_evac(e, ps, out):
                    return ecopy(e, out, ps)
                jobs = []
                for ec in range(EH // 128):
                    kch = [(vt[:, s * EH + ec * 128:s * EH + (ec + 1) * 128],
                            stb[:, s * BS:(s + 1) * BS]) for s in range(4)]
                    if c > 0:
                        kch += [(kvb_s(dc, ec * 128, 128),
                                 qt[:, dc * BS:(dc + 1) * BS]) for dc in range(NDC)]
                    jobs.append(Job(
                        kchunks=kch, evac=o_evac,
                        store=oloc[e0g + ec * 128:e0g + (ec + 1) * 128, t0:t0 + BS]))
                pipe_mm(nc, P, f"sc_o{h}{c}", jobs, BS, out_dt=BF16, nps=2, nout=4,
                        evac_engine="scalar")

                with (
                    nc.psum_tensor(f"sc_pua{h}{c}", [128, 512], F32) as pua,
                    nc.psum_tensor(f"sc_pub{h}{c}", [128, 512], F32) as pub,
                    nc.psum_tensor(f"sc_puc{h}{c}", [128, 512], F32) as puc,
                    nc.psum_tensor(f"sc_pud{h}{c}", [128, 512], F32) as pud,
                    nc.Block() as blk,
                ):
                    pus = [pua, pub, puc, pud]
                    b_mm, b_dve = P.n["mm"], P.n["dve"]

                    def ub_t(t):
                        i = 0
                        for dc in range(NDC):
                            for eb in range(EH // 512):
                                if i >= 4:
                                    t.wait_ge(P.sem["dve"], b_dve + i - 3)
                                last = None
                                for s in range(4):
                                    last = t.matmul(
                                        pus[i % 4][:, :],
                                        ktm[:, s * D + dc * 128:s * D + (dc + 1) * 128],
                                        vt[:, s * EH + eb * 512:s * EH + (eb + 1) * 512],
                                        start=(s == 0), stop=(s == 3))
                                last.then_inc(P.sem["mm"], 1)
                                i += 1
                    blk.tensor(ub_t)
                    P.n["mm"] = b_mm + 32

                    def ub_v(v_):
                        i = 0
                        for dc in range(NDC):
                            for eb in range(EH // 512):
                                v_.wait_ge(P.sem["mm"], b_mm + i + 1)
                                v_.tensor_add(kvf_s(dc, eb * 512, 512),
                                              kvf_s(dc, eb * 512, 512),
                                              pus[i % 4][:, :])
                                v_.tensor_copy(kvb_s(dc, eb * 512, 512),
                                               kvf_s(dc, eb * 512, 512)) \
                                    .then_inc(P.sem["dve"], 1)
                                i += 1
                    blk.vector(ub_v)
                    P.n["dve"] = b_dve + 32

            store_block(nc, P, [
                (sloc[dc * 128:(dc + 1) * 128, e0g:e0g + EH],
                 kvb[:, dc * EH:(dc + 1) * EH]) for dc in range(NDC)])

    if mode == "full":
        with nc.Block() as blk:
            def ccb(gg):
                gg.collective_compute(
                    "AllGather", ALU.bypass,
                    replica_groups=[[0, 1, 2, 3], [4, 5, 6, 7]],
                    ins=[sloc.ap().opt()], outs=[ccS.ap().opt()]) \
                    .then_inc(P.sem["cc"])
                P.n["cc"] += 1
                gg.wait_ge(P.sem["cc"], P.n["cc"])
            blk.gpsimd(ccb)

    stack2 = contextlib.ExitStack()
    Pb = stack2.enter_context(nc.sbuf_tensor("sc_P", [128, NDC * D], BF16))
    Pr = stack2.enter_context(nc.sbuf_tensor("sc_Pr", [128, GRP * D], BF16))
    Pf = stack2.enter_context(nc.sbuf_tensor("sc_Pf", [128, D], F32))
    qa = stack2.enter_context(nc.sbuf_tensor("sc_qa", [128, 2 * NDC * TBL], BF16))
    otg = stack2.enter_context(nc.sbuf_tensor("sc_otg", [128, 16 * TBL], BF16))
    tmpf = stack2.enter_context(nc.sbuf_tensor("sc_tf2", [128, TBL], F32))
    with stack2:
        if mode == "nop":
            simple_block(nc, "sc_pz", gpsimd=lambda gg: (gg.memset(Pb[:, :], 0.0),))
        else:
            for dc in range(NDC):
                load_block(nc, P, [
                    (Pr[:, :].rearrange("p (r e) -> p r e", r=GRP),
                     ccS.ap().rearrange("(r c p) e -> p r c e", r=GRP, p=128)[:, :, dc, :])])

                def pacc(v_, dc=dc):
                    v_.tensor_scalar_mul(Pf[:, :], Pr[:, 0:D], rmc[:, 0:1])
                    for rr in range(1, GRP):
                        v_.scalar_tensor_tensor(Pf[:, :], Pr[:, rr * D:(rr + 1) * D],
                                                rmc[:, rr:rr + 1], Pf[:, :],
                                                ALU.mult, ALU.add)
                    v_.tensor_copy(Pb[:, dc * D:(dc + 1) * D], Pf[:, :])
                simple_block(nc, f"sc_pa{dc}", vector=pacc)

        jobs = []
        for b in range(NTB):
            for ec in range(NDC):
                loads = []
                if ec == 0:
                    loads += [(qa[:, ((b % 2) * NDC + dc) * TBL:
                                  ((b % 2) * NDC + dc + 1) * TBL],
                               q[dc * 128:(dc + 1) * 128, b * TBL:(b + 1) * TBL])
                              for dc in range(NDC)]
                j = b * NDC + ec
                sl = j % 8
                loads += [(otg[:, sl * TBL:(sl + 1) * TBL],
                           oloc[ec * 128:(ec + 1) * 128, b * TBL:(b + 1) * TBL]),
                          (otg[:, (8 + sl) * TBL:(8 + sl + 1) * TBL],
                           g[ec * 128:(ec + 1) * 128, b * TBL:(b + 1) * TBL])]

                def mk_evac(sl=sl):
                    def evac(e, ps, out):
                        e.tensor_tensor(tmpf[:, :], ps,
                                        otg[:, sl * TBL:(sl + 1) * TBL], ALU.add)
                        return e.tensor_tensor(
                            out, tmpf[:, :],
                            otg[:, (8 + sl) * TBL:(8 + sl + 1) * TBL], ALU.mult)
                    return evac
                jobs.append(Job(
                    kchunks=[(Pb[:, dc * D + ec * 128:dc * D + (ec + 1) * 128],
                              qa[:, ((b % 2) * NDC + dc) * TBL:
                                 ((b % 2) * NDC + dc + 1) * TBL])
                             for dc in range(NDC)],
                    evac=mk_evac(),
                    store=og[ec * 128:(ec + 1) * 128, b * TBL:(b + 1) * TBL],
                    loads=loads))
        pipe_mm(nc, P, "sc_x", jobs, TBL, out_dt=BF16, nps=2, nout=4,
                evac_engine="vector", ld_ahead=6)




def emit_glu(nc, P, name, xin_dram, wg_dram, wu_dram, h_dram, n_in_c, n_i_c,
             gs_dram=None, us_dram=None, wg_off=0, wu_off=0):
    K = n_in_c
    stack = contextlib.ExitStack()
    xin = stack.enter_context(nc.sbuf_tensor(f"{name}_x", [128, K * NTB * TBL], BF16))
    wst = stack.enter_context(nc.sbuf_tensor(f"{name}_w", [128, 4 * 2 * K * 128], BF16))
    scl = stack.enter_context(nc.sbuf_tensor(f"{name}_s", [128, 2 * max(1, n_i_c)], F32))
    hg = stack.enter_context(nc.sbuf_tensor(f"{name}_hg", [128, 3 * 2 * TBL], BF16))
    stage = stack.enter_context(nc.sbuf_tensor(f"{name}_st", [128, 4 * TBL], BF16))
    psa = stack.enter_context(nc.psum_tensor(f"{name}_pa", [128, 2 * TBL], F32))
    psb = stack.enter_context(nc.psum_tensor(f"{name}_pb", [128, 2 * TBL], F32))
    with stack:
        pairs = [(xin[:, :].rearrange("p (c b t) -> p c b t", c=K, b=NTB)[:, :, b, :],
                  csrc(xin_dram, 0, K, b * TBL, TBL))
                 for b in range(NTB)]
        if gs_dram is not None:
            pairs += [(scl[:, ic:ic + 1], gs_dram[ic * 128:(ic + 1) * 128, 0:1])
                      for ic in range(n_i_c)]
            pairs += [(scl[:, n_i_c + ic:n_i_c + ic + 1],
                       us_dram[ic * 128:(ic + 1) * 128, 0:1]) for ic in range(n_i_c)]
        load_block(nc, P, pairs)

        def xi(c, b):
            return xin[:, (c * NTB + b) * TBL:(c * NTB + b + 1) * TBL]

        pss = [psa, psb]
        njob = n_i_c * NTB
        b_ld, b_mm, b_act, b_dve, b_st = (P.n["ld"], P.n["mm"], P.n["act"],
                                          P.n["dve"], P.n["st"])
        with nc.Block() as blk:
            @blk.gpsimd
            def _(g):
                nld = 0
                for ic in range(n_i_c):
                    wsl = ic % 4
                    j0 = ic * NTB
                    if j0 > 2:
                        g.wait_ge(P.sem["dve"], b_dve + j0 - 2)
                    if ic > 0:
                        g.wait_ge(P.sem["ld"], b_ld + 32 * ic)
                    g.dma_start(
                        out=wst[:, (2 * wsl) * K * 128:(2 * wsl + 1) * K * 128]
                            .rearrange("p (c o) -> p c o", c=K),
                        in_=wstrip_ap(wg_dram, K, wg_off + ic * 128, 128),
                    ).then_inc(P.sem["ld"], 16)
                    g.dma_start(
                        out=wst[:, (2 * wsl + 1) * K * 128:(2 * wsl + 2) * K * 128]
                            .rearrange("p (c o) -> p c o", c=K),
                        in_=wstrip_ap(wu_dram, K, wu_off + ic * 128, 128),
                    ).then_inc(P.sem["ld"], 16)
                    nld += 32
            P.n["ld"] = b_ld + 32 * n_i_c

            @blk.tensor
            def _(t):
                for ic in range(n_i_c):
                    wsl = ic % 4
                    t.wait_ge(P.sem["ld"], b_ld + 32 * (ic + 1))
                    for b in range(NTB):
                        j = ic * NTB + b
                        if j >= 2:
                            t.wait_ge(P.sem["act"], b_act + j - 1)
                        for c in range(K):
                            t.matmul(pss[j % 2][:, 0:TBL],
                                     wst[:, ((2 * wsl) * K + c) * 128:
                                         ((2 * wsl) * K + c + 1) * 128],
                                     xi(c, b), start=(c == 0), stop=(c == K - 1))
                        last = None
                        for c in range(K):
                            last = t.matmul(
                                pss[j % 2][:, TBL:2 * TBL],
                                wst[:, ((2 * wsl + 1) * K + c) * 128:
                                    ((2 * wsl + 1) * K + c + 1) * 128],
                                xi(c, b), start=(c == 0), stop=(c == K - 1))
                        last.then_inc(P.sem["mm"], 1)
            P.n["mm"] = b_mm + njob

            @blk.scalar
            def _(s):
                for ic in range(n_i_c):
                    for b in range(NTB):
                        j = ic * NTB + b
                        s.wait_ge(P.sem["mm"], b_mm + j + 1)
                        if j >= 3:
                            s.wait_ge(P.sem["dve"], b_dve + j - 2)
                        sl = j % 3
                        if gs_dram is not None:
                            s.activation(hg[:, (2 * sl) * TBL:(2 * sl + 1) * TBL],
                                         pss[j % 2][:, 0:TBL], AF.Silu,
                                         scale=scl[:, ic:ic + 1])
                            s.activation(hg[:, (2 * sl + 1) * TBL:(2 * sl + 2) * TBL],
                                         pss[j % 2][:, TBL:2 * TBL], AF.Copy,
                                         scale=scl[:, n_i_c + ic:n_i_c + ic + 1]) \
                                .then_inc(P.sem["act"], 1)
                        else:
                            s.activation(hg[:, (2 * sl) * TBL:(2 * sl + 1) * TBL],
                                         pss[j % 2][:, 0:TBL], AF.Silu)
                            s.activation(hg[:, (2 * sl + 1) * TBL:(2 * sl + 2) * TBL],
                                         pss[j % 2][:, TBL:2 * TBL], AF.Copy) \
                                .then_inc(P.sem["act"], 1)
            P.n["act"] = b_act + njob

            @blk.vector
            def _(v):
                for ic in range(n_i_c):
                    for b in range(NTB):
                        j = ic * NTB + b
                        v.wait_ge(P.sem["act"], b_act + j + 1)
                        if j >= 4:
                            v.wait_ge(P.sem["st"], b_st + 16 * (j - 3))
                        sl = j % 3
                        v.tensor_mul(stage[:, (j % 4) * TBL:(j % 4 + 1) * TBL],
                                     hg[:, (2 * sl) * TBL:(2 * sl + 1) * TBL],
                                     hg[:, (2 * sl + 1) * TBL:(2 * sl + 2) * TBL]) \
                            .then_inc(P.sem["dve"], 1)
            P.n["dve"] = b_dve + njob

            @blk.sync
            def _(sy):
                for ic in range(n_i_c):
                    for b in range(NTB):
                        j = ic * NTB + b
                        sy.wait_ge(P.sem["dve"], b_dve + j + 1)
                        sy.dma_start(
                            out=h_dram[ic * 128:(ic + 1) * 128, b * TBL:(b + 1) * TBL],
                            in_=stage[:, (j % 4) * TBL:(j % 4 + 1) * TBL],
                        ).then_inc(P.sem["st"], 16)
                sy.wait_ge(P.sem["st"], b_st + 16 * njob)
            P.n["st"] = b_st + 16 * njob


def emit_down(nc, P, name, h_dram, w_dram, out_dram, res_dram, n_i_c, n_out_c,
              ds_dram=None, out_dt=F32, nhalf=2):
    K = n_i_c
    oc_half = n_out_c // nhalf
    stack = contextlib.ExitStack()
    hst = stack.enter_context(nc.sbuf_tensor(f"{name}_h", [128, K * TBL], BF16))
    wst = stack.enter_context(
        nc.sbuf_tensor(f"{name}_w", [128, K * oc_half * 128], BF16))
    scl = stack.enter_context(nc.sbuf_tensor(f"{name}_s", [128, max(1, n_out_c)], F32))
    res = stack.enter_context(nc.sbuf_tensor(f"{name}_r", [128, 8 * TBL], F32))
    with stack:
        if ds_dram is not None:
            load_block(nc, P, [
                (scl[:, oc:oc + 1], ds_dram[oc * 128:(oc + 1) * 128, 0:1])
                for oc in range(n_out_c)])

        jobs = []
        for oh in range(nhalf):
            for b in range(NTB):
                for oc in range(oc_half):
                    j = len(jobs)
                    loads = []
                    gate = None
                    if b == 0 and oc == 0:
                        loads += [(wst[:, :].rearrange("p (c o) -> p c o", c=K)
                                   [:, 4 * qq:4 * (qq + 1), :],
                                   csrc(w_dram, 4 * qq, 4, oh * oc_half * 128,
                                        oc_half * 128))
                                  for qq in range(K // 4)]
                        gate = j - 1
                    if oc == 0:
                        loads += [(hst[:, :].rearrange("p (c t) -> p c t", c=K),
                                   csrc(h_dram, 0, K, b * TBL, TBL))]
                        if gate is None:
                            gate = j - 1
                    ocg = oh * oc_half + oc
                    rsl = j % 8
                    loads += [(res[:, rsl * TBL:(rsl + 1) * TBL],
                               res_dram[ocg * 128:(ocg + 1) * 128,
                                        b * TBL:(b + 1) * TBL])]

                    def mk_evac(ocg=ocg, rsl=rsl):
                        def evac(e, ps, out):
                            sc = scl[:, ocg:ocg + 1] if ds_dram is not None else 1.0
                            return e.scalar_tensor_tensor(
                                out, ps, sc, res[:, rsl * TBL:(rsl + 1) * TBL],
                                ALU.mult, ALU.add)
                        return evac

                    jobs.append(Job(
                        kchunks=[(wst[:, c * oc_half * 128 + oc * 128:
                                      c * oc_half * 128 + (oc + 1) * 128],
                                  hst[:, c * TBL:(c + 1) * TBL]) for c in range(K)],
                        evac=mk_evac(),
                        store=out_dram[ocg * 128:(ocg + 1) * 128,
                                       b * TBL:(b + 1) * TBL],
                        loads=loads, ld_gate=gate))
        pipe_mm(nc, P, name, jobs, TBL, out_dt=out_dt, nps=2, nout=4,
                evac_engine="vector", ld_ahead=6)


def emit_phase_wp_res(nc, P, dr):
    og = dr("og", [D, TPC], BF16, kind="ExternalInput")
    wp = dr("wp", [D, D], F32, kind="ExternalInput")
    x = dr("x", [D, TPC], F32, kind="ExternalInput")
    xmid = dr("xmid", [D, TPC], F32, kind="ExternalOutput")
    emit_proj_fm(nc, P, "wp", og, wp, xmid, None, NDC, NDC, res_dram=x, out_dt=F32)


def emit_phase_ffn0(nc, P, dr):
    xn = dr("xn", [D, TPC], BF16, kind="ExternalInput")
    wgate = dr("wgate", [D, INTER0], F32, kind="ExternalInput")
    wup = dr("wup", [D, INTER0], F32, kind="ExternalInput")
    gs = dr("gs", [INTER0, 1], F32, kind="ExternalInput")
    us = dr("us", [INTER0, 1], F32, kind="ExternalInput")
    h = dr("h", [INTER0, TPC], BF16, kind="ExternalOutput")
    emit_glu(nc, P, "f0", xn, wgate, wup, h, NDC, NIC0, gs_dram=gs, us_dram=us)


def emit_phase_down0(nc, P, dr):
    h = dr("h", [INTER0, TPC], BF16, kind="ExternalInput")
    wdown = dr("wdown", [INTER0, D], F32, kind="ExternalInput")
    ds = dr("ds", [D, 1], F32, kind="ExternalInput")
    xmid = dr("xmid", [D, TPC], F32, kind="ExternalInput")
    x1 = dr("x1", [D, TPC], F32, kind="ExternalOutput")
    emit_down(nc, P, "d0", h, wdown, x1, xmid, NIC0, NDC, ds_dram=ds)




def emit_fm2tm(nc, P, name, in_dram, out_dram, n_r_c=NDC, n_t_c=TPC // 128):
    stack = contextlib.ExitStack()
    xall = stack.enter_context(
        nc.sbuf_tensor(f"{name}_x", [128, n_r_c * n_t_c * 128], BF16))
    rowb = stack.enter_context(nc.sbuf_tensor(f"{name}_rb", [128, 2 * n_r_c * 128], BF16))
    pta = stack.enter_context(nc.psum_tensor(f"{name}_pa", [128, 1024], BF16))
    ptb = stack.enter_context(nc.psum_tensor(f"{name}_pb", [128, 1024], BF16))
    idn = stack.enter_context(nc.sbuf_tensor(f"{name}_id", [128, 128], BF16))
    with stack:
        simple_block(nc, f"{name}_id", gpsimd=lambda g: (
            g.iota(idn[:, :], [[1, 128]], channel_multiplier=-1,
                   allow_small_or_imprecise_dtypes=True),))
        simple_block(nc, f"{name}_id2", vector=lambda v: (
            v.tensor_single_scalar(idn[:, :], idn[:, :], 0.0, ALU.is_equal),))
        load_block(nc, P, [
            (xall[:, (rc * n_t_c + tc) * 128:(rc * n_t_c + tc + 1) * 128],
             in_dram[rc * 128:(rc + 1) * 128, tc * 128:(tc + 1) * 128])
            for rc in range(n_r_c) for tc in range(n_t_c)])

        pss = [pta, ptb]
        b_mm, b_dve, b_st = P.n["mm"], P.n["dve"], P.n["st"]
        with nc.Block() as blk:
            @blk.tensor
            def _(t):
                i = 0
                for tc in range(n_t_c):
                    for rc in range(n_r_c):
                        if i >= 2:
                            t.wait_ge(P.sem["dve"], b_dve + i - 1)
                        t.transpose(
                            pss[i % 2][:, 0:128],
                            xall[:, (rc * n_t_c + tc) * 128:(rc * n_t_c + tc + 1) * 128],
                            idn[:, :]).then_inc(P.sem["mm"], 1)
                        i += 1
            P.n["mm"] = b_mm + n_t_c * n_r_c

            @blk.vector
            def _(v):
                i = 0
                for tc in range(n_t_c):
                    for rc in range(n_r_c):
                        v.wait_ge(P.sem["mm"], b_mm + i + 1)
                        if tc >= 2 and rc == 0:
                            v.wait_ge(P.sem["st"], b_st + 16 * (tc - 1))
                        v.tensor_copy(
                            rowb[:, ((tc % 2) * n_r_c + rc) * 128:
                                 ((tc % 2) * n_r_c + rc + 1) * 128],
                            pss[i % 2][:, 0:128]).then_inc(P.sem["dve"], 1)
                        i += 1
            P.n["dve"] = b_dve + n_t_c * n_r_c

            @blk.sync
            def _(sy):
                for tc in range(n_t_c):
                    sy.wait_ge(P.sem["dve"], b_dve + (tc + 1) * n_r_c)
                    if tc > 0:
                        sy.wait_ge(P.sem["st"], b_st + 16 * tc)
                    sy.dma_start(
                        out=out_dram[tc * 128:(tc + 1) * 128, :],
                        in_=rowb[:, (tc % 2) * n_r_c * 128:
                                 ((tc % 2) + 1) * n_r_c * 128],
                    ).then_inc(P.sem["st"], 16)
                sy.wait_ge(P.sem["st"], b_st + 16 * n_t_c)
            P.n["st"] = b_st + 16 * n_t_c


def emit_phase_ln1(nc, P, dr):
    x1 = dr("x1", [D, TPC], F32, kind="ExternalInput")
    xn1 = dr("xn1", [D, TPC], BF16, kind="ExternalOutput")
    xn1t = dr("xn1t", [TPC, D], BF16, kind="ExternalOutput")
    emit_ln(nc, P, x1, [(xn1, "ln")], ln_pref="l1")
    emit_fm2tm(nc, P, "tq", xn1, xn1t)


def emit_phase_qkv1(nc, P, dr):
    xn1 = dr("xn1", [D, TPC], BF16, kind="ExternalInput")
    wq = dr("wq", [D, D], F32, kind="ExternalInput")
    wk = dr("wk", [D, NKV * HS], F32, kind="ExternalInput")
    wv = dr("wv", [D, NKV * HS], F32, kind="ExternalInput")
    q1 = dr("q1", [D, TPC], BF16, kind="ExternalOutput")
    k1 = dr("k1", [TPC, NKV * HS], BF16, kind="ExternalOutput")
    v1 = dr("v1", [TPC, NKV * HS], BF16, kind="ExternalOutput")
    emit_proj_fm(nc, P, "q1", xn1, wq, q1, None, NDC, NDC)
    emit_proj_tm(nc, P, "k1", xn1, wk, k1, None, NDC, NKV * HS)
    emit_proj_tm(nc, P, "v1", xn1, wv, v1, None, NDC, NKV * HS)


def emit_phase_wo_res(nc, P, dr):
    y = dr("y", [D, TPC], BF16, kind="ExternalInput")
    wo = dr("wo", [D, D], F32, kind="ExternalInput")
    res = dr("x1", [D, TPC], F32, kind="ExternalInput")
    out = dr("xa", [D, TPC], F32, kind="ExternalOutput")
    emit_proj_fm(nc, P, "wo", y, wo, out, None, NDC, NDC, res_dram=res, out_dt=F32)


def emit_phase_shared(nc, P, dr):
    xn1 = dr("xn1", [D, TPC], BF16, kind="ExternalInput")
    wug = dr("wug", [D, 2 * JSH], F32, kind="ExternalInput")
    h = dr("h", [JSH, TPC], BF16, kind="ExternalOutput")
    emit_glu(nc, P, "sh", xn1, wug, wug, h, NDC, NIC0, wg_off=0, wu_off=JSH)


def emit_phase_shdown(nc, P, dr):
    h = dr("h", [JSH, TPC], BF16, kind="ExternalInput")
    wdown = dr("wdown", [JSH, D], F32, kind="ExternalInput")
    xa = dr("xa", [D, TPC], F32, kind="ExternalInput")
    xa2 = dr("xa2", [D, TPC], F32, kind="ExternalOutput")
    emit_down(nc, P, "sd", h, wdown, xa2, xa, NIC0, NDC, ds_dram=None)




def emit_phase_attn1(nc, P, dr):
    q1 = dr("q1", [D, TPC], BF16, kind="ExternalInput")
    k1 = dr("k1", [TPC, NKV * HS], BF16, kind="ExternalInput")
    v1 = dr("v1", [TPC, NKV * HS], BF16, kind="ExternalInput")
    khalo = dr("khalo", [1152, NKV * HS], BF16, kind="ExternalInput")
    vhalo = dr("vhalo", [1152, NKV * HS], BF16, kind="ExternalInput")
    bandml = dr("bandml", [4, 128, QT], BF16, kind="ExternalInput")
    bandmr = dr("bandmr", [4, 128, QT], BF16, kind="ExternalInput")
    sinkm = dr("sinkm", [NQT, 4, QT], BF16, kind="ExternalInput")
    ninv = dr("ninv", [NQT, QT], F32, kind="ExternalInput")
    y = dr("y", [D, TPC], BF16, kind="ExternalOutput")

    NT_H = 9
    NT_A = 25

    stack = contextlib.ExitStack()
    kfm = stack.enter_context(nc.sbuf_tensor("a1_kfm", [128, NKV * 3200], BF16))
    vbuf = stack.enter_context(nc.sbuf_tensor("a1_vb", [128, NT_A * NKV * HS], BF16))
    qh = stack.enter_context(nc.sbuf_tensor("a1_qh", [128, 2 * QT], BF16))
    pbuf = stack.enter_context(nc.sbuf_tensor("a1_pb", [128, 13 * QT], BF16))
    mls = stack.enter_context(nc.sbuf_tensor("a1_ml", [128, 8 * QT], BF16))
    snk = stack.enter_context(nc.sbuf_tensor("a1_sm", [128, NQT * QT], BF16))
    niv = stack.enter_context(nc.sbuf_tensor("a1_ni", [1, NQT * QT], F32))
    rows = stack.enter_context(nc.sbuf_tensor("a1_rw", [1, 2 * QT], F32))
    denb = stack.enter_context(nc.sbuf_tensor("a1_db", [128, QT], F32))
    ybuf = stack.enter_context(nc.sbuf_tensor("a1_yb", [128, 4 * QT], BF16))
    ones_c = stack.enter_context(nc.sbuf_tensor("a1_oc", [128, 1], BF16))
    ones_r = stack.enter_context(nc.sbuf_tensor("a1_or", [1, 128], F32))
    idn = stack.enter_context(nc.sbuf_tensor("a1_id", [128, 128], BF16))
    bsh = stack.enter_context(nc.sbuf_tensor("a1_bs", [128, 1], F32))
    em12 = stack.enter_context(nc.sbuf_tensor("a1_em", [1, 2], F32))
    em12b = stack.enter_context(nc.sbuf_tensor("a1_emb", [1, 1], BF16))

    with stack:
        simple_block(nc, "a1_i0", gpsimd=lambda g: (
            g.iota(idn[:, :], [[1, 128]], channel_multiplier=-1,
                   allow_small_or_imprecise_dtypes=True),
            g.memset(ones_c[:, :], 1.0),
            g.memset(ones_r[:, :], 1.0),
            g.memset(em12[:, 0:1], 0.0),
            g.memset(bsh[:, :], ESHIFT)))
        simple_block(nc, "a1_em1", scalar=lambda s: (
            s.activation(em12b[:, :], em12[:, 0:1], AF.Exp,
                         bias=bsh[0:1, :], scale=SCALE_ATT),))
        simple_block(nc, "a1_em2", vector=lambda v: (
            v.tensor_scalar_mul(em12[:, 1:2], em12b[:, :], -1.0),))
        simple_block(nc, "a1_i1", vector=lambda v: (
            v.tensor_single_scalar(idn[:, :], idn[:, :], 0.0, ALU.is_equal),))

        load_block(nc, P, [
            (vbuf[:, 0:NT_H * NKV * HS].rearrange("p (b f) -> p b f", b=NT_H),
             csrc(vhalo, 0, NT_H, 0, NKV * HS)),
            (vbuf[:, NT_H * NKV * HS:].rearrange("p (b f) -> p b f", b=16),
             csrc(v1, 0, 16, 0, NKV * HS)),
        ] + [
            (mls[:, i * QT:(i + 1) * QT], bandml[i, :, :]) for i in range(4)
        ] + [
            (mls[:, (4 + i) * QT:(5 + i) * QT], bandmr[i, :, :]) for i in range(4)
        ] + [
            (snk[0:4, qt * QT:(qt + 1) * QT], sinkm[qt, :, :]) for qt in range(NQT)
        ] + [
            (niv[:, qt * QT:(qt + 1) * QT], ninv[qt:qt + 1, :]) for qt in range(NQT)
        ])

        with (
            nc.psum_tensor("a1_pta", [128, 1024], BF16) as pta,
            nc.psum_tensor("a1_ptb", [128, 1024], BF16) as ptb,
            nc.sbuf_tensor("a1_krow", [128, 2 * NKV * HS], BF16) as krow,
            nc.Block() as blk,
        ):
            pss = [pta, ptb]
            b_mm, b_dve, b_ld = P.n["mm"], P.n["dve"], P.n["ld"]
            NTT = 25 * NKV

            @blk.gpsimd
            def _(g):
                for bt in range(25):
                    if bt >= 2:
                        g.wait_ge(P.sem["mm"], b_mm + (bt - 1) * NKV)
                    if bt > 0:
                        g.wait_ge(P.sem["ld"], b_ld + 16 * bt)
                    src = (khalo[bt * 128:(bt + 1) * 128, :] if bt < 9 else
                           k1[(bt - 9) * 128:(bt - 9 + 1) * 128, :])
                    g.dma_start(
                        out=krow[:, (bt % 2) * NKV * HS:((bt % 2) + 1) * NKV * HS],
                        in_=src).then_inc(P.sem["ld"], 16)
            P.n["ld"] = b_ld + 25 * 16

            @blk.tensor
            def _(t):
                i = 0
                for bt in range(25):
                    t.wait_ge(P.sem["ld"], b_ld + 16 * (bt + 1))
                    for kv in range(NKV):
                        if i >= 2:
                            t.wait_ge(P.sem["dve"], b_dve + i - 1)
                        t.transpose(
                            pss[i % 2][:, 0:128],
                            krow[:, ((bt % 2) * NKV + kv) * HS:
                                 ((bt % 2) * NKV + kv + 1) * HS],
                            idn[:, :]).then_inc(P.sem["mm"], 1)
                        i += 1
            P.n["mm"] = b_mm + NTT

            @blk.vector
            def _(v_):
                i = 0
                for bt in range(25):
                    for kv in range(NKV):
                        v_.wait_ge(P.sem["mm"], b_mm + i + 1)
                        if bt < 8:
                            dst = kfm[:, kv * 3200 + bt * 128:kv * 3200 + (bt + 1) * 128]
                            src = pss[i % 2][:, 0:128]
                        elif bt == 8:
                            dst = kfm[:, kv * 3200 + 3072:kv * 3200 + 3076]
                            src = pss[i % 2][:, 0:4]
                        else:
                            c0 = 1024 + (bt - 9) * 128
                            dst = kfm[:, kv * 3200 + c0:kv * 3200 + c0 + 128]
                            src = pss[i % 2][:, 0:128]
                        v_.tensor_copy(dst, src).then_inc(P.sem["dve"], 1)
                        i += 1
            P.n["dve"] = b_dve + NTT

        MASKED = list(range(4)) + list(range(8, 12))
        b_ld, b_mm = P.n["ld"], P.n["mm"]
        b_act, b_dve, b_st = P.n["act"], P.n["dve"], P.n["st"]
        NPAIR = NH * NQT
        PMM, PACT, PDVE = 40, 13, 11
        with (
            nc.psum_tensor("a1_s0", [128, QT], F32) as S0,
            nc.psum_tensor("a1_s1", [128, QT], F32) as S1,
            nc.psum_tensor("a1_s2", [128, QT], F32) as S2,
            nc.psum_tensor("a1_sk", [4, QT], F32) as SK,
            nc.psum_tensor("a1_y0", [128, QT], F32) as Y0,
            nc.psum_tensor("a1_y1", [128, QT], F32) as Y1,
            nc.psum_tensor("a1_dn", [1, QT], F32) as DEN,
            nc.psum_tensor("a1_bc", [128, QT], F32) as BCp,
            nc.Block() as blk,
        ):
            Ss = [S0, S1, S2]
            Ys = [Y0, Y1]

            @blk.gpsimd
            def _(g):
                for p in range(NPAIR):
                    h, qt = p // NQT, p % NQT
                    if p >= 2:
                        g.wait_ge(P.sem["mm"], b_mm + (p - 1) * PMM)
                    if p > 0:
                        g.wait_ge(P.sem["ld"], b_ld + 16 * p)
                    g.dma_start(out=qh[:, (p % 2) * QT:(p % 2 + 1) * QT],
                                in_=q1[h * HS:(h + 1) * HS, qt * QT:(qt + 1) * QT]) \
                        .then_inc(P.sem["ld"], 16)
            P.n["ld"] = b_ld + NPAIR * 16

            @blk.tensor
            def _(t):
                for p in range(NPAIR):
                    h, qt = p // NQT, p % NQT
                    kv = h // (NH // NKV)
                    M0 = b_mm + p * PMM
                    A0 = b_act + p * PACT
                    D0 = b_dve + p * PDVE
                    t.wait_ge(P.sem["ld"], b_ld + 16 * (p + 1))
                    for kc in range(12):
                        if p > 0 and kc < 3:
                            t.wait_ge(P.sem["act"], A0)
                        elif kc >= 3:
                            t.wait_ge(P.sem["act"], A0 + kc - 2)
                        c0 = kv * 3200 + qt * QT + kc * 128
                        t.matmul(Ss[kc % 3][:, :], kfm[:, c0:c0 + 128],
                                 qh[:, (p % 2) * QT:(p % 2 + 1) * QT],
                                 start=True, stop=True).then_inc(P.sem["mm"], 1)
                    t.matmul(SK[:, :], kfm[:, kv * 3200 + 3072:kv * 3200 + 3076],
                             qh[:, (p % 2) * QT:(p % 2 + 1) * QT],
                             start=True, stop=True).then_inc(P.sem["mm"], 1)
                    t.wait_ge(P.sem["act"], A0 + 13)
                    t.wait_ge(P.sem["dve"], D0 + 9)
                    for kc in range(12):
                        bt = qt * 4 + kc
                        vt_i = bt if bt < 8 else bt + 1
                        vcol = (vt_i * NKV + kv) * HS
                        t.matmul(Ys[p % 2][:, :], vbuf[:, vcol:vcol + HS],
                                 pbuf[:, kc * QT:(kc + 1) * QT],
                                 start=(kc == 0), stop=False).then_inc(P.sem["mm"], 1)
                    t.matmul(Ys[p % 2][:, :],
                             vbuf[0:4, (8 * NKV + kv) * HS:(8 * NKV + kv + 1) * HS],
                             pbuf[0:4, 12 * QT:13 * QT],
                             start=False, stop=True).then_inc(P.sem["mm"], 1)
                    for kc in range(12):
                        t.matmul(DEN[:, :], ones_c[:, :],
                                 pbuf[:, kc * QT:(kc + 1) * QT],
                                 start=(kc == 0), stop=False).then_inc(P.sem["mm"], 1)
                    t.matmul(DEN[:, :], ones_c[0:4, :], pbuf[0:4, 12 * QT:13 * QT],
                             start=False, stop=True).then_inc(P.sem["mm"], 1)
                    t.wait_ge(P.sem["dve"], D0 + 10)
                    t.matmul(BCp[:, :], ones_r[:, :], rows[:, QT:2 * QT],
                             start=True, stop=True).then_inc(P.sem["mm"], 1)
            P.n["mm"] = b_mm + NPAIR * PMM

            @blk.scalar
            def _(s):
                for p in range(NPAIR):
                    M0 = b_mm + p * PMM
                    for kc in range(12):
                        s.wait_ge(P.sem["mm"], M0 + kc + 1)
                        s.activation(pbuf[:, kc * QT:(kc + 1) * QT],
                                     Ss[kc % 3][:, :], AF.Exp,
                                     bias=bsh[:, :], scale=SCALE_ATT) \
                            .then_inc(P.sem["act"], 1)
                    s.wait_ge(P.sem["mm"], M0 + 13)
                    s.activation(pbuf[0:4, 12 * QT:13 * QT], SK[:, :], AF.Exp,
                                 bias=bsh[0:4, :], scale=SCALE_ATT) \
                        .then_inc(P.sem["act"], 1)
            P.n["act"] = b_act + NPAIR * PACT

            @blk.vector
            def _(v_):
                for p in range(NPAIR):
                    h, qt = p // NQT, p % NQT
                    A0 = b_act + p * PACT
                    M0 = b_mm + p * PMM
                    for kc in MASKED:
                        v_.wait_ge(P.sem["act"], A0 + kc + 1)
                        mcol = kc if kc < 4 else (kc - 8) + 4
                        v_.tensor_mul(pbuf[:, kc * QT:(kc + 1) * QT],
                                      pbuf[:, kc * QT:(kc + 1) * QT],
                                      mls[:, mcol * QT:(mcol + 1) * QT]) \
                            .then_inc(P.sem["dve"], 1)
                    v_.wait_ge(P.sem["act"], A0 + 13)
                    v_.tensor_mul(pbuf[0:4, 12 * QT:13 * QT],
                                  pbuf[0:4, 12 * QT:13 * QT],
                                  snk[0:4, qt * QT:(qt + 1) * QT]) \
                        .then_inc(P.sem["dve"], 1)
                    v_.wait_ge(P.sem["mm"], M0 + 39)
                    cv = Ch(v_, P)
                    cv.scalar_tensor_tensor(rows[:, 0:QT],
                                            niv[:, qt * QT:(qt + 1) * QT],
                                            em12[:, 1:2], DEN[:, :],
                                            ALU.mult, ALU.add)
                    v_.reciprocal(rows[:, QT:2 * QT], rows[:, 0:QT]) \
                        .then_inc(P.sem["dve"], 1)
                    v_.wait_ge(P.sem["mm"], M0 + 40)
                    cv.tensor_copy(denb[:, :], BCp[:, :])
                    if p >= 4:
                        v_.wait_ge(P.sem["st"], b_st + 16 * (p - 3))
                    v_.tensor_mul(ybuf[:, (p % 4) * QT:(p % 4 + 1) * QT],
                                  Ys[p % 2][:, :], denb[:, :]).then_inc(P.sem["dve"], 1)
            P.n["dve"] = b_dve + NPAIR * PDVE

            @blk.sync
            def _(sy):
                for p in range(NPAIR):
                    h, qt = p // NQT, p % NQT
                    sy.wait_ge(P.sem["dve"], b_dve + p * PDVE + 11)
                    if p > 0:
                        sy.wait_ge(P.sem["st"], b_st + 16 * p)
                    sy.dma_start(out=y[h * HS:(h + 1) * HS, qt * QT:(qt + 1) * QT],
                                 in_=ybuf[:, (p % 4) * QT:(p % 4 + 1) * QT]) \
                        .then_inc(P.sem["st"], 16)
                sy.wait_ge(P.sem["st"], b_st + 16 * NPAIR)
            P.n["st"] = b_st + NPAIR * 16



def emit_phase_route(nc, P, dr, debug=False):
    xn1 = dr("xn1", [D, TPC], BF16, kind="ExternalInput")
    wr = dr("wr", [D, NE], F32, kind="ExternalInput")
    cmask = dr("cmask", [8, 1], F32, kind="ExternalInput")
    sdst0 = dr("sdst0", [TPC, 1], I32, kind="ExternalOutput")
    sdst1 = dr("sdst1", [TPC, 1], I32, kind="ExternalOutput")
    gidx0 = dr("gidx0", [TPC, 1], I32, kind="ExternalOutput")
    gidx1 = dr("gidx1", [TPC, 1], I32, kind="ExternalOutput")
    cw0 = dr("cw0", [TPC, 1], F32, kind="ExternalOutput")
    cw1 = dr("cw1", [TPC, 1], F32, kind="ExternalOutput")
    cnt_in = nc.dram_tensor("cnt_in", [1, NE], F32)
    cnt_out = nc.dram_tensor("cnt_out", [NCORE, NE], F32)
    if debug:
        dbg_lg = dr("dbg_lg", [128, 16 * NE], F32, kind="ExternalOutput")
        dbg_oh = dr("dbg_oh", [128, 16 * NE], F32, kind="ExternalOutput")
        dbg_cum = dr("dbg_cum", [128, 16 * NE], F32, kind="ExternalOutput")
        dbg_oh2 = dr("dbg_oh2", [128, 16 * NE], F32, kind="ExternalOutput")
        dbg_cols = dr("dbg_cols", [128, 24 * 16], F32, kind="ExternalOutput")

    NT = TPC // 128
    stack = contextlib.ExitStack()
    xr = stack.enter_context(nc.sbuf_tensor("rt_x", [128, NDC * 128], BF16))
    wrs = stack.enter_context(nc.sbuf_tensor("rt_w", [128, NDC * NE], BF16))
    lg = stack.enter_context(nc.sbuf_tensor("rt_lg", [128, NT * NE], F32))
    oh1 = stack.enter_context(nc.sbuf_tensor("rt_o1", [128, NT * NE], F32))
    oh2 = stack.enter_context(nc.sbuf_tensor("rt_o2", [128, NT * NE], F32))
    bothb = stack.enter_context(nc.sbuf_tensor("rt_bb", [128, NT * NE], BF16))
    cum = stack.enter_context(nc.sbuf_tensor("rt_cm", [128, NT * NE], F32))
    cols = stack.enter_context(nc.sbuf_tensor("rt_cl", [128, 24 * NT], F32))
    t8 = stack.enter_context(nc.sbuf_tensor("rt_t8", [128, 2 * NE], F32))
    iota8 = stack.enter_context(nc.sbuf_tensor("rt_i8", [128, NE], F32))
    im9 = stack.enter_context(nc.sbuf_tensor("rt_i9", [128, NE], F32))
    zrow = stack.enter_context(nc.sbuf_tensor("rt_zr", [128, 1], F32))
    triu = stack.enter_context(nc.sbuf_tensor("rt_tr", [128, 128], BF16))
    ones_c = stack.enter_context(nc.sbuf_tensor("rt_oc", [128, 1], BF16))
    ones_rf = stack.enter_context(nc.sbuf_tensor("rt_of", [1, 128], F32))
    carry = stack.enter_context(nc.sbuf_tensor("rt_cr", [1, NE], F32))
    cnts = stack.enter_context(nc.sbuf_tensor("rt_ct", [8, NE], F32))
    cmk = stack.enter_context(nc.sbuf_tensor("rt_ck", [8, 1], F32))
    base_bc = stack.enter_context(nc.sbuf_tensor("rt_bc", [128, NE], F32))
    icol = stack.enter_context(nc.sbuf_tensor("rt_ic", [128, 4], I32))

    CW1, CW2, POS0, POS1, DM, E0, E1, GP0, GP1, V0, V1, P0C, P1C, SD0, SD1, \
        OK0, OK1, GI0, GI1, T0, T1, T2, T3, T4 = range(24)

    def cl(tc, i):
        return cols[:, (i * NT + tc):(i * NT + tc) + 1]

    with stack:
        def init(g):
            g.iota(iota8[:, :], [[1, NE]], channel_multiplier=0,
                   allow_small_or_imprecise_dtypes=True)
            g.iota(im9[:, :], [[1, NE]], base=-9, channel_multiplier=0,
                   allow_small_or_imprecise_dtypes=True)
            g.memset(zrow[:, :], float(ZROW))
            g.iota(triu[:, :], [[1, 128]], channel_multiplier=-1,
                   allow_small_or_imprecise_dtypes=True)
            g.memset(ones_c[:, :], 1.0)
            g.memset(ones_rf[:, :], 1.0)
            g.memset(carry[:, :], 0.0)
        simple_block(nc, "rt_init", gpsimd=init)
        simple_block(nc, "rt_init2", vector=lambda v: (
            v.tensor_scalar(triu[:, :], triu[:, :], 1.0, None, ALU.is_ge),))

        load_block(nc, P, [(wrs[:, c * NE:(c + 1) * NE], wr[c * 128:(c + 1) * 128, :])
                           for c in range(NDC)]
                   + [(cmk[:, :], cmask[:, :])])
        jobs = []
        for tc in range(NT):
            loads = [(xr[:, :].rearrange("p (c t) -> p c t", c=NDC),
                      csrc(xn1, 0, NDC, tc * 128, 128))]
            jobs.append(Job(
                kchunks=[(xr[:, c * 128:(c + 1) * 128], wrs[:, c * NE:(c + 1) * NE])
                         for c in range(NDC)],
                evac=lambda e, ps, out: e.tensor_copy(out, ps),
                store=None, loads=loads, ld_gate=tc - 1, ow=NE))
        pipe_mm(nc, P, "rt_lgp", jobs, NE, out_dt=F32, nps=2, evac_engine="vector",
                persistent_out=lambda j: lg[:, j * NE:(j + 1) * NE])

        def tops(v):
            v = Ch(v, P)
            for tc in range(NT):
                L = lg[:, tc * NE:(tc + 1) * NE]
                v.reduce_max(cl(tc, T0), L, AX)
                v.tensor_scalar(t8[:, 0:NE], L, cl(tc, T0), None, ALU.is_equal)
                v.tensor_mul(t8[:, NE:2 * NE], t8[:, 0:NE], im9[:, :])
                v.tensor_scalar_add(t8[:, NE:2 * NE], t8[:, NE:2 * NE], 9.0)
                v.tensor_reduce(cl(tc, T1), t8[:, NE:2 * NE], AX, ALU.min)
                v.tensor_scalar(oh1[:, tc * NE:(tc + 1) * NE], iota8[:, :],
                                cl(tc, T1), None, ALU.is_equal)
                v.scalar_tensor_tensor(t8[:, 0:NE], oh1[:, tc * NE:(tc + 1) * NE],
                                       -1e38, L, ALU.mult, ALU.add)
                v.reduce_max(cl(tc, T2), t8[:, 0:NE], AX)
                v.tensor_scalar(t8[:, NE:2 * NE], t8[:, 0:NE],
                                cl(tc, T2), None, ALU.is_equal)
                v.tensor_mul(t8[:, 0:NE], t8[:, NE:2 * NE], im9[:, :])
                v.tensor_scalar_add(t8[:, 0:NE], t8[:, 0:NE], 9.0)
                v.tensor_reduce(cl(tc, T3), t8[:, 0:NE], AX, ALU.min)
                v.tensor_scalar(oh2[:, tc * NE:(tc + 1) * NE], iota8[:, :],
                                cl(tc, T3), None, ALU.is_equal)
                v.tensor_sub(cl(tc, DM), cl(tc, T0), cl(tc, T2))
                v.tensor_add(t8[:, 0:NE], oh1[:, tc * NE:(tc + 1) * NE],
                             oh2[:, tc * NE:(tc + 1) * NE])
                v.tensor_copy(bothb[:, tc * NE:(tc + 1) * NE], t8[:, 0:NE])
                v.tensor_mul(t8[:, 0:NE], oh1[:, tc * NE:(tc + 1) * NE], iota8[:, :])
                v.reduce_sum(cl(tc, E0), t8[:, 0:NE], AX)
                v.tensor_mul(t8[:, 0:NE], oh2[:, tc * NE:(tc + 1) * NE], iota8[:, :])
                v.reduce_sum(cl(tc, E1), t8[:, 0:NE], AX)
        simple_block(nc, "rt_tops", vector=tops)
        simple_block(nc, "rt_sig", scalar=lambda s: [
            s.activation(cl(tc, CW1), cl(tc, DM), AF.Sigmoid) for tc in range(NT)])
        simple_block(nc, "rt_cw2", vector=lambda v: [
            Ch(v, P).tensor_scalar(cl(tc, CW2), cl(tc, CW1), -1.0, 1.0,
                                   ALU.mult, ALU.add)
            for tc in range(NT)])

        for tc in range(NT):
            with (
                nc.psum_tensor(f"rt_p1{tc}", [128, NE], F32) as p1,
                nc.psum_tensor(f"rt_p2{tc}", [128, NE], F32) as p2,
                nc.psum_tensor(f"rt_p3{tc}", [1, NE], F32) as p3,
            ):
                def mm1(t, tc=tc, p1=p1, p2=p2, p3=p3):
                    t.matmul(p2[:, :], ones_rf[:, :], carry[:, :],
                             start=True, stop=True)
                    t.matmul(p1[:, :], triu[:, :], bothb[:, tc * NE:(tc + 1) * NE],
                             start=True, stop=True)
                    t.matmul(p3[:, :], ones_c[:, :], bothb[:, tc * NE:(tc + 1) * NE],
                             start=True, stop=True)
                simple_block(nc, f"rt_cu{tc}", tensor=mm1)

                def ev1(v, tc=tc, p1=p1, p2=p2, p3=p3):
                    v = Ch(v, P)
                    v.tensor_copy(t8[:, 0:NE], p2[:, :])
                    v.scalar_tensor_tensor(cum[:, tc * NE:(tc + 1) * NE],
                                           p1[:, :], 1.0, t8[:, 0:NE],
                                           ALU.mult, ALU.add)
                    v.tensor_add(carry[:, :], carry[:, :], p3[:, :])
                simple_block(nc, f"rt_cv{tc}", vector=ev1)

        def posb(v):
            v = Ch(v, P)
            for tc in range(NT):
                O1 = oh1[:, tc * NE:(tc + 1) * NE]
                O2 = oh2[:, tc * NE:(tc + 1) * NE]
                CU = cum[:, tc * NE:(tc + 1) * NE]
                v.tensor_mul(t8[:, 0:NE], O1, CU)
                v.reduce_sum(cl(tc, POS0), t8[:, 0:NE], AX)
                v.tensor_add(t8[:, 0:NE], CU, O1)
                v.tensor_mul(t8[:, 0:NE], t8[:, 0:NE], O2)
                v.reduce_sum(cl(tc, POS1), t8[:, 0:NE], AX)
        simple_block(nc, "rt_pos", vector=posb)

        store_block(nc, P, [(cnt_in[:, :], carry[:, :])])
        with nc.Block() as blk:
            def ccb(gg):
                gg.collective_compute(
                    "AllGather", ALU.bypass, replica_groups=[list(range(NCORE))],
                    ins=[cnt_in.ap().opt()], outs=[cnt_out.ap().opt()]) \
                    .then_inc(P.sem["cc"])
                P.n["cc"] += 1
                gg.wait_ge(P.sem["cc"], P.n["cc"])
            blk.gpsimd(ccb)
        load_block(nc, P, [(cnts[:, :], cnt_out[:, :])])
        with (
            nc.psum_tensor("rt_pb", [1, NE], F32) as pb,
            nc.psum_tensor("rt_pb2", [128, NE], F32) as pb2,
        ):
            simple_block(nc, "rt_bs", tensor=lambda t: (
                t.matmul(pb[:, :], cmk[:, :], cnts[:, :], start=True, stop=True),))
            simple_block(nc, "rt_bse", vector=lambda v: (
                v.tensor_copy(carry[:, :], pb[:, :]),))
            simple_block(nc, "rt_bs2", tensor=lambda t: (
                t.matmul(pb2[:, :], ones_rf[:, :], carry[:, :], start=True, stop=True),))
            simple_block(nc, "rt_bse2", vector=lambda v: (
                v.tensor_copy(base_bc[:, :], pb2[:, :]),))

        def finb(v):
            v = Ch(v, P)
            for tc in range(NT):
                O1 = oh1[:, tc * NE:(tc + 1) * NE]
                O2 = oh2[:, tc * NE:(tc + 1) * NE]
                v.tensor_mul(t8[:, 0:NE], O1, base_bc[:, :])
                v.reduce_sum(cl(tc, T0), t8[:, 0:NE], AX)
                v.tensor_add(cl(tc, GP0), cl(tc, T0), cl(tc, POS0))
                v.tensor_mul(t8[:, 0:NE], O2, base_bc[:, :])
                v.reduce_sum(cl(tc, T1), t8[:, 0:NE], AX)
                v.tensor_add(cl(tc, GP1), cl(tc, T1), cl(tc, POS1))
                v.tensor_single_scalar(cl(tc, V0), cl(tc, GP0), float(CGLB), ALU.is_lt)
                v.tensor_single_scalar(cl(tc, V1), cl(tc, GP1), float(CGLB), ALU.is_lt)
                v.tensor_mul(cl(tc, CW1), cl(tc, CW1), cl(tc, V0))
                v.tensor_mul(cl(tc, CW2), cl(tc, CW2), cl(tc, V1))
                v.tensor_scalar_min(cl(tc, P0C), cl(tc, POS0), float(ESTR - 1))
                v.tensor_scalar_min(cl(tc, P1C), cl(tc, POS1), float(ESTR - 1))
                v.scalar_tensor_tensor(cl(tc, SD0), cl(tc, E0), float(ESTR),
                                       cl(tc, P0C), ALU.mult, ALU.add)
                v.scalar_tensor_tensor(cl(tc, SD1), cl(tc, E1), float(ESTR),
                                       cl(tc, P1C), ALU.mult, ALU.add)
                v.tensor_single_scalar(cl(tc, T2), cl(tc, POS0), float(CAP), ALU.is_lt)
                v.tensor_mul(cl(tc, OK0), cl(tc, T2), cl(tc, V0))
                v.tensor_single_scalar(cl(tc, T3), cl(tc, POS1), float(CAP), ALU.is_lt)
                v.tensor_mul(cl(tc, OK1), cl(tc, T3), cl(tc, V1))
                v.tensor_scalar_add(cl(tc, T4), cl(tc, SD0), -float(ZROW))
                v.tensor_mul(cl(tc, T4), cl(tc, T4), cl(tc, OK0))
                v.tensor_scalar_add(cl(tc, GI0), cl(tc, T4), float(ZROW))
                v.tensor_scalar_add(cl(tc, T4), cl(tc, SD1), -float(ZROW))
                v.tensor_mul(cl(tc, T4), cl(tc, T4), cl(tc, OK1))
                v.tensor_scalar_add(cl(tc, GI1), cl(tc, T4), float(ZROW))
        simple_block(nc, "rt_fin", vector=finb)

        if debug:
            store_block(nc, P, [(dbg_lg[:, :], lg[:, :]), (dbg_oh[:, :], oh1[:, :]),
                                (dbg_cum[:, :], cum[:, :]), (dbg_cols[:, :], cols[:, :]),
                                (dbg_oh2[:, :], oh2[:, :])])
        b_dve = P.n["dve"]
        b_st0 = P.n["st"]
        with nc.Block() as blk:
            def conv(v):
                for tc in range(NT):
                    if tc > 0:
                        v.wait_ge(P.sem["st"], b_st0 + 6 * 16 * tc)
                    v.tensor_copy(icol[:, 0:1], cl(tc, SD0))
                    v.tensor_copy(icol[:, 1:2], cl(tc, SD1))
                    v.tensor_copy(icol[:, 2:3], cl(tc, GI0))
                    v.tensor_copy(icol[:, 3:4], cl(tc, GI1)).then_inc(P.sem["dve"], 1)
            blk.vector(conv)
            P.n["dve"] = b_dve + NT

            def stb(sy):
                for tc in range(NT):
                    sy.wait_ge(P.sem["dve"], b_dve + tc + 1)
                    if tc > 0:
                        sy.wait_ge(P.sem["st"], P.n["st"] + 6 * 16 * tc)
                    r = slice(tc * 128, (tc + 1) * 128)
                    sy.dma_start(out=sdst0[r, :], in_=icol[:, 0:1]).then_inc(P.sem["st"], 16)
                    sy.dma_start(out=sdst1[r, :], in_=icol[:, 1:2]).then_inc(P.sem["st"], 16)
                    sy.dma_start(out=gidx0[r, :], in_=icol[:, 2:3]).then_inc(P.sem["st"], 16)
                    sy.dma_start(out=gidx1[r, :], in_=icol[:, 3:4]).then_inc(P.sem["st"], 16)
                    sy.dma_start(out=cw0[r, :], in_=cl(tc, CW1)).then_inc(P.sem["st"], 16)
                    sy.dma_start(out=cw1[r, :], in_=cl(tc, CW2)).then_inc(P.sem["st"], 16)
                sy.wait_ge(P.sem["st"], P.n["st"] + 6 * 16 * NT)
            blk.sync(stb)
            P.n["st"] += 6 * 16 * NT



def emit_phase_moe(nc, P, dr, debug=False):
    xn1t = dr("xn1t", [TPC, D], BF16, kind="ExternalInput")
    sdst0 = dr("sdst0", [TPC, 1], I32, kind="ExternalInput")
    sdst1 = dr("sdst1", [TPC, 1], I32, kind="ExternalInput")
    gidx0 = dr("gidx0", [TPC, 1], I32, kind="ExternalInput")
    gidx1 = dr("gidx1", [TPC, 1], I32, kind="ExternalInput")
    cw0 = dr("cw0", [TPC, 1], F32, kind="ExternalInput")
    cw1 = dr("cw1", [TPC, 1], F32, kind="ExternalInput")
    wug = dr("wug", [NE, D, MEXP], F32, kind="ExternalInput")
    wdn = dr("wdn", [NE, IEXP, D], F32, kind="ExternalInput")
    xa2 = dr("xa2", [D, TPC], F32, kind="ExternalInput")
    out = dr("out", [D, TPC], F32, kind="ExternalOutput")

    buf = nc.dram_tensor("moe_buf", [ZROW + 1, D], BF16)
    obuf = nc.dram_tensor("moe_obuf", [ZROW + 1, D], BF16)
    if debug:
        dbuf = dr("dbuf", [ZROW + 1, D], BF16, kind="ExternalOutput")
        dobuf = dr("dobuf", [ZROW + 1, D], BF16, kind="ExternalOutput")

    NT = TPC // 128

    stack = contextlib.ExitStack()
    xrow = stack.enter_context(nc.sbuf_tensor("mo_x", [128, 2 * D], BF16))
    idxc = stack.enter_context(nc.sbuf_tensor("mo_ix", [128, 4], I32))
    cwc = stack.enter_context(nc.sbuf_tensor("mo_cw", [128, 2], F32))
    zr = stack.enter_context(nc.sbuf_tensor("mo_zr", [1, D], BF16))
    idn = stack.enter_context(nc.sbuf_tensor("mo_id", [128, 128], BF16))

    with stack:
        simple_block(nc, "mo_i0", gpsimd=lambda g: (
            g.iota(idn[:, :], [[1, 128]], channel_multiplier=-1,
                   allow_small_or_imprecise_dtypes=True),
            g.memset(zr[:, :], 0.0)))
        simple_block(nc, "mo_i1", vector=lambda v: (
            v.tensor_single_scalar(idn[:, :], idn[:, :], 0.0, ALU.is_equal),))
        store_block(nc, P, [(obuf[ZROW:ZROW + 1, :], zr[:, :])])

        with nc.Block() as blk:
            b_ld = P.n["ld"]

            def disp(g):
                n = b_ld
                for tc in range(NT):
                    sl = tc % 2
                    g.wait_ge(P.sem["ld"], n)
                    g.dma_start(out=xrow[:, sl * D:(sl + 1) * D],
                                in_=xn1t[tc * 128:(tc + 1) * 128, :]) \
                        .then_inc(P.sem["ld"], 16)
                    g.dma_start(out=idxc[:, 0:1],
                                in_=sdst0[tc * 128:(tc + 1) * 128, :]) \
                        .then_inc(P.sem["ld"], 16)
                    g.dma_start(out=idxc[:, 1:2],
                                in_=sdst1[tc * 128:(tc + 1) * 128, :]) \
                        .then_inc(P.sem["ld"], 16)
                    n += 48
                    g.wait_ge(P.sem["ld"], n)
                    g.indirect_dma_start(
                        out=buf[:, :],
                        out_offset=bass.IndirectOffsetOnAxis(idxc[:, 0:1], 0),
                        in_=xrow[:, sl * D:(sl + 1) * D], in_offset=None) \
                        .then_inc(P.sem["ld"], 16)
                    g.indirect_dma_start(
                        out=buf[:, :],
                        out_offset=bass.IndirectOffsetOnAxis(idxc[:, 1:2], 0),
                        in_=xrow[:, sl * D:(sl + 1) * D], in_offset=None) \
                        .then_inc(P.sem["ld"], 16)
                    n += 32
                g.wait_ge(P.sem["ld"], n)
            blk.gpsimd(disp)
            P.n["ld"] += NT * 5 * 16

        NSG = CAP // 512
        NMC = MEXP // 128
        NIC = IEXP // 128
        for e in range(NE):
            estk = contextlib.ExitStack()
            wugs = estk.enter_context(nc.sbuf_tensor(f"mo_wu{e}", [128, NDC * MEXP], BF16))
            wdns = estk.enter_context(nc.sbuf_tensor(f"mo_wd{e}", [128, NIC * D], BF16))
            bfm = estk.enter_context(nc.sbuf_tensor(f"mo_bf{e}", [128, NDC * 512], BF16))
            hfm = estk.enter_context(nc.sbuf_tensor(f"mo_hf{e}", [128, NIC * 512], BF16))
            hup = estk.enter_context(nc.sbuf_tensor(f"mo_hu{e}", [128, NMC * 512], BF16))
            with estk:
                load_block(nc, P, [
                    (wugs[:, dc * MEXP:(dc + 1) * MEXP],
                     wug[e, dc * 128:(dc + 1) * 128, :]) for dc in range(NDC)
                ] + [
                    (wdns[:, ic * D:(ic + 1) * D],
                     wdn[e, ic * 128:(ic + 1) * 128, :]) for ic in range(NIC)])
                for sg in range(NSG):
                    with nc.Block() as blk:
                        b_st2 = P.n["st"]

                        def tload(sy, e=e, sg=sg, b_st2=b_st2):
                            for dc in range(NDC):
                                sy.dma_start(
                                    out=bfm[:, dc * 512:(dc + 1) * 512],
                                    in_=buf[e * ESTR + sg * 512:e * ESTR + (sg + 1) * 512,
                                            dc * 128:(dc + 1) * 128],
                                    transpose=True).then_inc(P.sem["st"], 16)
                            sy.wait_ge(P.sem["st"], b_st2 + NDC * 16)
                        blk.sync(tload)
                        P.n["st"] += NDC * 16

                    jobs = []
                    for mc in range(NMC):
                        def mk_ev(mc=mc):
                            def ev(eng, ps, o):
                                if mc < NIC:
                                    return eng.activation(
                                        hup[:, mc * 512:(mc + 1) * 512], ps, AF.Silu)
                                return eng.activation(
                                    hup[:, mc * 512:(mc + 1) * 512], ps, AF.Copy)
                            return ev
                        jobs.append(Job(
                            kchunks=[(wugs[:, dc * MEXP + mc * 128:
                                          dc * MEXP + (mc + 1) * 128],
                                      bfm[:, dc * 512:(dc + 1) * 512])
                                     for dc in range(NDC)],
                            evac=mk_ev(), store=None))
                    pipe_mm(nc, P, f"mo_ug{e}_{sg}", jobs, 512, out_dt=BF16, nps=2,
                            evac_engine="scalar", persistent_out=lambda j: None)
                    simple_block(nc, f"mo_cb{e}_{sg}", vector=lambda v: [
                        v.tensor_mul(hfm[:, ic * 512:(ic + 1) * 512],
                                     hup[:, ic * 512:(ic + 1) * 512],
                                     hup[:, (NIC + ic) * 512:(NIC + ic + 1) * 512])
                        for ic in range(NIC)])

                    jobs = []
                    for sc in range(4):
                        for ob in range(4):
                            jobs.append(Job(
                                kchunks=[(hfm[:, ic * 512 + sc * 128:
                                              ic * 512 + (sc + 1) * 128],
                                          wdns[:, ic * D + ob * 512:
                                               ic * D + (ob + 1) * 512])
                                         for ic in range(NIC)],
                                evac=lambda eng, ps, o: ecopy(eng, o, ps),
                                store=obuf[e * ESTR + sg * 512 + sc * 128:
                                           e * ESTR + sg * 512 + (sc + 1) * 128,
                                           ob * 512:(ob + 1) * 512]))
                    pipe_mm(nc, P, f"mo_dn{e}_{sg}", jobs, 512, out_dt=BF16,
                            nps=2, nout=4, evac_engine="scalar")

        if debug:
            with nc.Block() as blk:
                def dmp(g):
                    n0 = P.n["ld"]
                    for rr in range(0, ZROW + 1, 2048):
                        rw = min(2048, ZROW + 1 - rr)
                        g.wait_ge(P.sem["ld"], P.n["ld"])
                        g.dma_start(out=dbuf[rr:rr + rw, :], in_=buf[rr:rr + rw, :]) \
                            .then_inc(P.sem["ld"], 16)
                        P.n["ld"] += 16
                        g.wait_ge(P.sem["ld"], P.n["ld"])
                        g.dma_start(out=dobuf[rr:rr + rw, :], in_=obuf[rr:rr + rw, :]) \
                            .then_inc(P.sem["ld"], 16)
                        P.n["ld"] += 16
                    g.wait_ge(P.sem["ld"], P.n["ld"])
                blk.gpsimd(dmp)
        cstk = contextlib.ExitStack()
        gath = cstk.enter_context(nc.sbuf_tensor("mo_g", [128, 2 * D], BF16))
        ytm = cstk.enter_context(nc.sbuf_tensor("mo_y", [128, D], BF16))
        fstg = cstk.enter_context(nc.sbuf_tensor("mo_fs", [128, D], F32))
        xat = cstk.enter_context(nc.sbuf_tensor("mo_xa", [128, 2 * D], F32))
        cstk.__enter__() if False else None
        for tc in range(NT):
            load_block(nc, P, [
                (idxc[:, 2:3], gidx0[tc * 128:(tc + 1) * 128, :]),
                (idxc[:, 3:4], gidx1[tc * 128:(tc + 1) * 128, :]),
                (cwc[:, 0:1], cw0[tc * 128:(tc + 1) * 128, :]),
                (cwc[:, 1:2], cw1[tc * 128:(tc + 1) * 128, :]),
            ] + [(xat[:, 0:NDC * 128].rearrange("p (c t) -> p c t", c=NDC),
                  csrc(xa2, 0, NDC, tc * 128, 128))])
            with nc.Block() as blk:
                b_ld2 = P.n["ld"]

                def gat(g):
                    g.indirect_dma_start(
                        out=gath[:, 0:D], out_offset=None, in_=obuf[:, :],
                        in_offset=bass.IndirectOffsetOnAxis(idxc[:, 2:3], 0)) \
                        .then_inc(P.sem["ld"], 16)
                    g.wait_ge(P.sem["ld"], b_ld2 + 16)
                    g.indirect_dma_start(
                        out=gath[:, D:2 * D], out_offset=None, in_=obuf[:, :],
                        in_offset=bass.IndirectOffsetOnAxis(idxc[:, 3:4], 0)) \
                        .then_inc(P.sem["ld"], 16)
                    g.wait_ge(P.sem["ld"], b_ld2 + 32)
                blk.gpsimd(gat)
                P.n["ld"] = b_ld2 + 32
            def _cmb(v, tc=tc):
                v = Ch(v, P)
                v.tensor_scalar_mul(fstg[:, 0:D], gath[:, 0:D], cwc[:, 0:1])
                v.scalar_tensor_tensor(fstg[:, 0:D], gath[:, D:2 * D], cwc[:, 1:2],
                                       fstg[:, 0:D], ALU.mult, ALU.add)
                v.tensor_copy(ytm[:, :], fstg[:, 0:D])
            simple_block(nc, f"mo_y{tc}", vector=_cmb)
            with (
                nc.psum_tensor(f"mo_yp{tc}a", [128, 1024], BF16) as pta,
                nc.psum_tensor(f"mo_yp{tc}b", [128, 1024], BF16) as ptb,
                nc.Block() as blk,
            ):
                pss = [pta, ptb]
                b_mm, b_dve, b_st3 = P.n["mm"], P.n["dve"], P.n["st"]

                def tb2(t):
                    for dc in range(NDC):
                        if dc >= 2:
                            t.wait_ge(P.sem["dve"], b_dve + dc - 1)
                        t.transpose(pss[dc % 2][:, 0:128],
                                    ytm[:, dc * 128:(dc + 1) * 128],
                                    idn[:, :]).then_inc(P.sem["mm"], 1)
                blk.tensor(tb2)
                P.n["mm"] = b_mm + NDC

                def te2(v_):
                    for dc in range(NDC):
                        v_.wait_ge(P.sem["mm"], b_mm + dc + 1)
                        v_.tensor_tensor(fstg[:, dc * 128:(dc + 1) * 128],
                                         pss[dc % 2][:, 0:128],
                                         xat[:, dc * 128:(dc + 1) * 128],
                                         ALU.add).then_inc(P.sem["dve"], 1)
                blk.vector(te2)
                P.n["dve"] = b_dve + NDC

                def st2(sy):
                    for dc in range(NDC):
                        sy.wait_ge(P.sem["dve"], b_dve + dc + 1)
                        if dc > 0:
                            sy.wait_ge(P.sem["st"], b_st3 + 16 * dc)
                        sy.dma_start(
                            out=out[dc * 128:(dc + 1) * 128, tc * 128:(tc + 1) * 128],
                            in_=fstg[:, dc * 128:(dc + 1) * 128]) \
                            .then_inc(P.sem["st"], 16)
                    sy.wait_ge(P.sem["st"], b_st3 + 16 * NDC)
                blk.sync(st2)
                P.n["st"] = b_st3 + 16 * NDC
        cstk.close()




def _passthrough_dr(nc):
    def dr(name, shape, dt, kind=None):
        if kind is None:
            return nc.dram_tensor(name, shape, dt)
        return nc.dram_tensor(name, shape, dt, kind=kind)
    return dr


def _mk_build(emit):
    @prog
    def b(nc, P, **kw):
        emit(nc, P, _passthrough_dr(nc), **kw)
    return b


build_wo_res = _mk_build(emit_phase_wo_res)
build_ln0 = _mk_build(emit_phase_ln0)
build_qkvg = _mk_build(emit_phase_qkvg)
build_scan = _mk_build(emit_phase_scan)
build_wp_res = _mk_build(emit_phase_wp_res)
build_ffn0 = _mk_build(emit_phase_ffn0)
build_down0 = _mk_build(emit_phase_down0)
build_ln1 = _mk_build(emit_phase_ln1)
build_qkv1 = _mk_build(emit_phase_qkv1)
build_attn1 = _mk_build(emit_phase_attn1)
build_shared = _mk_build(emit_phase_shared)
build_shdown = _mk_build(emit_phase_shdown)
build_route = _mk_build(emit_phase_route)
build_moe = _mk_build(emit_phase_moe)



def emit_phase_halo(nc, P, dr):
    k1 = dr("k1", [TPC, NKV * HS], BF16, kind="ExternalInput")
    v1 = dr("v1", [TPC, NKV * HS], BF16, kind="ExternalInput")
    kidx = dr("kidx", [1152, 1], I32, kind="ExternalInput")
    vidx = dr("vidx", [1152, 1], I32, kind="ExternalInput")
    khalo = dr("khalo", [1152, NKV * HS], BF16, kind="ExternalOutput")
    vhalo = dr("vhalo", [1152, NKV * HS], BF16, kind="ExternalOutput")
    contrib = nc.dram_tensor("hx_contrib", [2064, 512], BF16)
    agh = nc.dram_tensor("hx_agh", [GRP * 2064, 512], BF16)

    stack = contextlib.ExitStack()
    zt = stack.enter_context(nc.sbuf_tensor("hx_z", [128, 512], BF16))
    idxs = stack.enter_context(nc.sbuf_tensor("hx_i", [128, 2], I32))
    row = stack.enter_context(nc.sbuf_tensor("hx_r", [128, 512], BF16))
    with stack:
        with nc.Block() as blk:
            def cb(g):
                g.memset(zt[:, :], 0.0)
                for dst0, src, rws in [(0, k1, (1024, 2048)), (1024, k1, (0, 4)),
                                       (1028, v1, (0, 4)), (1032, v1, (1024, 2048))]:
                    g.wait_ge(P.sem["ld"], P.n["ld"])
                    g.dma_start(out=contrib[dst0:dst0 + (rws[1] - rws[0]), :],
                                in_=src[rws[0]:rws[1], :]).then_inc(P.sem["ld"], 16)
                    P.n["ld"] += 16
                g.wait_ge(P.sem["ld"], P.n["ld"])
                g.dma_start(out=contrib[2056:2064, :], in_=zt[0:8, :]) \
                    .then_inc(P.sem["ld"], 16)
                P.n["ld"] += 16
                g.wait_ge(P.sem["ld"], P.n["ld"])
                g.collective_compute(
                    "AllGather", ALU.bypass,
                    replica_groups=[[0, 1, 2, 3], [4, 5, 6, 7]],
                    ins=[contrib.ap().opt()], outs=[agh.ap().opt()]) \
                    .then_inc(P.sem["cc"])
                P.n["cc"] += 1
                g.wait_ge(P.sem["cc"], P.n["cc"])
                for j in range(9):
                    for which, (idxd, outd, col) in enumerate(
                            [(kidx, khalo, 0), (vidx, vhalo, 1)]):
                        g.wait_ge(P.sem["ld"], P.n["ld"])
                        g.dma_start(out=idxs[:, col:col + 1],
                                    in_=idxd[j * 128:(j + 1) * 128, :]) \
                            .then_inc(P.sem["ld"], 16)
                        P.n["ld"] += 16
                        g.wait_ge(P.sem["ld"], P.n["ld"])
                        g.indirect_dma_start(
                            out=row[:, :], out_offset=None, in_=agh[:, :],
                            in_offset=bass.IndirectOffsetOnAxis(idxs[:, col:col + 1], 0)) \
                            .then_inc(P.sem["ld"], 16)
                        P.n["ld"] += 16
                        g.wait_ge(P.sem["ld"], P.n["ld"])
                        g.dma_start(out=outd[j * 128:(j + 1) * 128, :], in_=row[:, :]) \
                            .then_inc(P.sem["ld"], 16)
                        P.n["ld"] += 16
                g.wait_ge(P.sem["ld"], P.n["ld"])
            blk.gpsimd(cb)


_EI, _EO = "ExternalInput", "ExternalOutput"
FUSED_ALIASES = {
    "ln0": {"x": ("x", _EI), "xn": ("xn", None), "xnr": ("xnr", None)},
    "qkvg": {"xnr": ("xnr", None), "wq": ("l0wq", _EI), "wk": ("l0wk", _EI),
             "wv": ("l0wv", _EI), "wg": ("l0wg", _EI), "q": ("l0q", None),
             "k": ("l0k", None), "g": ("l0g", None), "v": ("l0v", None)},
    "scan": {"q": ("l0q", None), "k": ("l0k", None), "v": ("l0v", None),
             "g": ("l0g", None), "rankm": ("rankm", _EI), "ident": ("ident", _EI),
             "trilm": ("trilm", _EI), "og": ("og", None)},
    "wp_res": {"og": ("og", None), "wp": ("l0wp", _EI), "x": ("x", _EI),
               "xmid": ("xmid", None)},
    "ffn0": {"xn": ("xn", None), "wgate": ("l0wgate", _EI), "wup": ("l0wup", _EI),
             "gs": ("l0gs", _EI), "us": ("l0us", _EI), "h": ("h0", None)},
    "down0": {"h": ("h0", None), "wdown": ("l0wdown", _EI), "ds": ("l0ds", _EI),
              "xmid": ("xmid", None), "x1": ("x1", None)},
    "ln1": {"x1": ("x1", None), "xn1": ("xn1", None), "xn1t": ("xn1t", None)},
    "qkv1": {"xn1": ("xn1", None), "wq": ("l1wq", _EI), "wk": ("l1wk", _EI),
             "wv": ("l1wv", _EI), "q1": ("q1", None), "k1": ("k1", None),
             "v1": ("v1", None)},
    "halo": {"k1": ("k1", None), "v1": ("v1", None), "kidx": ("kidx", _EI),
             "vidx": ("vidx", _EI), "khalo": ("khalo", None), "vhalo": ("vhalo", None)},
    "attn1": {"q1": ("q1", None), "k1": ("k1", None), "v1": ("v1", None),
              "khalo": ("khalo", None), "vhalo": ("vhalo", None),
              "bandml": ("bandml", _EI), "bandmr": ("bandmr", _EI),
              "sinkm": ("sinkm", _EI), "ninv": ("ninv", _EI), "y": ("y1", None)},
    "wo_res": {"y": ("y1", None), "wo": ("l1wo", _EI), "x1": ("x1", None),
               "xa": ("xa", None)},
    "shared": {"xn1": ("xn1", None), "wug": ("l1wsug", _EI), "h": ("hsh", None)},
    "shdown": {"h": ("hsh", None), "wdown": ("l1wsdn", _EI), "xa": ("xa", None),
               "xa2": ("xa2", None)},
    "route": {"xn1": ("xn1", None), "wr": ("l1wr", _EI), "cmask": ("cmask", _EI),
              "sdst0": ("sdst0", None), "sdst1": ("sdst1", None),
              "gidx0": ("gidx0", None), "gidx1": ("gidx1", None),
              "cw0": ("cw0", None), "cw1": ("cw1", None)},
    "moe": {"xn1t": ("xn1t", None), "sdst0": ("sdst0", None), "sdst1": ("sdst1", None),
            "gidx0": ("gidx0", None), "gidx1": ("gidx1", None), "cw0": ("cw0", None),
            "cw1": ("cw1", None), "wug": ("l1wug", _EI), "wdn": ("l1wdn", _EI),
            "xa2": ("xa2", None), "out": ("out", _EO)},
}

FUSED_ORDER = ["ln0", "qkvg", "scan", "wp_res", "ffn0", "down0", "ln1", "qkv1",
               "halo", "attn1", "wo_res", "shared", "shdown", "route", "moe"]


@prog
def build_fused(nc, P):
    cache = {}

    def mk_dr(amap):
        def dr(name, shape, dt, kind=None):
            cname, ckind = amap[name]
            if cname in cache:
                return cache[cname]
            if ckind is None:
                t = nc.dram_tensor(cname, list(shape), dt)
            else:
                t = nc.dram_tensor(cname, list(shape), dt, kind=ckind)
            cache[cname] = t
            return t
        return dr

    for ph in FUSED_ORDER:
        emit = globals()[f"emit_phase_{ph}"]
        emit(nc, P, mk_dr(FUSED_ALIASES[ph]))


def make_halo_idx():
    kidxs, vidxs = [], []
    for c in range(NCORE):
        r = c % GRP
        Z = 2056
        ki = np.full(1152, Z, np.int32)
        vi = np.full(1152, Z, np.int32)
        if r > 0:
            ki[0:1024] = (r - 1) * 2064 + np.arange(1024)
            vi[0:1024] = (r - 1) * 2064 + 1032 + np.arange(1024)
        ki[1024:1028] = 1024 + np.arange(4)
        vi[1024:1028] = 1028 + np.arange(4)
        kidxs.append(np.ascontiguousarray(ki.reshape(-1, 1)))
        vidxs.append(np.ascontiguousarray(vi.reshape(-1, 1)))
    return kidxs, vidxs


_FUSED_NC = None


def kernel_fused(L0_attn_x, L0_attn_Wq, L0_attn_Wk, L0_attn_Wv, L0_attn_Wg,
                 L0_attn_Wp, L0_ffn_gate_w, L0_ffn_up_w, L0_ffn_down_w,
                 L0_ffn_gate_s, L0_ffn_up_s, L0_ffn_down_s, L1_attn_Wq,
                 L1_attn_Wk, L1_attn_Wv, L1_attn_Wo, L1_ffn_router,
                 L1_ffn_w_up_gate, L1_ffn_w_down, L1_ffn_shared_w_up_gate,
                 L1_ffn_shared_w_down):
    def T(w):
        return np.ascontiguousarray(np.asarray(w).T)

    def C(w):
        return np.ascontiguousarray(np.asarray(w))

    global _FUSED_NC
    xs = shard_x(np.asarray(L0_attn_x))
    rankm, ident, trilm = host_scan_inputs()
    bandml, bandmr, sinkms, ninvs = make_attn_inputs()
    kidxs, vidxs = make_halo_idx()
    shared_w = {
        "l0wq": T(L0_attn_Wq), "l0wk": T(L0_attn_Wk), "l0wv": T(L0_attn_Wv),
        "l0wg": T(L0_attn_Wg), "l0wp": T(L0_attn_Wp),
        "l0wgate": T(L0_ffn_gate_w), "l0wup": T(L0_ffn_up_w),
        "l0wdown": T(L0_ffn_down_w),
        "l0gs": C(np.asarray(L0_ffn_gate_s).reshape(-1, 1)),
        "l0us": C(np.asarray(L0_ffn_up_s).reshape(-1, 1)),
        "l0ds": C(np.asarray(L0_ffn_down_s).reshape(-1, 1)),
        "l1wq": T(L1_attn_Wq), "l1wk": T(L1_attn_Wk), "l1wv": T(L1_attn_Wv),
        "l1wo": T(L1_attn_Wo), "l1wr": C(L1_ffn_router),
        "l1wug": C(L1_ffn_w_up_gate), "l1wdn": C(L1_ffn_w_down),
        "l1wsug": C(L1_ffn_shared_w_up_gate), "l1wsdn": C(L1_ffn_shared_w_down),
        "ident": ident, "trilm": trilm, "bandml": bandml, "bandmr": bandmr,
    }
    in_maps = []
    for c in range(NCORE):
        cm = np.zeros((NCORE, 1), np.float32)
        cm[:c, 0] = 1.0
        in_maps.append({
            "x": xs[c], "rankm": rankm[c], "sinkm": sinkms[c], "ninv": ninvs[c],
            "cmask": cm, "kidx": kidxs[c], "vidx": vidxs[c], **shared_w})
    if _FUSED_NC is None:
        _FUSED_NC = build_fused()
    res = _run(_FUSED_NC, in_maps)
    out = np.empty((2, 8192, D), np.float32)
    for c in range(NCORE):
        b, qr = c // GRP, c % GRP
        out[b, qr * TPC:(qr + 1) * TPC, :] = res[c]["out"].T
    return out



def _run(nc, in_maps):
    return run_bass_kernel_spmd(nc, in_maps, core_ids=list(range(NCORE))).results


def shard_x(x):
    out = []
    for c in range(NCORE):
        b, qr = c // GRP, c % GRP
        out.append(np.ascontiguousarray(x[b, qr * TPC:(qr + 1) * TPC, :].T))
    return out



def make_attn_inputs():
    i = np.arange(QT)
    bandml = np.zeros((4, 128, QT), np.float32)
    bandmr = np.zeros((4, 128, QT), np.float32)
    for kc in range(4):
        for rr in range(128):
            bandml[kc, rr, :] = (kc * 128 + rr >= i + 1)
    for j in range(4):
        for rr in range(128):
            bandmr[j, rr, :] = ((8 + j) * 128 + rr <= 1024 + i)
    bandml = bandml.astype(BF)
    bandmr = bandmr.astype(BF)
    sinkms, ninvs = [], []
    for c in range(NCORE):
        first = (c % GRP == 0)
        sm = np.ones((NQT, 4, QT), np.float32)
        nv = np.zeros((NQT, QT), np.float32)
        if first:
            for qt in range(NQT):
                q0 = qt * QT
                qv = q0 + i
                for js in range(4):
                    sm[qt, js, :] = np.where(
                        qv < 1024, 0.0,
                        np.where(qv <= 1027, (qv - 1023 > js).astype(np.float32), 1.0))
                lim = 1023 - q0
                nv[qt, :] = np.maximum(0, np.minimum(1024 + i, lim) - i)
        sinkms.append(sm.astype(BF))
        ninvs.append(nv)
    return bandml, bandmr, sinkms, ninvs


def make_halo(k_all, v_all):
    kh, vh = [], []
    for c in range(NCORE):
        b, qr = c // GRP, c % GRP
        k = np.zeros((1152, NKV * HS), BF)
        v = np.zeros((1152, NKV * HS), BF)
        if qr > 0:
            k[0:1024] = k_all[c - 1][TPC - 1024:]
            v[0:1024] = v_all[c - 1][TPC - 1024:]
        k[1024:1028] = k_all[b * GRP][0:4]
        v[1024:1028] = v_all[b * GRP][0:4]
        kh.append(k)
        vh.append(v)
    return kh, vh


def host_scan_inputs():
    rankm = []
    for c in range(NCORE):
        row = np.zeros((128, GRP), np.float32)
        row[:, :c % GRP] = 1.0
        rankm.append(row)
    ident = np.eye(128, dtype=np.float32).astype(BF)
    trilm = np.zeros((4, 128, BS), np.float32)
    for kc in range(4):
        for rr in range(128):
            trilm[kc, rr, kc * 128 + rr:] = 1.0
    return rankm, ident, trilm.astype(BF)


def kernel_unfused(L0_attn_x, L0_attn_Wq, L0_attn_Wk, L0_attn_Wv, L0_attn_Wg, L0_attn_Wp,
           L0_ffn_gate_w, L0_ffn_up_w, L0_ffn_down_w, L0_ffn_gate_s, L0_ffn_up_s,
           L0_ffn_down_s, L1_attn_Wq, L1_attn_Wk, L1_attn_Wv, L1_attn_Wo,
           L1_ffn_router, L1_ffn_w_up_gate, L1_ffn_w_down,
           L1_ffn_shared_w_up_gate, L1_ffn_shared_w_down):
    def T(w):
        return np.ascontiguousarray(np.asarray(w).T)

    def C(w):
        return np.ascontiguousarray(np.asarray(w))

    R = range(NCORE)
    xs = shard_x(np.asarray(L0_attn_x))

    r1 = _run(build_ln0(), [{"x": xs[c]} for c in R])
    wq, wk, wv, wg = T(L0_attn_Wq), T(L0_attn_Wk), T(L0_attn_Wv), T(L0_attn_Wg)
    r2 = _run(build_qkvg(), [
        {"xnr": r1[c]["xnr"], "wq": wq, "wk": wk, "wv": wv, "wg": wg} for c in R])
    rankm, ident, trilm = host_scan_inputs()
    r3 = _run(build_scan(), [
        {"q": r2[c]["q"], "k": r2[c]["k"], "v": r2[c]["v"], "g": r2[c]["g"],
         "rankm": rankm[c], "ident": ident, "trilm": trilm} for c in R])
    wp = T(L0_attn_Wp)
    r4 = _run(build_wp_res(), [
        {"og": r3[c]["og"], "wp": wp, "x": xs[c]} for c in R])
    wgate, wup = T(L0_ffn_gate_w), T(L0_ffn_up_w)
    gs = C(np.asarray(L0_ffn_gate_s).reshape(-1, 1))
    us = C(np.asarray(L0_ffn_up_s).reshape(-1, 1))
    r5 = _run(build_ffn0(), [
        {"xn": r1[c]["xn"], "wgate": wgate, "wup": wup, "gs": gs, "us": us}
        for c in R])
    wdown0 = T(L0_ffn_down_w)
    ds = C(np.asarray(L0_ffn_down_s).reshape(-1, 1))
    r6 = _run(build_down0(), [
        {"h": r5[c]["h"], "wdown": wdown0, "ds": ds, "xmid": r4[c]["xmid"]}
        for c in R])
    r7 = _run(build_ln1(), [{"x1": r6[c]["x1"]} for c in R])
    wq1, wk1, wv1 = T(L1_attn_Wq), T(L1_attn_Wk), T(L1_attn_Wv)
    r8 = _run(build_qkv1(), [
        {"xn1": r7[c]["xn1"], "wq": wq1, "wk": wk1, "wv": wv1} for c in R])
    kh, vh = make_halo([r8[c]["k1"] for c in R], [r8[c]["v1"] for c in R])
    bandml, bandmr, sinkms, ninvs = make_attn_inputs()
    r9 = _run(build_attn1(), [
        {"q1": r8[c]["q1"], "k1": r8[c]["k1"], "v1": r8[c]["v1"],
         "khalo": kh[c], "vhalo": vh[c], "bandml": bandml, "bandmr": bandmr,
         "sinkm": sinkms[c], "ninv": ninvs[c]} for c in R])
    wo = T(L1_attn_Wo)
    r10 = _run(build_wo_res(), [
        {"y": r9[c]["y"], "wo": wo, "x1": r6[c]["x1"]} for c in R])
    wsug = C(L1_ffn_shared_w_up_gate)
    r11 = _run(build_shared(), [{"xn1": r7[c]["xn1"], "wug": wsug} for c in R])
    wsdn = C(L1_ffn_shared_w_down)
    r12 = _run(build_shdown(), [
        {"h": r11[c]["h"], "wdown": wsdn, "xa": r10[c]["xa"]} for c in R])
    wr = C(L1_ffn_router)
    cmasks = []
    for c in R:
        cm = np.zeros((NCORE, 1), np.float32)
        cm[:c, 0] = 1.0
        cmasks.append(cm)
    r13 = _run(build_route(), [
        {"xn1": r7[c]["xn1"], "wr": wr, "cmask": cmasks[c]} for c in R])
    wug_a = C(L1_ffn_w_up_gate)
    wdn_a = C(L1_ffn_w_down)
    r14 = _run(build_moe(), [
        {"xn1t": r7[c]["xn1t"], "sdst0": r13[c]["sdst0"], "sdst1": r13[c]["sdst1"],
         "gidx0": r13[c]["gidx0"], "gidx1": r13[c]["gidx1"],
         "cw0": r13[c]["cw0"], "cw1": r13[c]["cw1"],
         "wug": wug_a, "wdn": wdn_a, "xa2": r12[c]["xa2"]} for c in R])

    out = np.empty((2, 2 * GRP * 0 + 8192, D), np.float32)
    for c in R:
        b, qr = c // GRP, c % GRP
        out[b, qr * TPC:(qr + 1) * TPC, :] = r14[c]["out"].T
    return out


kernel = kernel_fused


# version 07
# speedup vs baseline: 1.0263x; 1.0013x over previous
import contextlib
import numpy as np
import ml_dtypes
import concourse.bass as bass
import concourse.mybir as mybir
from concourse.bass_utils import run_bass_kernel_spmd

F32 = mybir.dt.float32
BF16 = mybir.dt.bfloat16
I32 = mybir.dt.int32
AF = mybir.ActivationFunctionType
ALU = mybir.AluOpType
AX = mybir.AxisListType.X
BF = ml_dtypes.bfloat16

D = 2048
TPC = 2048
NTB = 4
TBL = 512
NDC = 16
NCORE = 8
GRP = 4

INTER0 = 5632
NIC0 = 44

BS = 512
NCH = TPC // BS

NH, NKV, HS, WIN, SINK = 16, 4, 128, 1024, 4
QT = 512
NQT = TPC // QT
KBUF = WIN + QT
NKC = KBUF // 128
SCALE_ATT = 1.0 / float(np.sqrt(np.float32(HS)))
ESHIFT = -12.0
EM12 = float(np.exp(np.float64(ESHIFT)))

NE = 8
CAP = 2048
ESTR = 2056
ZROW = NE * ESTR
CGLB = 5120
MEXP = 2816
IEXP = 1408
JSH = 5632

LN_EPS = 1e-5
RMS_EPS = float(np.finfo(np.float32).eps)


def _bfh(x):
    return np.ascontiguousarray(x).astype(BF)


def ecopy(e, out, in_):
    if hasattr(e, "tensor_copy"):
        return e.tensor_copy(out, in_)
    return e.copy(out, in_)



class Pool:
    NAMES = ("ld", "st", "mm", "dve", "act", "cc", "gp", "ch")

    def __init__(self, nc, stack):
        self.nc = nc
        self.sem = {}
        self.n = {}
        for nm in self.NAMES:
            self.sem[nm] = stack.enter_context(nc.semaphore(f"P_{nm}"))
            self.n[nm] = 0


def prog(build_fn):
    def wrapper(*a, **kw):
        nc = bass.Bass("TRN2", target_bir_lowering=False)
        stack = contextlib.ExitStack()
        P = Pool(nc, stack)
        with stack:
            build_fn(nc, P, *a, **kw)
        return nc
    return wrapper


ENG2CTR = {"vector": "dve", "scalar": "act"}


class Ch:

    def __init__(self, eng, P, nm="ch"):
        self._e = eng
        self._P = P
        self._nm = nm

    def __getattr__(self, a):
        f = getattr(self._e, a)

        def w(*args, **kw):
            inst = f(*args, **kw)
            self._P.n[self._nm] += 1
            inst.then_inc(self._P.sem[self._nm], 1)
            self._e.wait_ge(self._P.sem[self._nm], self._P.n[self._nm])
            return inst
        return w



def simple_block(nc, name, **bodies):
    with nc.Block() as blk:
        for eng, fn in bodies.items():
            getattr(blk, eng)(fn)


def load_block(nc, P, pairs, engine="gpsimd"):
    sem = P.sem["ld"]
    with nc.Block() as blk:
        def body(g):
            for (sb, dr) in pairs:
                g.dma_start(out=sb, in_=dr).then_inc(sem, 16)
                P.n["ld"] += 16
            g.wait_ge(sem, P.n["ld"])
        getattr(blk, engine)(body)


def store_block(nc, P, pairs, engine="sync"):
    sem = P.sem["st"]
    with nc.Block() as blk:
        def body(g):
            for (dr, sb) in pairs:
                g.dma_start(out=dr, in_=sb).then_inc(sem, 16)
                P.n["st"] += 16
            g.wait_ge(sem, P.n["st"])
        getattr(blk, engine)(body)


class Job:
    def __init__(self, kchunks, evac, store=None, loads=None, pw=128, ld_gate=None,
                 ow=None):
        self.kchunks = kchunks
        self.evac = evac
        self.store = store
        self.loads = loads or []
        self.pw = pw
        self.ld_gate = ld_gate
        self.ow = ow


def pipe_mm(nc, P, name, jobs, out_w, out_dt=BF16, nps=3, nout=4,
            evac_engine="scalar", ps_w=None, ld_ahead=4, persistent_out=None):
    ps_w = ps_w or out_w
    n = len(jobs)
    cumld = np.cumsum([0] + [len(j.loads) for j in jobs])
    b_ld, b_mm, b_st = P.n["ld"], P.n["mm"], P.n["st"]
    ec = ENG2CTR[evac_engine]
    b_ev = P.n[ec]
    nstores = [0] * (n + 1)
    for j, job in enumerate(jobs):
        nstores[j + 1] = nstores[j] + (1 if job.store is not None else 0)

    stack = contextlib.ExitStack()
    stage = stack.enter_context(
        nc.sbuf_tensor(f"{name}_stage", [128, max(1, nout) * out_w], out_dt))
    pss = [stack.enter_context(nc.psum_tensor(f"{name}_ps{i}", [128, ps_w], F32))
           for i in range(nps)]
    with stack, nc.Block() as blk:

        @blk.gpsimd
        def _(g):
            for j, job in enumerate(jobs):
                if not job.loads:
                    continue
                gate = job.ld_gate if job.ld_gate is not None else j - ld_ahead
                if gate > 0:
                    g.wait_ge(P.sem[ec], b_ev + gate)
                if cumld[j] > 0:
                    g.wait_ge(P.sem["ld"], b_ld + 16 * int(cumld[j]))
                for (sb, dr) in job.loads:
                    g.dma_start(out=sb, in_=dr).then_inc(P.sem["ld"], 16)
        P.n["ld"] = b_ld + 16 * int(cumld[n])

        @blk.tensor
        def _(t):
            for j, job in enumerate(jobs):
                if cumld[j + 1] > 0:
                    t.wait_ge(P.sem["ld"], b_ld + 16 * int(cumld[j + 1]))
                if j >= nps:
                    t.wait_ge(P.sem[ec], b_ev + j - nps + 1)
                ps = pss[j % nps]
                nk = len(job.kchunks)
                ow = job.ow or out_w
                last = None
                for kk, (lh, rh) in enumerate(job.kchunks):
                    last = t.matmul(ps[0:job.pw, 0:ow], lh, rh,
                                    start=(kk == 0), stop=(kk == nk - 1))
                last.then_inc(P.sem["mm"], 1)
        P.n["mm"] = b_mm + n

        def evac_body(e):
            for j, job in enumerate(jobs):
                e.wait_ge(P.sem["mm"], b_mm + j + 1)
                ow = job.ow or out_w
                if persistent_out is not None:
                    out_ap = persistent_out(j)
                else:
                    if j >= nout:
                        e.wait_ge(P.sem["st"], b_st + 16 * nstores[j - nout + 1])
                    out_ap = stage[0:job.pw,
                                   (j % nout) * out_w:(j % nout) * out_w + ow]
                job.evac(e, pss[j % nps][0:job.pw, 0:ow], out_ap) \
                    .then_inc(P.sem[ec], 1)

        if evac_engine == "scalar":
            @blk.scalar
            def _(s):
                evac_body(s)
        else:
            @blk.vector
            def _(v):
                evac_body(v)
        P.n[ec] = b_ev + n

        @blk.sync
        def _(sy):
            for j, job in enumerate(jobs):
                if job.store is None:
                    continue
                sy.wait_ge(P.sem[ec], b_ev + j + 1)
                if nstores[j] > 0:
                    sy.wait_ge(P.sem["st"], b_st + 16 * nstores[j])
                sy.dma_start(
                    out=job.store,
                    in_=stage[0:job.pw,
                              (j % nout) * out_w:(j % nout) * out_w + (job.ow or out_w)],
                ).then_inc(P.sem["st"], 16)
            sy.wait_ge(P.sem["st"], b_st + 16 * nstores[n])
            sy.wait_ge(P.sem[ec], b_ev + n)
        P.n["st"] = b_st + 16 * nstores[n]



def csrc(dram, c0, nchunk, f0, w):
    return dram.ap().rearrange("(c p) f -> p c f", p=128)[:, c0:c0 + nchunk, f0:f0 + w]


def wstrip_ap(w_dram, n_in_c, oc0, width):
    return w_dram.ap().rearrange("(c p) o -> p c o", p=128)[:, :, oc0:oc0 + width]



def emit_ln(nc, P, x_dram, outs, ln_pref="ln"):
    stack = contextlib.ExitStack()
    xt = stack.enter_context(nc.sbuf_tensor(ln_pref + "_xt", [128, NDC * TBL], BF16))
    sq = stack.enter_context(nc.sbuf_tensor(ln_pref + "_sq", [128, NDC * TBL], BF16))
    ones_col = stack.enter_context(nc.sbuf_tensor(ln_pref + "_oc", [128, 1], BF16))
    ones_row = stack.enter_context(nc.sbuf_tensor(ln_pref + "_or", [1, 128], F32))
    rows = stack.enter_context(nc.sbuf_tensor(ln_pref + "_rows", [1, 10 * TBL], F32))
    bca = stack.enter_context(nc.sbuf_tensor(ln_pref + "_bca", [128, 4 * TBL], F32))
    outt = stack.enter_context(
        nc.sbuf_tensor(ln_pref + "_outt", [128, len(outs) * NDC * TBL], BF16))
    tmp = stack.enter_context(nc.sbuf_tensor(ln_pref + "_tmp", [128, TBL], F32))

    def r(i):
        return rows[:, i * TBL:(i + 1) * TBL]
    SUM, SSQ, MEAN, VAR, RSTD, R2, A2, B1, B2, TMPR = range(10)

    with stack:
        def init(g):
            g.memset(ones_col[:, :], 1.0)
            g.memset(ones_row[:, :], 1.0)
        simple_block(nc, ln_pref + "_init", gpsimd=init)

        for b in range(NTB):
            load_block(nc, P, [
                (xt[:, :].rearrange("p (c t) -> p c t", c=NDC),
                 csrc(x_dram, 0, NDC, b * TBL, TBL))])

            def sqb(s):
                for c in range(NDC):
                    s.activation(sq[:, c * TBL:(c + 1) * TBL],
                                 xt[:, c * TBL:(c + 1) * TBL], AF.Square)
            simple_block(nc, f"{ln_pref}_sq{b}", scalar=sqb)

            with nc.psum_tensor(f"{ln_pref}_ps{b}", [1, 2 * TBL], F32) as ps:
                def mmb(t):
                    for c in range(NDC):
                        t.matmul(ps[:, 0:TBL], ones_col[:, :],
                                 xt[:, c * TBL:(c + 1) * TBL],
                                 start=(c == 0), stop=(c == NDC - 1))
                        t.matmul(ps[:, TBL:2 * TBL], ones_col[:, :],
                                 sq[:, c * TBL:(c + 1) * TBL],
                                 start=(c == 0), stop=(c == NDC - 1))
                simple_block(nc, f"{ln_pref}_mm{b}", tensor=mmb)

                def r1(v):
                    v = Ch(v, P)
                    v.tensor_scalar_mul(r(MEAN), ps[:, 0:TBL], 1.0 / D)
                    v.tensor_scalar_mul(r(SSQ), ps[:, TBL:2 * TBL], 1.0 / D)
                    v.tensor_mul(r(VAR), r(MEAN), r(MEAN))
                    v.tensor_sub(r(VAR), r(SSQ), r(VAR))
                    v.tensor_scalar_add(r(TMPR), r(VAR), LN_EPS)
                    v.reciprocal(r(TMPR), r(TMPR))
                simple_block(nc, f"{ln_pref}_r1{b}", vector=r1)

            simple_block(nc, f"{ln_pref}_r2{b}", scalar=lambda s: (
                s.activation(r(RSTD), r(TMPR), AF.Sqrt),))

            def r3(v):
                v = Ch(v, P)
                v.tensor_mul(r(TMPR), r(RSTD), r(RSTD))
                v.tensor_mul(r(TMPR), r(TMPR), r(VAR))
                v.tensor_scalar_add(r(TMPR), r(TMPR), RMS_EPS)
                v.reciprocal(r(TMPR), r(TMPR))
            simple_block(nc, f"{ln_pref}_r3{b}", vector=r3)
            simple_block(nc, f"{ln_pref}_r4{b}", scalar=lambda s: (
                s.activation(r(R2), r(TMPR), AF.Sqrt),))

            def r5(v):
                v = Ch(v, P)
                v.tensor_mul(r(B1), r(MEAN), r(RSTD))
                v.tensor_scalar_mul(r(B1), r(B1), -1.0)
                v.tensor_mul(r(A2), r(RSTD), r(R2))
                v.tensor_mul(r(B2), r(B1), r(R2))
            simple_block(nc, f"{ln_pref}_r5{b}", vector=r5)

            with nc.psum_tensor(f"{ln_pref}_pbc{b}", [128, 4 * TBL], F32) as pbc:
                def bcb(t):
                    for i, idx in enumerate([RSTD, B1, A2, B2]):
                        t.matmul(pbc[:, i * TBL:(i + 1) * TBL], ones_row[:, :],
                                 r(idx), start=True, stop=True)
                simple_block(nc, f"{ln_pref}_bc{b}", tensor=bcb)
                simple_block(nc, f"{ln_pref}_ev{b}", vector=lambda v: (
                    v.tensor_copy(bca[:, :], pbc[:, :]),))

            with nc.Block() as blk:
                b_dve = P.n["dve"]

                def nrm(v):
                    for oi, (odr, mode) in enumerate(outs):
                        a0 = 0 if mode == "ln" else 2
                        ac = bca[:, a0 * TBL:(a0 + 1) * TBL]
                        bc = bca[:, (a0 + 1) * TBL:(a0 + 2) * TBL]
                        for c in range(NDC):
                            v.tensor_tensor(tmp[:, :], xt[:, c * TBL:(c + 1) * TBL],
                                            ac, ALU.mult)
                            v.tensor_tensor(
                                outt[:, (oi * NDC + c) * TBL:(oi * NDC + c + 1) * TBL],
                                tmp[:, :], bc, ALU.add).then_inc(P.sem["dve"], 1)
                            P.n["dve"] += 1
                blk.vector(nrm)

                def nst(sy):
                    for oi, (odr, mode) in enumerate(outs):
                        sy.wait_ge(P.sem["dve"], b_dve + (oi + 1) * NDC)
                        if oi > 0:
                            sy.wait_ge(P.sem["st"], P.n["st"])
                        sy.dma_start(
                            out=csrc(odr, 0, NDC, b * TBL, TBL),
                            in_=outt[:, oi * NDC * TBL:(oi + 1) * NDC * TBL]
                            .rearrange("p (c t) -> p c t", c=NDC),
                        ).then_inc(P.sem["st"], 16)
                        P.n["st"] += 16
                    sy.wait_ge(P.sem["st"], P.n["st"])
                blk.sync(nst)


def emit_phase_ln0(nc, P, dr):
    x = dr("x", [D, TPC], F32, kind="ExternalInput")
    xn = dr("xn", [D, TPC], BF16, kind="ExternalOutput")
    xnr = dr("xnr", [D, TPC], BF16, kind="ExternalOutput")
    emit_ln(nc, P, x, [(xn, "ln"), (xnr, "lnrms")])



def emit_proj_fm(nc, P, name, xin_dram, w_dram, out_dram, act, n_in_c, n_out_c,
                 scale_dram=None, res_dram=None, out_dt=BF16, wsrc_off=0,
                 tok_cols=TPC, res_dt=F32):
    K = n_in_c
    ntb = tok_cols // TBL
    stack = contextlib.ExitStack()
    xin = stack.enter_context(nc.sbuf_tensor(f"{name}_x", [128, K * ntb * TBL], BF16))
    wst = stack.enter_context(nc.sbuf_tensor(f"{name}_w", [128, 8 * K * 128], BF16))
    scl = stack.enter_context(nc.sbuf_tensor(f"{name}_s", [128, max(1, n_out_c)], F32))
    res = stack.enter_context(
        nc.sbuf_tensor(f"{name}_r", [128, 8 * TBL], res_dt)) \
        if res_dram is not None else None
    with stack:
        pairs = [(xin[:, :].rearrange("p (c b t) -> p c b t", c=K, b=ntb)[:, :, b, :],
                  csrc(xin_dram, 0, K, b * TBL, TBL))
                 for b in range(ntb)]
        if scale_dram is not None:
            pairs += [(scl[:, oc:oc + 1], scale_dram[oc * 128:(oc + 1) * 128, 0:1])
                      for oc in range(n_out_c)]
        load_block(nc, P, pairs)

        def xi(c, b):
            return xin[:, (c * ntb + b) * TBL:(c * ntb + b + 1) * TBL]

        jobs = []
        for oc in range(n_out_c):
            wsl = oc % 8
            for b in range(ntb):
                loads = []
                if b == 0:
                    loads.append((
                        wst[:, wsl * K * 128:(wsl + 1) * K * 128]
                        .rearrange("p (c o) -> p c o", c=K),
                        wstrip_ap(w_dram, K, wsrc_off + oc * 128, 128)))
                if res_dram is not None:
                    rsl = (oc * ntb + b) % 8
                    loads.append((res[:, rsl * TBL:(rsl + 1) * TBL],
                                  res_dram[oc * 128:(oc + 1) * 128,
                                           b * TBL:(b + 1) * TBL]))

                def mk_evac(oc=oc, b=b):
                    rsl = (oc * ntb + b) % 8

                    def evac(e, ps, out):
                        if res_dram is not None:
                            sc = scl[:, oc:oc + 1] if scale_dram is not None else 1.0
                            return e.scalar_tensor_tensor(
                                out, ps, sc, res[:, rsl * TBL:(rsl + 1) * TBL],
                                ALU.mult, ALU.add)
                        if scale_dram is not None and act is None:
                            return e.tensor_scalar_mul(out, ps, scl[:, oc:oc + 1])
                        if scale_dram is not None:
                            return e.activation(out, ps, act, scale=scl[:, oc:oc + 1])
                        if act is not None:
                            return e.activation(out, ps, act)
                        return ecopy(e, out, ps)
                    return evac

                jobs.append(Job(
                    kchunks=[(wst[:, (wsl * K + c) * 128:(wsl * K + c + 1) * 128],
                              xi(c, b)) for c in range(K)],
                    evac=mk_evac(),
                    store=out_dram[oc * 128:(oc + 1) * 128, b * TBL:(b + 1) * TBL],
                    loads=loads))
        ee = "vector" if (res_dram is not None
                          or (scale_dram is not None and act is None)) else "scalar"
        pipe_mm(nc, P, name, jobs, TBL, out_dt=out_dt, nps=3, nout=4,
                evac_engine=ee, ld_ahead=4)


def emit_proj_tm(nc, P, name, xin_dram, w_dram, out_dram, act, n_in_c, out_total):
    K = n_in_c
    NOB = (out_total + 511) // 512
    stack = contextlib.ExitStack()
    xin = stack.enter_context(nc.sbuf_tensor(f"{name}_x", [128, K * NTB * TBL], BF16))
    wst = stack.enter_context(nc.sbuf_tensor(f"{name}_w", [128, K * out_total], BF16))
    with stack:
        pairs = [(xin[:, :].rearrange("p (c b t) -> p c b t", c=K, b=NTB)[:, :, b, :],
                  csrc(xin_dram, 0, K, b * TBL, TBL))
                 for b in range(NTB)]
        pairs += [(wst[:, :].rearrange("p (c o) -> p c o", c=K)[:, 4 * q:4 * (q + 1), :],
                   csrc(w_dram, 4 * q, 4, 0, out_total)) for q in range(K // 4)]
        load_block(nc, P, pairs)

        def xi(c, tc):
            b, sub = tc // 4, tc % 4
            base = (c * NTB + b) * TBL + sub * 128
            return xin[:, base:base + 128]

        def evac(e, ps, out):
            if act is not None:
                return e.activation(out, ps, act)
            return ecopy(e, out, ps)

        jobs = []
        for tc in range(TPC // 128):
            for ob in range(NOB):
                ow = min(512, out_total - ob * 512)
                jobs.append(Job(
                    kchunks=[(xi(c, tc),
                              wst[:, c * out_total + ob * 512:c * out_total + ob * 512 + ow])
                             for c in range(K)],
                    evac=evac,
                    store=out_dram[tc * 128:(tc + 1) * 128, ob * 512:ob * 512 + ow]))
        pipe_mm(nc, P, name, jobs, 512, out_dt=BF16, nps=3, nout=4,
                evac_engine="scalar")


def emit_phase_qkvg(nc, P, dr):
    xnr = dr("xnr", [D, TPC], BF16, kind="ExternalInput")
    wq = dr("wq", [D, D], F32, kind="ExternalInput")
    wk = dr("wk", [D, D], F32, kind="ExternalInput")
    wv = dr("wv", [D, D], F32, kind="ExternalInput")
    wg = dr("wg", [D, D], F32, kind="ExternalInput")
    q = dr("q", [D, TPC], BF16, kind="ExternalOutput")
    k = dr("k", [D, TPC], BF16, kind="ExternalOutput")
    g = dr("g", [D, TPC], BF16, kind="ExternalOutput")
    v = dr("v", [TPC, D], BF16, kind="ExternalOutput")
    emit_proj_fm(nc, P, "pq", xnr, wq, q, AF.Silu, NDC, NDC)
    emit_proj_fm(nc, P, "pk", xnr, wk, k, AF.Silu, NDC, NDC)
    emit_proj_fm(nc, P, "pg", xnr, wg, g, AF.Sigmoid, NDC, NDC)
    emit_proj_tm(nc, P, "pv", xnr, wv, v, AF.Silu, NDC, D)



def emit_phase_scan(nc, P, dr, mode="full"):
    q = dr("q", [D, TPC], BF16, kind="ExternalInput")
    k = dr("k", [D, TPC], BF16, kind="ExternalInput")
    v = dr("v", [TPC, D], BF16, kind="ExternalInput")
    g = dr("g", [D, TPC], BF16, kind="ExternalInput")
    rankm = dr("rankm", [128, GRP], F32, kind="ExternalInput")
    ident = dr("ident", [128, 128], BF16, kind="ExternalInput")
    trilm = dr("trilm", [4, 128, BS], BF16, kind="ExternalInput")
    og = dr("og", [D, TPC], BF16, kind="ExternalOutput")

    sloc = nc.dram_tensor("sloc", [D, D], BF16)
    ccS = nc.dram_tensor("ccS", [GRP * D, D], BF16)
    oloc = nc.dram_tensor("oloc", [D, TPC], BF16)

    EH = 1024
    stack = contextlib.ExitStack()
    kvf = stack.enter_context(nc.sbuf_tensor("sc_kvf", [128, NDC * EH], F32))
    kvb = stack.enter_context(nc.sbuf_tensor("sc_kvb", [128, NDC * EH], BF16))
    qt = stack.enter_context(nc.sbuf_tensor("sc_qt", [128, NDC * BS], BF16))
    kt = stack.enter_context(nc.sbuf_tensor("sc_kt", [128, NDC * BS], BF16))
    ktm = stack.enter_context(nc.sbuf_tensor("sc_ktm", [128, 4 * D], BF16))
    vt = stack.enter_context(nc.sbuf_tensor("sc_vt", [128, 4 * EH], BF16))
    stb = stack.enter_context(nc.sbuf_tensor("sc_stb", [128, 4 * BS], BF16))
    idn = stack.enter_context(nc.sbuf_tensor("sc_idn", [128, 128], BF16))
    trm = stack.enter_context(nc.sbuf_tensor("sc_trm", [128, 4 * BS], BF16))
    rmc = stack.enter_context(nc.sbuf_tensor("sc_rmc", [128, GRP], F32))

    def kvf_s(dc, e0, w):
        return kvf[:, dc * EH + e0:dc * EH + e0 + w]

    def kvb_s(dc, e0, w):
        return kvb[:, dc * EH + e0:dc * EH + e0 + w]

    with stack:
        load_block(nc, P, [(idn[:, :], ident[:, :]), (rmc[:, :], rankm[:, :])]
                   + [(trm[:, kc * BS:(kc + 1) * BS], trilm[kc, :, :])
                      for kc in range(4)])

        for h in range(2):
            e0g = h * EH
            simple_block(nc, f"sc_z{h}", gpsimd=lambda gg: (
                gg.memset(kvf[:, :], 0.0), gg.memset(kvb[:, :], 0.0)))
            for c in range(NCH):
                t0 = c * BS
                load_block(nc, P, [
                    (qt[:, :].rearrange("p (c t) -> p c t", c=NDC),
                     csrc(q, 0, NDC, t0, BS)),
                    (kt[:, :].rearrange("p (c t) -> p c t", c=NDC),
                     csrc(k, 0, NDC, t0, BS)),
                    (vt[:, :].rearrange("p (s e) -> p s e", s=4),
                     csrc(v, t0 // 128, 4, e0g, EH)),
                ])

                with (
                    nc.psum_tensor(f"sc_pta{h}{c}", [128, 1024], BF16) as pta,
                    nc.psum_tensor(f"sc_ptb{h}{c}", [128, 1024], BF16) as ptb,
                    nc.Block() as blk,
                ):
                    psts = [pta, ptb]
                    b_mm, b_dve = P.n["mm"], P.n["dve"]

                    def tb(t):
                        i = 0
                        for dc in range(NDC):
                            for s in range(4):
                                if i >= 2:
                                    t.wait_ge(P.sem["dve"], b_dve + i - 1)
                                t.transpose(
                                    psts[i % 2][:, 0:128],
                                    kt[:, dc * BS + s * 128:dc * BS + (s + 1) * 128],
                                    idn[:, :]).then_inc(P.sem["mm"], 1)
                                i += 1
                    blk.tensor(tb)
                    P.n["mm"] = b_mm + 64

                    def te(v_):
                        i = 0
                        for dc in range(NDC):
                            for s in range(4):
                                v_.wait_ge(P.sem["mm"], b_mm + i + 1)
                                v_.tensor_copy(
                                    ktm[:, s * D + dc * 128:s * D + (dc + 1) * 128],
                                    psts[i % 2][:, 0:128]).then_inc(P.sem["dve"], 1)
                                i += 1
                    blk.vector(te)
                    P.n["dve"] = b_dve + 64

                def sc_evac(kc):
                    def evac(e, ps, out):
                        return e.scalar_tensor_tensor(
                            out, ps, 1.0, trm[:, kc * BS:(kc + 1) * BS],
                            ALU.mult, ALU.mult)
                    return evac
                jobs = [Job(
                    kchunks=[(kt[:, dc * BS + kc * 128:dc * BS + (kc + 1) * 128],
                              qt[:, dc * BS:(dc + 1) * BS]) for dc in range(NDC)],
                    evac=sc_evac(kc), store=None) for kc in range(4)]
                pipe_mm(nc, P, f"sc_s{h}{c}", jobs, BS, out_dt=BF16, nps=3,
                        evac_engine="vector",
                        persistent_out=lambda j: stb[:, j * BS:(j + 1) * BS])

                def o_evac(e, ps, out):
                    return ecopy(e, out, ps)
                jobs = []
                for ec in range(EH // 128):
                    kch = [(vt[:, s * EH + ec * 128:s * EH + (ec + 1) * 128],
                            stb[:, s * BS:(s + 1) * BS]) for s in range(4)]
                    if c > 0:
                        kch += [(kvb_s(dc, ec * 128, 128),
                                 qt[:, dc * BS:(dc + 1) * BS]) for dc in range(NDC)]
                    jobs.append(Job(
                        kchunks=kch, evac=o_evac,
                        store=oloc[e0g + ec * 128:e0g + (ec + 1) * 128, t0:t0 + BS]))
                pipe_mm(nc, P, f"sc_o{h}{c}", jobs, BS, out_dt=BF16, nps=3, nout=4,
                        evac_engine="scalar")

                with (
                    nc.psum_tensor(f"sc_pua{h}{c}", [128, 512], F32) as pua,
                    nc.psum_tensor(f"sc_pub{h}{c}", [128, 512], F32) as pub,
                    nc.psum_tensor(f"sc_puc{h}{c}", [128, 512], F32) as puc,
                    nc.psum_tensor(f"sc_pud{h}{c}", [128, 512], F32) as pud,
                    nc.Block() as blk,
                ):
                    pus = [pua, pub, puc, pud]
                    b_mm, b_dve = P.n["mm"], P.n["dve"]

                    def ub_t(t):
                        i = 0
                        for dc in range(NDC):
                            for eb in range(EH // 512):
                                if i >= 4:
                                    t.wait_ge(P.sem["dve"], b_dve + i - 3)
                                last = None
                                for s in range(4):
                                    last = t.matmul(
                                        pus[i % 4][:, :],
                                        ktm[:, s * D + dc * 128:s * D + (dc + 1) * 128],
                                        vt[:, s * EH + eb * 512:s * EH + (eb + 1) * 512],
                                        start=(s == 0), stop=(s == 3))
                                last.then_inc(P.sem["mm"], 1)
                                i += 1
                    blk.tensor(ub_t)
                    P.n["mm"] = b_mm + 32

                    def ub_v(v_):
                        i = 0
                        for dc in range(NDC):
                            for eb in range(EH // 512):
                                v_.wait_ge(P.sem["mm"], b_mm + i + 1)
                                v_.tensor_add(kvf_s(dc, eb * 512, 512),
                                              kvf_s(dc, eb * 512, 512),
                                              pus[i % 4][:, :])
                                v_.tensor_copy(kvb_s(dc, eb * 512, 512),
                                               kvf_s(dc, eb * 512, 512)) \
                                    .then_inc(P.sem["dve"], 1)
                                i += 1
                    blk.vector(ub_v)
                    P.n["dve"] = b_dve + 32

            store_block(nc, P, [
                (sloc[dc * 128:(dc + 1) * 128, e0g:e0g + EH],
                 kvb[:, dc * EH:(dc + 1) * EH]) for dc in range(NDC)])

    if mode == "full":
        with nc.Block() as blk:
            def ccb(gg):
                gg.collective_compute(
                    "AllGather", ALU.bypass,
                    replica_groups=[[0, 1, 2, 3], [4, 5, 6, 7]],
                    ins=[sloc.ap().opt()], outs=[ccS.ap().opt()]) \
                    .then_inc(P.sem["cc"])
                P.n["cc"] += 1
                gg.wait_ge(P.sem["cc"], P.n["cc"])
            blk.gpsimd(ccb)

    stack2 = contextlib.ExitStack()
    Pb = stack2.enter_context(nc.sbuf_tensor("sc_P", [128, NDC * D], BF16))
    Pr = stack2.enter_context(nc.sbuf_tensor("sc_Pr", [128, GRP * D], BF16))
    Pf = stack2.enter_context(nc.sbuf_tensor("sc_Pf", [128, D], F32))
    qa = stack2.enter_context(nc.sbuf_tensor("sc_qa", [128, 2 * NDC * TBL], BF16))
    otg = stack2.enter_context(nc.sbuf_tensor("sc_otg", [128, 16 * TBL], BF16))
    tmpf = stack2.enter_context(nc.sbuf_tensor("sc_tf2", [128, TBL], F32))
    with stack2:
        if mode == "nop":
            simple_block(nc, "sc_pz", gpsimd=lambda gg: (gg.memset(Pb[:, :], 0.0),))
        else:
            for dc in range(NDC):
                load_block(nc, P, [
                    (Pr[:, :].rearrange("p (r e) -> p r e", r=GRP),
                     ccS.ap().rearrange("(r c p) e -> p r c e", r=GRP, p=128)[:, :, dc, :])])

                def pacc(v_, dc=dc):
                    v_.tensor_scalar_mul(Pf[:, :], Pr[:, 0:D], rmc[:, 0:1])
                    for rr in range(1, GRP):
                        v_.scalar_tensor_tensor(Pf[:, :], Pr[:, rr * D:(rr + 1) * D],
                                                rmc[:, rr:rr + 1], Pf[:, :],
                                                ALU.mult, ALU.add)
                    v_.tensor_copy(Pb[:, dc * D:(dc + 1) * D], Pf[:, :])
                simple_block(nc, f"sc_pa{dc}", vector=pacc)

        jobs = []
        for b in range(NTB):
            for ec in range(NDC):
                loads = []
                if ec == 0:
                    loads += [(qa[:, ((b % 2) * NDC + dc) * TBL:
                                  ((b % 2) * NDC + dc + 1) * TBL],
                               q[dc * 128:(dc + 1) * 128, b * TBL:(b + 1) * TBL])
                              for dc in range(NDC)]
                j = b * NDC + ec
                sl = j % 8
                loads += [(otg[:, sl * TBL:(sl + 1) * TBL],
                           oloc[ec * 128:(ec + 1) * 128, b * TBL:(b + 1) * TBL]),
                          (otg[:, (8 + sl) * TBL:(8 + sl + 1) * TBL],
                           g[ec * 128:(ec + 1) * 128, b * TBL:(b + 1) * TBL])]

                def mk_evac(sl=sl):
                    def evac(e, ps, out):
                        e.tensor_tensor(tmpf[:, :], ps,
                                        otg[:, sl * TBL:(sl + 1) * TBL], ALU.add)
                        return e.tensor_tensor(
                            out, tmpf[:, :],
                            otg[:, (8 + sl) * TBL:(8 + sl + 1) * TBL], ALU.mult)
                    return evac
                jobs.append(Job(
                    kchunks=[(Pb[:, dc * D + ec * 128:dc * D + (ec + 1) * 128],
                              qa[:, ((b % 2) * NDC + dc) * TBL:
                                 ((b % 2) * NDC + dc + 1) * TBL])
                             for dc in range(NDC)],
                    evac=mk_evac(),
                    store=og[ec * 128:(ec + 1) * 128, b * TBL:(b + 1) * TBL],
                    loads=loads))
        pipe_mm(nc, P, "sc_x", jobs, TBL, out_dt=BF16, nps=3, nout=4,
                evac_engine="vector", ld_ahead=6)




def emit_glu(nc, P, name, xin_dram, wg_dram, wu_dram, h_dram, n_in_c, n_i_c,
             gs_dram=None, us_dram=None, wg_off=0, wu_off=0):
    K = n_in_c
    stack = contextlib.ExitStack()
    xin = stack.enter_context(nc.sbuf_tensor(f"{name}_x", [128, K * NTB * TBL], BF16))
    wst = stack.enter_context(nc.sbuf_tensor(f"{name}_w", [128, 4 * 2 * K * 128], BF16))
    scl = stack.enter_context(nc.sbuf_tensor(f"{name}_s", [128, 2 * max(1, n_i_c)], F32))
    hg = stack.enter_context(nc.sbuf_tensor(f"{name}_hg", [128, 3 * 2 * TBL], BF16))
    stage = stack.enter_context(nc.sbuf_tensor(f"{name}_st", [128, 4 * TBL], BF16))
    psa = stack.enter_context(nc.psum_tensor(f"{name}_pa", [128, 2 * TBL], F32))
    psb = stack.enter_context(nc.psum_tensor(f"{name}_pb", [128, 2 * TBL], F32))
    with stack:
        pairs = [(xin[:, :].rearrange("p (c b t) -> p c b t", c=K, b=NTB)[:, :, b, :],
                  csrc(xin_dram, 0, K, b * TBL, TBL))
                 for b in range(NTB)]
        if gs_dram is not None:
            pairs += [(scl[:, ic:ic + 1], gs_dram[ic * 128:(ic + 1) * 128, 0:1])
                      for ic in range(n_i_c)]
            pairs += [(scl[:, n_i_c + ic:n_i_c + ic + 1],
                       us_dram[ic * 128:(ic + 1) * 128, 0:1]) for ic in range(n_i_c)]
        load_block(nc, P, pairs)

        def xi(c, b):
            return xin[:, (c * NTB + b) * TBL:(c * NTB + b + 1) * TBL]

        pss = [psa, psb]
        njob = n_i_c * NTB
        b_ld, b_mm, b_act, b_dve, b_st = (P.n["ld"], P.n["mm"], P.n["act"],
                                          P.n["dve"], P.n["st"])
        with nc.Block() as blk:
            @blk.gpsimd
            def _(g):
                nld = 0
                for ic in range(n_i_c):
                    wsl = ic % 4
                    j0 = ic * NTB
                    if j0 > 2:
                        g.wait_ge(P.sem["dve"], b_dve + j0 - 2)
                    if ic > 0:
                        g.wait_ge(P.sem["ld"], b_ld + 32 * ic)
                    g.dma_start(
                        out=wst[:, (2 * wsl) * K * 128:(2 * wsl + 1) * K * 128]
                            .rearrange("p (c o) -> p c o", c=K),
                        in_=wstrip_ap(wg_dram, K, wg_off + ic * 128, 128),
                    ).then_inc(P.sem["ld"], 16)
                    g.dma_start(
                        out=wst[:, (2 * wsl + 1) * K * 128:(2 * wsl + 2) * K * 128]
                            .rearrange("p (c o) -> p c o", c=K),
                        in_=wstrip_ap(wu_dram, K, wu_off + ic * 128, 128),
                    ).then_inc(P.sem["ld"], 16)
                    nld += 32
            P.n["ld"] = b_ld + 32 * n_i_c

            @blk.tensor
            def _(t):
                for ic in range(n_i_c):
                    wsl = ic % 4
                    t.wait_ge(P.sem["ld"], b_ld + 32 * (ic + 1))
                    for b in range(NTB):
                        j = ic * NTB + b
                        if j >= 2:
                            t.wait_ge(P.sem["act"], b_act + j - 1)
                        for c in range(K):
                            t.matmul(pss[j % 2][:, 0:TBL],
                                     wst[:, ((2 * wsl) * K + c) * 128:
                                         ((2 * wsl) * K + c + 1) * 128],
                                     xi(c, b), start=(c == 0), stop=(c == K - 1))
                        last = None
                        for c in range(K):
                            last = t.matmul(
                                pss[j % 2][:, TBL:2 * TBL],
                                wst[:, ((2 * wsl + 1) * K + c) * 128:
                                    ((2 * wsl + 1) * K + c + 1) * 128],
                                xi(c, b), start=(c == 0), stop=(c == K - 1))
                        last.then_inc(P.sem["mm"], 1)
            P.n["mm"] = b_mm + njob

            @blk.scalar
            def _(s):
                for ic in range(n_i_c):
                    for b in range(NTB):
                        j = ic * NTB + b
                        s.wait_ge(P.sem["mm"], b_mm + j + 1)
                        if j >= 3:
                            s.wait_ge(P.sem["dve"], b_dve + j - 2)
                        sl = j % 3
                        if gs_dram is not None:
                            s.activation(hg[:, (2 * sl) * TBL:(2 * sl + 1) * TBL],
                                         pss[j % 2][:, 0:TBL], AF.Silu,
                                         scale=scl[:, ic:ic + 1])
                            s.activation(hg[:, (2 * sl + 1) * TBL:(2 * sl + 2) * TBL],
                                         pss[j % 2][:, TBL:2 * TBL], AF.Copy,
                                         scale=scl[:, n_i_c + ic:n_i_c + ic + 1]) \
                                .then_inc(P.sem["act"], 1)
                        else:
                            s.activation(hg[:, (2 * sl) * TBL:(2 * sl + 1) * TBL],
                                         pss[j % 2][:, 0:TBL], AF.Silu)
                            s.activation(hg[:, (2 * sl + 1) * TBL:(2 * sl + 2) * TBL],
                                         pss[j % 2][:, TBL:2 * TBL], AF.Copy) \
                                .then_inc(P.sem["act"], 1)
            P.n["act"] = b_act + njob

            @blk.vector
            def _(v):
                for ic in range(n_i_c):
                    for b in range(NTB):
                        j = ic * NTB + b
                        v.wait_ge(P.sem["act"], b_act + j + 1)
                        if j >= 4:
                            v.wait_ge(P.sem["st"], b_st + 16 * (j - 3))
                        sl = j % 3
                        v.tensor_mul(stage[:, (j % 4) * TBL:(j % 4 + 1) * TBL],
                                     hg[:, (2 * sl) * TBL:(2 * sl + 1) * TBL],
                                     hg[:, (2 * sl + 1) * TBL:(2 * sl + 2) * TBL]) \
                            .then_inc(P.sem["dve"], 1)
            P.n["dve"] = b_dve + njob

            @blk.sync
            def _(sy):
                for ic in range(n_i_c):
                    for b in range(NTB):
                        j = ic * NTB + b
                        sy.wait_ge(P.sem["dve"], b_dve + j + 1)
                        sy.dma_start(
                            out=h_dram[ic * 128:(ic + 1) * 128, b * TBL:(b + 1) * TBL],
                            in_=stage[:, (j % 4) * TBL:(j % 4 + 1) * TBL],
                        ).then_inc(P.sem["st"], 16)
                sy.wait_ge(P.sem["st"], b_st + 16 * njob)
            P.n["st"] = b_st + 16 * njob


def emit_down(nc, P, name, h_dram, w_dram, out_dram, res_dram, n_i_c, n_out_c,
              ds_dram=None, out_dt=F32, nhalf=2):
    K = n_i_c
    oc_half = n_out_c // nhalf
    stack = contextlib.ExitStack()
    hst = stack.enter_context(nc.sbuf_tensor(f"{name}_h", [128, K * TBL], BF16))
    wst = stack.enter_context(
        nc.sbuf_tensor(f"{name}_w", [128, K * oc_half * 128], BF16))
    scl = stack.enter_context(nc.sbuf_tensor(f"{name}_s", [128, max(1, n_out_c)], F32))
    res = stack.enter_context(nc.sbuf_tensor(f"{name}_r", [128, 8 * TBL], F32))
    with stack:
        if ds_dram is not None:
            load_block(nc, P, [
                (scl[:, oc:oc + 1], ds_dram[oc * 128:(oc + 1) * 128, 0:1])
                for oc in range(n_out_c)])

        jobs = []
        for oh in range(nhalf):
            for b in range(NTB):
                for oc in range(oc_half):
                    j = len(jobs)
                    loads = []
                    gate = None
                    if b == 0 and oc == 0:
                        loads += [(wst[:, :].rearrange("p (c o) -> p c o", c=K)
                                   [:, 4 * qq:4 * (qq + 1), :],
                                   csrc(w_dram, 4 * qq, 4, oh * oc_half * 128,
                                        oc_half * 128))
                                  for qq in range(K // 4)]
                        gate = j - 1
                    if oc == 0:
                        loads += [(hst[:, :].rearrange("p (c t) -> p c t", c=K),
                                   csrc(h_dram, 0, K, b * TBL, TBL))]
                        if gate is None:
                            gate = j - 1
                    ocg = oh * oc_half + oc
                    rsl = j % 8
                    loads += [(res[:, rsl * TBL:(rsl + 1) * TBL],
                               res_dram[ocg * 128:(ocg + 1) * 128,
                                        b * TBL:(b + 1) * TBL])]

                    def mk_evac(ocg=ocg, rsl=rsl):
                        def evac(e, ps, out):
                            sc = scl[:, ocg:ocg + 1] if ds_dram is not None else 1.0
                            return e.scalar_tensor_tensor(
                                out, ps, sc, res[:, rsl * TBL:(rsl + 1) * TBL],
                                ALU.mult, ALU.add)
                        return evac

                    jobs.append(Job(
                        kchunks=[(wst[:, c * oc_half * 128 + oc * 128:
                                      c * oc_half * 128 + (oc + 1) * 128],
                                  hst[:, c * TBL:(c + 1) * TBL]) for c in range(K)],
                        evac=mk_evac(),
                        store=out_dram[ocg * 128:(ocg + 1) * 128,
                                       b * TBL:(b + 1) * TBL],
                        loads=loads, ld_gate=gate))
        pipe_mm(nc, P, name, jobs, TBL, out_dt=out_dt, nps=3, nout=4,
                evac_engine="vector", ld_ahead=6)


def emit_phase_wp_res(nc, P, dr):
    og = dr("og", [D, TPC], BF16, kind="ExternalInput")
    wp = dr("wp", [D, D], F32, kind="ExternalInput")
    x = dr("x", [D, TPC], F32, kind="ExternalInput")
    xmid = dr("xmid", [D, TPC], F32, kind="ExternalOutput")
    emit_proj_fm(nc, P, "wp", og, wp, xmid, None, NDC, NDC, res_dram=x, out_dt=F32)


def emit_phase_ffn0(nc, P, dr):
    xn = dr("xn", [D, TPC], BF16, kind="ExternalInput")
    wgate = dr("wgate", [D, INTER0], F32, kind="ExternalInput")
    wup = dr("wup", [D, INTER0], F32, kind="ExternalInput")
    gs = dr("gs", [INTER0, 1], F32, kind="ExternalInput")
    us = dr("us", [INTER0, 1], F32, kind="ExternalInput")
    h = dr("h", [INTER0, TPC], BF16, kind="ExternalOutput")
    emit_glu(nc, P, "f0", xn, wgate, wup, h, NDC, NIC0, gs_dram=gs, us_dram=us)


def emit_phase_down0(nc, P, dr):
    h = dr("h", [INTER0, TPC], BF16, kind="ExternalInput")
    wdown = dr("wdown", [INTER0, D], F32, kind="ExternalInput")
    ds = dr("ds", [D, 1], F32, kind="ExternalInput")
    xmid = dr("xmid", [D, TPC], F32, kind="ExternalInput")
    x1 = dr("x1", [D, TPC], F32, kind="ExternalOutput")
    emit_down(nc, P, "d0", h, wdown, x1, xmid, NIC0, NDC, ds_dram=ds)




def emit_fm2tm(nc, P, name, in_dram, out_dram, n_r_c=NDC, n_t_c=TPC // 128):
    stack = contextlib.ExitStack()
    xall = stack.enter_context(
        nc.sbuf_tensor(f"{name}_x", [128, n_r_c * n_t_c * 128], BF16))
    rowb = stack.enter_context(nc.sbuf_tensor(f"{name}_rb", [128, 2 * n_r_c * 128], BF16))
    pta = stack.enter_context(nc.psum_tensor(f"{name}_pa", [128, 1024], BF16))
    ptb = stack.enter_context(nc.psum_tensor(f"{name}_pb", [128, 1024], BF16))
    idn = stack.enter_context(nc.sbuf_tensor(f"{name}_id", [128, 128], BF16))
    with stack:
        simple_block(nc, f"{name}_id", gpsimd=lambda g: (
            g.iota(idn[:, :], [[1, 128]], channel_multiplier=-1,
                   allow_small_or_imprecise_dtypes=True),))
        simple_block(nc, f"{name}_id2", vector=lambda v: (
            v.tensor_single_scalar(idn[:, :], idn[:, :], 0.0, ALU.is_equal),))
        load_block(nc, P, [
            (xall[:, (rc * n_t_c + tc) * 128:(rc * n_t_c + tc + 1) * 128],
             in_dram[rc * 128:(rc + 1) * 128, tc * 128:(tc + 1) * 128])
            for rc in range(n_r_c) for tc in range(n_t_c)])

        pss = [pta, ptb]
        b_mm, b_dve, b_st = P.n["mm"], P.n["dve"], P.n["st"]
        with nc.Block() as blk:
            @blk.tensor
            def _(t):
                i = 0
                for tc in range(n_t_c):
                    for rc in range(n_r_c):
                        if i >= 2:
                            t.wait_ge(P.sem["dve"], b_dve + i - 1)
                        t.transpose(
                            pss[i % 2][:, 0:128],
                            xall[:, (rc * n_t_c + tc) * 128:(rc * n_t_c + tc + 1) * 128],
                            idn[:, :]).then_inc(P.sem["mm"], 1)
                        i += 1
            P.n["mm"] = b_mm + n_t_c * n_r_c

            @blk.vector
            def _(v):
                i = 0
                for tc in range(n_t_c):
                    for rc in range(n_r_c):
                        v.wait_ge(P.sem["mm"], b_mm + i + 1)
                        if tc >= 2 and rc == 0:
                            v.wait_ge(P.sem["st"], b_st + 16 * (tc - 1))
                        v.tensor_copy(
                            rowb[:, ((tc % 2) * n_r_c + rc) * 128:
                                 ((tc % 2) * n_r_c + rc + 1) * 128],
                            pss[i % 2][:, 0:128]).then_inc(P.sem["dve"], 1)
                        i += 1
            P.n["dve"] = b_dve + n_t_c * n_r_c

            @blk.sync
            def _(sy):
                for tc in range(n_t_c):
                    sy.wait_ge(P.sem["dve"], b_dve + (tc + 1) * n_r_c)
                    if tc > 0:
                        sy.wait_ge(P.sem["st"], b_st + 16 * tc)
                    sy.dma_start(
                        out=out_dram[tc * 128:(tc + 1) * 128, :],
                        in_=rowb[:, (tc % 2) * n_r_c * 128:
                                 ((tc % 2) + 1) * n_r_c * 128],
                    ).then_inc(P.sem["st"], 16)
                sy.wait_ge(P.sem["st"], b_st + 16 * n_t_c)
            P.n["st"] = b_st + 16 * n_t_c


def emit_phase_ln1(nc, P, dr):
    x1 = dr("x1", [D, TPC], F32, kind="ExternalInput")
    xn1 = dr("xn1", [D, TPC], BF16, kind="ExternalOutput")
    xn1t = dr("xn1t", [TPC, D], BF16, kind="ExternalOutput")
    emit_ln(nc, P, x1, [(xn1, "ln")], ln_pref="l1")
    emit_fm2tm(nc, P, "tq", xn1, xn1t)


def emit_phase_qkv1(nc, P, dr):
    xn1 = dr("xn1", [D, TPC], BF16, kind="ExternalInput")
    wq = dr("wq", [D, D], F32, kind="ExternalInput")
    wk = dr("wk", [D, NKV * HS], F32, kind="ExternalInput")
    wv = dr("wv", [D, NKV * HS], F32, kind="ExternalInput")
    q1 = dr("q1", [D, TPC], BF16, kind="ExternalOutput")
    k1 = dr("k1", [TPC, NKV * HS], BF16, kind="ExternalOutput")
    v1 = dr("v1", [TPC, NKV * HS], BF16, kind="ExternalOutput")
    emit_proj_fm(nc, P, "q1", xn1, wq, q1, None, NDC, NDC)
    emit_proj_tm(nc, P, "k1", xn1, wk, k1, None, NDC, NKV * HS)
    emit_proj_tm(nc, P, "v1", xn1, wv, v1, None, NDC, NKV * HS)


def emit_phase_wo_res(nc, P, dr):
    y = dr("y", [D, TPC], BF16, kind="ExternalInput")
    wo = dr("wo", [D, D], F32, kind="ExternalInput")
    res = dr("x1", [D, TPC], F32, kind="ExternalInput")
    out = dr("xa", [D, TPC], F32, kind="ExternalOutput")
    emit_proj_fm(nc, P, "wo", y, wo, out, None, NDC, NDC, res_dram=res, out_dt=F32)


def emit_phase_shared(nc, P, dr):
    xn1 = dr("xn1", [D, TPC], BF16, kind="ExternalInput")
    wug = dr("wug", [D, 2 * JSH], F32, kind="ExternalInput")
    h = dr("h", [JSH, TPC], BF16, kind="ExternalOutput")
    emit_glu(nc, P, "sh", xn1, wug, wug, h, NDC, NIC0, wg_off=0, wu_off=JSH)


def emit_phase_shdown(nc, P, dr):
    h = dr("h", [JSH, TPC], BF16, kind="ExternalInput")
    wdown = dr("wdown", [JSH, D], F32, kind="ExternalInput")
    xa = dr("xa", [D, TPC], F32, kind="ExternalInput")
    xa2 = dr("xa2", [D, TPC], F32, kind="ExternalOutput")
    emit_down(nc, P, "sd", h, wdown, xa2, xa, NIC0, NDC, ds_dram=None)




def emit_phase_attn1(nc, P, dr):
    q1 = dr("q1", [D, TPC], BF16, kind="ExternalInput")
    k1 = dr("k1", [TPC, NKV * HS], BF16, kind="ExternalInput")
    v1 = dr("v1", [TPC, NKV * HS], BF16, kind="ExternalInput")
    khalo = dr("khalo", [1152, NKV * HS], BF16, kind="ExternalInput")
    vhalo = dr("vhalo", [1152, NKV * HS], BF16, kind="ExternalInput")
    bandml = dr("bandml", [4, 128, QT], BF16, kind="ExternalInput")
    bandmr = dr("bandmr", [4, 128, QT], BF16, kind="ExternalInput")
    sinkm = dr("sinkm", [NQT, 4, QT], BF16, kind="ExternalInput")
    ninv = dr("ninv", [NQT, QT], F32, kind="ExternalInput")
    y = dr("y", [D, TPC], BF16, kind="ExternalOutput")

    NT_H = 9
    NT_A = 25

    stack = contextlib.ExitStack()
    kfm = stack.enter_context(nc.sbuf_tensor("a1_kfm", [128, NKV * 3200], BF16))
    vbuf = stack.enter_context(nc.sbuf_tensor("a1_vb", [128, NT_A * NKV * HS], BF16))
    qh = stack.enter_context(nc.sbuf_tensor("a1_qh", [128, 2 * QT], BF16))
    pbuf = stack.enter_context(nc.sbuf_tensor("a1_pb", [128, 13 * QT], BF16))
    mls = stack.enter_context(nc.sbuf_tensor("a1_ml", [128, 8 * QT], BF16))
    snk = stack.enter_context(nc.sbuf_tensor("a1_sm", [128, NQT * QT], BF16))
    niv = stack.enter_context(nc.sbuf_tensor("a1_ni", [1, NQT * QT], F32))
    rows = stack.enter_context(nc.sbuf_tensor("a1_rw", [1, 2 * QT], F32))
    denb = stack.enter_context(nc.sbuf_tensor("a1_db", [128, QT], F32))
    ybuf = stack.enter_context(nc.sbuf_tensor("a1_yb", [128, 4 * QT], BF16))
    ones_c = stack.enter_context(nc.sbuf_tensor("a1_oc", [128, 1], BF16))
    ones_r = stack.enter_context(nc.sbuf_tensor("a1_or", [1, 128], F32))
    idn = stack.enter_context(nc.sbuf_tensor("a1_id", [128, 128], BF16))
    bsh = stack.enter_context(nc.sbuf_tensor("a1_bs", [128, 1], F32))
    em12 = stack.enter_context(nc.sbuf_tensor("a1_em", [1, 2], F32))
    em12b = stack.enter_context(nc.sbuf_tensor("a1_emb", [1, 1], BF16))

    with stack:
        simple_block(nc, "a1_i0", gpsimd=lambda g: (
            g.iota(idn[:, :], [[1, 128]], channel_multiplier=-1,
                   allow_small_or_imprecise_dtypes=True),
            g.memset(ones_c[:, :], 1.0),
            g.memset(ones_r[:, :], 1.0),
            g.memset(em12[:, 0:1], 0.0),
            g.memset(bsh[:, :], ESHIFT)))
        simple_block(nc, "a1_em1", scalar=lambda s: (
            s.activation(em12b[:, :], em12[:, 0:1], AF.Exp,
                         bias=bsh[0:1, :], scale=SCALE_ATT),))
        simple_block(nc, "a1_em2", vector=lambda v: (
            v.tensor_scalar_mul(em12[:, 1:2], em12b[:, :], -1.0),))
        simple_block(nc, "a1_i1", vector=lambda v: (
            v.tensor_single_scalar(idn[:, :], idn[:, :], 0.0, ALU.is_equal),))

        load_block(nc, P, [
            (vbuf[:, 0:NT_H * NKV * HS].rearrange("p (b f) -> p b f", b=NT_H),
             csrc(vhalo, 0, NT_H, 0, NKV * HS)),
            (vbuf[:, NT_H * NKV * HS:].rearrange("p (b f) -> p b f", b=16),
             csrc(v1, 0, 16, 0, NKV * HS)),
        ] + [
            (mls[:, i * QT:(i + 1) * QT], bandml[i, :, :]) for i in range(4)
        ] + [
            (mls[:, (4 + i) * QT:(5 + i) * QT], bandmr[i, :, :]) for i in range(4)
        ] + [
            (snk[0:4, qt * QT:(qt + 1) * QT], sinkm[qt, :, :]) for qt in range(NQT)
        ] + [
            (niv[:, qt * QT:(qt + 1) * QT], ninv[qt:qt + 1, :]) for qt in range(NQT)
        ])

        with (
            nc.psum_tensor("a1_pta", [128, 1024], BF16) as pta,
            nc.psum_tensor("a1_ptb", [128, 1024], BF16) as ptb,
            nc.sbuf_tensor("a1_krow", [128, 2 * NKV * HS], BF16) as krow,
            nc.Block() as blk,
        ):
            pss = [pta, ptb]
            b_mm, b_dve, b_ld = P.n["mm"], P.n["dve"], P.n["ld"]
            NTT = 25 * NKV

            @blk.gpsimd
            def _(g):
                for bt in range(25):
                    if bt >= 2:
                        g.wait_ge(P.sem["mm"], b_mm + (bt - 1) * NKV)
                    if bt > 0:
                        g.wait_ge(P.sem["ld"], b_ld + 16 * bt)
                    src = (khalo[bt * 128:(bt + 1) * 128, :] if bt < 9 else
                           k1[(bt - 9) * 128:(bt - 9 + 1) * 128, :])
                    g.dma_start(
                        out=krow[:, (bt % 2) * NKV * HS:((bt % 2) + 1) * NKV * HS],
                        in_=src).then_inc(P.sem["ld"], 16)
            P.n["ld"] = b_ld + 25 * 16

            @blk.tensor
            def _(t):
                i = 0
                for bt in range(25):
                    t.wait_ge(P.sem["ld"], b_ld + 16 * (bt + 1))
                    for kv in range(NKV):
                        if i >= 2:
                            t.wait_ge(P.sem["dve"], b_dve + i - 1)
                        t.transpose(
                            pss[i % 2][:, 0:128],
                            krow[:, ((bt % 2) * NKV + kv) * HS:
                                 ((bt % 2) * NKV + kv + 1) * HS],
                            idn[:, :]).then_inc(P.sem["mm"], 1)
                        i += 1
            P.n["mm"] = b_mm + NTT

            @blk.vector
            def _(v_):
                i = 0
                for bt in range(25):
                    for kv in range(NKV):
                        v_.wait_ge(P.sem["mm"], b_mm + i + 1)
                        if bt < 8:
                            dst = kfm[:, kv * 3200 + bt * 128:kv * 3200 + (bt + 1) * 128]
                            src = pss[i % 2][:, 0:128]
                        elif bt == 8:
                            dst = kfm[:, kv * 3200 + 3072:kv * 3200 + 3076]
                            src = pss[i % 2][:, 0:4]
                        else:
                            c0 = 1024 + (bt - 9) * 128
                            dst = kfm[:, kv * 3200 + c0:kv * 3200 + c0 + 128]
                            src = pss[i % 2][:, 0:128]
                        v_.tensor_copy(dst, src).then_inc(P.sem["dve"], 1)
                        i += 1
            P.n["dve"] = b_dve + NTT

        MASKED = list(range(4)) + list(range(8, 12))
        b_ld, b_mm = P.n["ld"], P.n["mm"]
        b_act, b_dve, b_st = P.n["act"], P.n["dve"], P.n["st"]
        NPAIR = NH * NQT
        PMM, PACT, PDVE = 40, 13, 11
        with (
            nc.psum_tensor("a1_s0", [128, QT], F32) as S0,
            nc.psum_tensor("a1_s1", [128, QT], F32) as S1,
            nc.psum_tensor("a1_s2", [128, QT], F32) as S2,
            nc.psum_tensor("a1_sk", [4, QT], F32) as SK,
            nc.psum_tensor("a1_y0", [128, QT], F32) as Y0,
            nc.psum_tensor("a1_y1", [128, QT], F32) as Y1,
            nc.psum_tensor("a1_dn", [1, QT], F32) as DEN,
            nc.psum_tensor("a1_bc", [128, QT], F32) as BCp,
            nc.Block() as blk,
        ):
            Ss = [S0, S1, S2]
            Ys = [Y0, Y1]

            @blk.gpsimd
            def _(g):
                for p in range(NPAIR):
                    h, qt = p // NQT, p % NQT
                    if p >= 2:
                        g.wait_ge(P.sem["mm"], b_mm + (p - 1) * PMM)
                    if p > 0:
                        g.wait_ge(P.sem["ld"], b_ld + 16 * p)
                    g.dma_start(out=qh[:, (p % 2) * QT:(p % 2 + 1) * QT],
                                in_=q1[h * HS:(h + 1) * HS, qt * QT:(qt + 1) * QT]) \
                        .then_inc(P.sem["ld"], 16)
            P.n["ld"] = b_ld + NPAIR * 16

            @blk.tensor
            def _(t):
                for p in range(NPAIR):
                    h, qt = p // NQT, p % NQT
                    kv = h // (NH // NKV)
                    M0 = b_mm + p * PMM
                    A0 = b_act + p * PACT
                    D0 = b_dve + p * PDVE
                    t.wait_ge(P.sem["ld"], b_ld + 16 * (p + 1))
                    for kc in range(12):
                        if p > 0 and kc < 3:
                            t.wait_ge(P.sem["act"], A0)
                        elif kc >= 3:
                            t.wait_ge(P.sem["act"], A0 + kc - 2)
                        c0 = kv * 3200 + qt * QT + kc * 128
                        t.matmul(Ss[kc % 3][:, :], kfm[:, c0:c0 + 128],
                                 qh[:, (p % 2) * QT:(p % 2 + 1) * QT],
                                 start=True, stop=True).then_inc(P.sem["mm"], 1)
                    t.matmul(SK[:, :], kfm[:, kv * 3200 + 3072:kv * 3200 + 3076],
                             qh[:, (p % 2) * QT:(p % 2 + 1) * QT],
                             start=True, stop=True).then_inc(P.sem["mm"], 1)
                    t.wait_ge(P.sem["act"], A0 + 13)
                    t.wait_ge(P.sem["dve"], D0 + 9)
                    for kc in range(12):
                        bt = qt * 4 + kc
                        vt_i = bt if bt < 8 else bt + 1
                        vcol = (vt_i * NKV + kv) * HS
                        t.matmul(Ys[p % 2][:, :], vbuf[:, vcol:vcol + HS],
                                 pbuf[:, kc * QT:(kc + 1) * QT],
                                 start=(kc == 0), stop=False).then_inc(P.sem["mm"], 1)
                    t.matmul(Ys[p % 2][:, :],
                             vbuf[0:4, (8 * NKV + kv) * HS:(8 * NKV + kv + 1) * HS],
                             pbuf[0:4, 12 * QT:13 * QT],
                             start=False, stop=True).then_inc(P.sem["mm"], 1)
                    for kc in range(12):
                        t.matmul(DEN[:, :], ones_c[:, :],
                                 pbuf[:, kc * QT:(kc + 1) * QT],
                                 start=(kc == 0), stop=False).then_inc(P.sem["mm"], 1)
                    t.matmul(DEN[:, :], ones_c[0:4, :], pbuf[0:4, 12 * QT:13 * QT],
                             start=False, stop=True).then_inc(P.sem["mm"], 1)
                    t.wait_ge(P.sem["dve"], D0 + 10)
                    t.matmul(BCp[:, :], ones_r[:, :], rows[:, QT:2 * QT],
                             start=True, stop=True).then_inc(P.sem["mm"], 1)
            P.n["mm"] = b_mm + NPAIR * PMM

            @blk.scalar
            def _(s):
                for p in range(NPAIR):
                    M0 = b_mm + p * PMM
                    for kc in range(12):
                        s.wait_ge(P.sem["mm"], M0 + kc + 1)
                        s.activation(pbuf[:, kc * QT:(kc + 1) * QT],
                                     Ss[kc % 3][:, :], AF.Exp,
                                     bias=bsh[:, :], scale=SCALE_ATT) \
                            .then_inc(P.sem["act"], 1)
                    s.wait_ge(P.sem["mm"], M0 + 13)
                    s.activation(pbuf[0:4, 12 * QT:13 * QT], SK[:, :], AF.Exp,
                                 bias=bsh[0:4, :], scale=SCALE_ATT) \
                        .then_inc(P.sem["act"], 1)
            P.n["act"] = b_act + NPAIR * PACT

            @blk.vector
            def _(v_):
                for p in range(NPAIR):
                    h, qt = p // NQT, p % NQT
                    A0 = b_act + p * PACT
                    M0 = b_mm + p * PMM
                    for kc in MASKED:
                        v_.wait_ge(P.sem["act"], A0 + kc + 1)
                        mcol = kc if kc < 4 else (kc - 8) + 4
                        v_.tensor_mul(pbuf[:, kc * QT:(kc + 1) * QT],
                                      pbuf[:, kc * QT:(kc + 1) * QT],
                                      mls[:, mcol * QT:(mcol + 1) * QT]) \
                            .then_inc(P.sem["dve"], 1)
                    v_.wait_ge(P.sem["act"], A0 + 13)
                    v_.tensor_mul(pbuf[0:4, 12 * QT:13 * QT],
                                  pbuf[0:4, 12 * QT:13 * QT],
                                  snk[0:4, qt * QT:(qt + 1) * QT]) \
                        .then_inc(P.sem["dve"], 1)
                    v_.wait_ge(P.sem["mm"], M0 + 39)
                    cv = Ch(v_, P)
                    cv.scalar_tensor_tensor(rows[:, 0:QT],
                                            niv[:, qt * QT:(qt + 1) * QT],
                                            em12[:, 1:2], DEN[:, :],
                                            ALU.mult, ALU.add)
                    v_.reciprocal(rows[:, QT:2 * QT], rows[:, 0:QT]) \
                        .then_inc(P.sem["dve"], 1)
                    v_.wait_ge(P.sem["mm"], M0 + 40)
                    cv.tensor_copy(denb[:, :], BCp[:, :])
                    if p >= 4:
                        v_.wait_ge(P.sem["st"], b_st + 16 * (p - 3))
                    v_.tensor_mul(ybuf[:, (p % 4) * QT:(p % 4 + 1) * QT],
                                  Ys[p % 2][:, :], denb[:, :]).then_inc(P.sem["dve"], 1)
            P.n["dve"] = b_dve + NPAIR * PDVE

            @blk.sync
            def _(sy):
                for p in range(NPAIR):
                    h, qt = p // NQT, p % NQT
                    sy.wait_ge(P.sem["dve"], b_dve + p * PDVE + 11)
                    if p > 0:
                        sy.wait_ge(P.sem["st"], b_st + 16 * p)
                    sy.dma_start(out=y[h * HS:(h + 1) * HS, qt * QT:(qt + 1) * QT],
                                 in_=ybuf[:, (p % 4) * QT:(p % 4 + 1) * QT]) \
                        .then_inc(P.sem["st"], 16)
                sy.wait_ge(P.sem["st"], b_st + 16 * NPAIR)
            P.n["st"] = b_st + NPAIR * 16



def emit_phase_route(nc, P, dr, debug=False):
    xn1 = dr("xn1", [D, TPC], BF16, kind="ExternalInput")
    wr = dr("wr", [D, NE], F32, kind="ExternalInput")
    cmask = dr("cmask", [8, 1], F32, kind="ExternalInput")
    sdst0 = dr("sdst0", [TPC, 1], I32, kind="ExternalOutput")
    sdst1 = dr("sdst1", [TPC, 1], I32, kind="ExternalOutput")
    gidx0 = dr("gidx0", [TPC, 1], I32, kind="ExternalOutput")
    gidx1 = dr("gidx1", [TPC, 1], I32, kind="ExternalOutput")
    cw0 = dr("cw0", [TPC, 1], F32, kind="ExternalOutput")
    cw1 = dr("cw1", [TPC, 1], F32, kind="ExternalOutput")
    cnt_in = nc.dram_tensor("cnt_in", [1, NE], F32)
    cnt_out = nc.dram_tensor("cnt_out", [NCORE, NE], F32)
    if debug:
        dbg_lg = dr("dbg_lg", [128, 16 * NE], F32, kind="ExternalOutput")
        dbg_oh = dr("dbg_oh", [128, 16 * NE], F32, kind="ExternalOutput")
        dbg_cum = dr("dbg_cum", [128, 16 * NE], F32, kind="ExternalOutput")
        dbg_oh2 = dr("dbg_oh2", [128, 16 * NE], F32, kind="ExternalOutput")
        dbg_cols = dr("dbg_cols", [128, 24 * 16], F32, kind="ExternalOutput")

    NT = TPC // 128
    stack = contextlib.ExitStack()
    xr = stack.enter_context(nc.sbuf_tensor("rt_x", [128, NDC * 128], BF16))
    wrs = stack.enter_context(nc.sbuf_tensor("rt_w", [128, NDC * NE], BF16))
    lg = stack.enter_context(nc.sbuf_tensor("rt_lg", [128, NT * NE], F32))
    oh1 = stack.enter_context(nc.sbuf_tensor("rt_o1", [128, NT * NE], F32))
    oh2 = stack.enter_context(nc.sbuf_tensor("rt_o2", [128, NT * NE], F32))
    bothb = stack.enter_context(nc.sbuf_tensor("rt_bb", [128, NT * NE], BF16))
    cum = stack.enter_context(nc.sbuf_tensor("rt_cm", [128, NT * NE], F32))
    cols = stack.enter_context(nc.sbuf_tensor("rt_cl", [128, 24 * NT], F32))
    t8 = stack.enter_context(nc.sbuf_tensor("rt_t8", [128, 2 * NE], F32))
    iota8 = stack.enter_context(nc.sbuf_tensor("rt_i8", [128, NE], F32))
    im9 = stack.enter_context(nc.sbuf_tensor("rt_i9", [128, NE], F32))
    zrow = stack.enter_context(nc.sbuf_tensor("rt_zr", [128, 1], F32))
    triu = stack.enter_context(nc.sbuf_tensor("rt_tr", [128, 128], BF16))
    ones_c = stack.enter_context(nc.sbuf_tensor("rt_oc", [128, 1], BF16))
    ones_rf = stack.enter_context(nc.sbuf_tensor("rt_of", [1, 128], F32))
    carry = stack.enter_context(nc.sbuf_tensor("rt_cr", [1, NE], F32))
    cnts = stack.enter_context(nc.sbuf_tensor("rt_ct", [8, NE], F32))
    cmk = stack.enter_context(nc.sbuf_tensor("rt_ck", [8, 1], F32))
    base_bc = stack.enter_context(nc.sbuf_tensor("rt_bc", [128, NE], F32))
    icol = stack.enter_context(nc.sbuf_tensor("rt_ic", [128, 4], I32))

    CW1, CW2, POS0, POS1, DM, E0, E1, GP0, GP1, V0, V1, P0C, P1C, SD0, SD1, \
        OK0, OK1, GI0, GI1, T0, T1, T2, T3, T4 = range(24)

    def cl(tc, i):
        return cols[:, (i * NT + tc):(i * NT + tc) + 1]

    with stack:
        def init(g):
            g.iota(iota8[:, :], [[1, NE]], channel_multiplier=0,
                   allow_small_or_imprecise_dtypes=True)
            g.iota(im9[:, :], [[1, NE]], base=-9, channel_multiplier=0,
                   allow_small_or_imprecise_dtypes=True)
            g.memset(zrow[:, :], float(ZROW))
            g.iota(triu[:, :], [[1, 128]], channel_multiplier=-1,
                   allow_small_or_imprecise_dtypes=True)
            g.memset(ones_c[:, :], 1.0)
            g.memset(ones_rf[:, :], 1.0)
            g.memset(carry[:, :], 0.0)
        simple_block(nc, "rt_init", gpsimd=init)
        simple_block(nc, "rt_init2", vector=lambda v: (
            v.tensor_scalar(triu[:, :], triu[:, :], 1.0, None, ALU.is_ge),))

        load_block(nc, P, [(wrs[:, c * NE:(c + 1) * NE], wr[c * 128:(c + 1) * 128, :])
                           for c in range(NDC)]
                   + [(cmk[:, :], cmask[:, :])])
        jobs = []
        for tc in range(NT):
            loads = [(xr[:, :].rearrange("p (c t) -> p c t", c=NDC),
                      csrc(xn1, 0, NDC, tc * 128, 128))]
            jobs.append(Job(
                kchunks=[(xr[:, c * 128:(c + 1) * 128], wrs[:, c * NE:(c + 1) * NE])
                         for c in range(NDC)],
                evac=lambda e, ps, out: e.tensor_copy(out, ps),
                store=None, loads=loads, ld_gate=tc - 1, ow=NE))
        pipe_mm(nc, P, "rt_lgp", jobs, NE, out_dt=F32, nps=3, evac_engine="vector",
                persistent_out=lambda j: lg[:, j * NE:(j + 1) * NE])

        def tops(v):
            v = Ch(v, P)
            for tc in range(NT):
                L = lg[:, tc * NE:(tc + 1) * NE]
                v.reduce_max(cl(tc, T0), L, AX)
                v.tensor_scalar(t8[:, 0:NE], L, cl(tc, T0), None, ALU.is_equal)
                v.tensor_mul(t8[:, NE:2 * NE], t8[:, 0:NE], im9[:, :])
                v.tensor_scalar_add(t8[:, NE:2 * NE], t8[:, NE:2 * NE], 9.0)
                v.tensor_reduce(cl(tc, T1), t8[:, NE:2 * NE], AX, ALU.min)
                v.tensor_scalar(oh1[:, tc * NE:(tc + 1) * NE], iota8[:, :],
                                cl(tc, T1), None, ALU.is_equal)
                v.scalar_tensor_tensor(t8[:, 0:NE], oh1[:, tc * NE:(tc + 1) * NE],
                                       -1e38, L, ALU.mult, ALU.add)
                v.reduce_max(cl(tc, T2), t8[:, 0:NE], AX)
                v.tensor_scalar(t8[:, NE:2 * NE], t8[:, 0:NE],
                                cl(tc, T2), None, ALU.is_equal)
                v.tensor_mul(t8[:, 0:NE], t8[:, NE:2 * NE], im9[:, :])
                v.tensor_scalar_add(t8[:, 0:NE], t8[:, 0:NE], 9.0)
                v.tensor_reduce(cl(tc, T3), t8[:, 0:NE], AX, ALU.min)
                v.tensor_scalar(oh2[:, tc * NE:(tc + 1) * NE], iota8[:, :],
                                cl(tc, T3), None, ALU.is_equal)
                v.tensor_sub(cl(tc, DM), cl(tc, T0), cl(tc, T2))
                v.tensor_add(t8[:, 0:NE], oh1[:, tc * NE:(tc + 1) * NE],
                             oh2[:, tc * NE:(tc + 1) * NE])
                v.tensor_copy(bothb[:, tc * NE:(tc + 1) * NE], t8[:, 0:NE])
                v.tensor_mul(t8[:, 0:NE], oh1[:, tc * NE:(tc + 1) * NE], iota8[:, :])
                v.reduce_sum(cl(tc, E0), t8[:, 0:NE], AX)
                v.tensor_mul(t8[:, 0:NE], oh2[:, tc * NE:(tc + 1) * NE], iota8[:, :])
                v.reduce_sum(cl(tc, E1), t8[:, 0:NE], AX)
        simple_block(nc, "rt_tops", vector=tops)
        simple_block(nc, "rt_sig", scalar=lambda s: [
            s.activation(cl(tc, CW1), cl(tc, DM), AF.Sigmoid) for tc in range(NT)])
        simple_block(nc, "rt_cw2", vector=lambda v: [
            Ch(v, P).tensor_scalar(cl(tc, CW2), cl(tc, CW1), -1.0, 1.0,
                                   ALU.mult, ALU.add)
            for tc in range(NT)])

        for tc in range(NT):
            with (
                nc.psum_tensor(f"rt_p1{tc}", [128, NE], F32) as p1,
                nc.psum_tensor(f"rt_p2{tc}", [128, NE], F32) as p2,
                nc.psum_tensor(f"rt_p3{tc}", [1, NE], F32) as p3,
            ):
                def mm1(t, tc=tc, p1=p1, p2=p2, p3=p3):
                    t.matmul(p2[:, :], ones_rf[:, :], carry[:, :],
                             start=True, stop=True)
                    t.matmul(p1[:, :], triu[:, :], bothb[:, tc * NE:(tc + 1) * NE],
                             start=True, stop=True)
                    t.matmul(p3[:, :], ones_c[:, :], bothb[:, tc * NE:(tc + 1) * NE],
                             start=True, stop=True)
                simple_block(nc, f"rt_cu{tc}", tensor=mm1)

                def ev1(v, tc=tc, p1=p1, p2=p2, p3=p3):
                    v = Ch(v, P)
                    v.tensor_copy(t8[:, 0:NE], p2[:, :])
                    v.scalar_tensor_tensor(cum[:, tc * NE:(tc + 1) * NE],
                                           p1[:, :], 1.0, t8[:, 0:NE],
                                           ALU.mult, ALU.add)
                    v.tensor_add(carry[:, :], carry[:, :], p3[:, :])
                simple_block(nc, f"rt_cv{tc}", vector=ev1)

        def posb(v):
            v = Ch(v, P)
            for tc in range(NT):
                O1 = oh1[:, tc * NE:(tc + 1) * NE]
                O2 = oh2[:, tc * NE:(tc + 1) * NE]
                CU = cum[:, tc * NE:(tc + 1) * NE]
                v.tensor_mul(t8[:, 0:NE], O1, CU)
                v.reduce_sum(cl(tc, POS0), t8[:, 0:NE], AX)
                v.tensor_add(t8[:, 0:NE], CU, O1)
                v.tensor_mul(t8[:, 0:NE], t8[:, 0:NE], O2)
                v.reduce_sum(cl(tc, POS1), t8[:, 0:NE], AX)
        simple_block(nc, "rt_pos", vector=posb)

        store_block(nc, P, [(cnt_in[:, :], carry[:, :])])
        with nc.Block() as blk:
            def ccb(gg):
                gg.collective_compute(
                    "AllGather", ALU.bypass, replica_groups=[list(range(NCORE))],
                    ins=[cnt_in.ap().opt()], outs=[cnt_out.ap().opt()]) \
                    .then_inc(P.sem["cc"])
                P.n["cc"] += 1
                gg.wait_ge(P.sem["cc"], P.n["cc"])
            blk.gpsimd(ccb)
        load_block(nc, P, [(cnts[:, :], cnt_out[:, :])])
        with (
            nc.psum_tensor("rt_pb", [1, NE], F32) as pb,
            nc.psum_tensor("rt_pb2", [128, NE], F32) as pb2,
        ):
            simple_block(nc, "rt_bs", tensor=lambda t: (
                t.matmul(pb[:, :], cmk[:, :], cnts[:, :], start=True, stop=True),))
            simple_block(nc, "rt_bse", vector=lambda v: (
                v.tensor_copy(carry[:, :], pb[:, :]),))
            simple_block(nc, "rt_bs2", tensor=lambda t: (
                t.matmul(pb2[:, :], ones_rf[:, :], carry[:, :], start=True, stop=True),))
            simple_block(nc, "rt_bse2", vector=lambda v: (
                v.tensor_copy(base_bc[:, :], pb2[:, :]),))

        def finb(v):
            v = Ch(v, P)
            for tc in range(NT):
                O1 = oh1[:, tc * NE:(tc + 1) * NE]
                O2 = oh2[:, tc * NE:(tc + 1) * NE]
                v.tensor_mul(t8[:, 0:NE], O1, base_bc[:, :])
                v.reduce_sum(cl(tc, T0), t8[:, 0:NE], AX)
                v.tensor_add(cl(tc, GP0), cl(tc, T0), cl(tc, POS0))
                v.tensor_mul(t8[:, 0:NE], O2, base_bc[:, :])
                v.reduce_sum(cl(tc, T1), t8[:, 0:NE], AX)
                v.tensor_add(cl(tc, GP1), cl(tc, T1), cl(tc, POS1))
                v.tensor_single_scalar(cl(tc, V0), cl(tc, GP0), float(CGLB), ALU.is_lt)
                v.tensor_single_scalar(cl(tc, V1), cl(tc, GP1), float(CGLB), ALU.is_lt)
                v.tensor_mul(cl(tc, CW1), cl(tc, CW1), cl(tc, V0))
                v.tensor_mul(cl(tc, CW2), cl(tc, CW2), cl(tc, V1))
                v.tensor_scalar_min(cl(tc, P0C), cl(tc, POS0), float(ESTR - 1))
                v.tensor_scalar_min(cl(tc, P1C), cl(tc, POS1), float(ESTR - 1))
                v.scalar_tensor_tensor(cl(tc, SD0), cl(tc, E0), float(ESTR),
                                       cl(tc, P0C), ALU.mult, ALU.add)
                v.scalar_tensor_tensor(cl(tc, SD1), cl(tc, E1), float(ESTR),
                                       cl(tc, P1C), ALU.mult, ALU.add)
                v.tensor_single_scalar(cl(tc, T2), cl(tc, POS0), float(CAP), ALU.is_lt)
                v.tensor_mul(cl(tc, OK0), cl(tc, T2), cl(tc, V0))
                v.tensor_single_scalar(cl(tc, T3), cl(tc, POS1), float(CAP), ALU.is_lt)
                v.tensor_mul(cl(tc, OK1), cl(tc, T3), cl(tc, V1))
                v.tensor_scalar_add(cl(tc, T4), cl(tc, SD0), -float(ZROW))
                v.tensor_mul(cl(tc, T4), cl(tc, T4), cl(tc, OK0))
                v.tensor_scalar_add(cl(tc, GI0), cl(tc, T4), float(ZROW))
                v.tensor_scalar_add(cl(tc, T4), cl(tc, SD1), -float(ZROW))
                v.tensor_mul(cl(tc, T4), cl(tc, T4), cl(tc, OK1))
                v.tensor_scalar_add(cl(tc, GI1), cl(tc, T4), float(ZROW))
        simple_block(nc, "rt_fin", vector=finb)

        if debug:
            store_block(nc, P, [(dbg_lg[:, :], lg[:, :]), (dbg_oh[:, :], oh1[:, :]),
                                (dbg_cum[:, :], cum[:, :]), (dbg_cols[:, :], cols[:, :]),
                                (dbg_oh2[:, :], oh2[:, :])])
        b_dve = P.n["dve"]
        b_st0 = P.n["st"]
        with nc.Block() as blk:
            def conv(v):
                for tc in range(NT):
                    if tc > 0:
                        v.wait_ge(P.sem["st"], b_st0 + 6 * 16 * tc)
                    v.tensor_copy(icol[:, 0:1], cl(tc, SD0))
                    v.tensor_copy(icol[:, 1:2], cl(tc, SD1))
                    v.tensor_copy(icol[:, 2:3], cl(tc, GI0))
                    v.tensor_copy(icol[:, 3:4], cl(tc, GI1)).then_inc(P.sem["dve"], 1)
            blk.vector(conv)
            P.n["dve"] = b_dve + NT

            def stb(sy):
                for tc in range(NT):
                    sy.wait_ge(P.sem["dve"], b_dve + tc + 1)
                    if tc > 0:
                        sy.wait_ge(P.sem["st"], P.n["st"] + 6 * 16 * tc)
                    r = slice(tc * 128, (tc + 1) * 128)
                    sy.dma_start(out=sdst0[r, :], in_=icol[:, 0:1]).then_inc(P.sem["st"], 16)
                    sy.dma_start(out=sdst1[r, :], in_=icol[:, 1:2]).then_inc(P.sem["st"], 16)
                    sy.dma_start(out=gidx0[r, :], in_=icol[:, 2:3]).then_inc(P.sem["st"], 16)
                    sy.dma_start(out=gidx1[r, :], in_=icol[:, 3:4]).then_inc(P.sem["st"], 16)
                    sy.dma_start(out=cw0[r, :], in_=cl(tc, CW1)).then_inc(P.sem["st"], 16)
                    sy.dma_start(out=cw1[r, :], in_=cl(tc, CW2)).then_inc(P.sem["st"], 16)
                sy.wait_ge(P.sem["st"], P.n["st"] + 6 * 16 * NT)
            blk.sync(stb)
            P.n["st"] += 6 * 16 * NT



def emit_phase_moe(nc, P, dr, debug=False):
    xn1t = dr("xn1t", [TPC, D], BF16, kind="ExternalInput")
    sdst0 = dr("sdst0", [TPC, 1], I32, kind="ExternalInput")
    sdst1 = dr("sdst1", [TPC, 1], I32, kind="ExternalInput")
    gidx0 = dr("gidx0", [TPC, 1], I32, kind="ExternalInput")
    gidx1 = dr("gidx1", [TPC, 1], I32, kind="ExternalInput")
    cw0 = dr("cw0", [TPC, 1], F32, kind="ExternalInput")
    cw1 = dr("cw1", [TPC, 1], F32, kind="ExternalInput")
    wug = dr("wug", [NE, D, MEXP], F32, kind="ExternalInput")
    wdn = dr("wdn", [NE, IEXP, D], F32, kind="ExternalInput")
    xa2 = dr("xa2", [D, TPC], F32, kind="ExternalInput")
    out = dr("out", [D, TPC], F32, kind="ExternalOutput")

    buf = nc.dram_tensor("moe_buf", [ZROW + 1, D], BF16)
    obuf = nc.dram_tensor("moe_obuf", [ZROW + 1, D], BF16)
    if debug:
        dbuf = dr("dbuf", [ZROW + 1, D], BF16, kind="ExternalOutput")
        dobuf = dr("dobuf", [ZROW + 1, D], BF16, kind="ExternalOutput")

    NT = TPC // 128

    stack = contextlib.ExitStack()
    xrow = stack.enter_context(nc.sbuf_tensor("mo_x", [128, 2 * D], BF16))
    idxc = stack.enter_context(nc.sbuf_tensor("mo_ix", [128, 4], I32))
    cwc = stack.enter_context(nc.sbuf_tensor("mo_cw", [128, 2], F32))
    zr = stack.enter_context(nc.sbuf_tensor("mo_zr", [1, D], BF16))
    idn = stack.enter_context(nc.sbuf_tensor("mo_id", [128, 128], BF16))

    with stack:
        simple_block(nc, "mo_i0", gpsimd=lambda g: (
            g.iota(idn[:, :], [[1, 128]], channel_multiplier=-1,
                   allow_small_or_imprecise_dtypes=True),
            g.memset(zr[:, :], 0.0)))
        simple_block(nc, "mo_i1", vector=lambda v: (
            v.tensor_single_scalar(idn[:, :], idn[:, :], 0.0, ALU.is_equal),))
        store_block(nc, P, [(obuf[ZROW:ZROW + 1, :], zr[:, :])])

        with nc.Block() as blk:
            b_ld = P.n["ld"]

            def disp(g):
                n = b_ld
                for tc in range(NT):
                    sl = tc % 2
                    g.wait_ge(P.sem["ld"], n)
                    g.dma_start(out=xrow[:, sl * D:(sl + 1) * D],
                                in_=xn1t[tc * 128:(tc + 1) * 128, :]) \
                        .then_inc(P.sem["ld"], 16)
                    g.dma_start(out=idxc[:, 0:1],
                                in_=sdst0[tc * 128:(tc + 1) * 128, :]) \
                        .then_inc(P.sem["ld"], 16)
                    g.dma_start(out=idxc[:, 1:2],
                                in_=sdst1[tc * 128:(tc + 1) * 128, :]) \
                        .then_inc(P.sem["ld"], 16)
                    n += 48
                    g.wait_ge(P.sem["ld"], n)
                    g.indirect_dma_start(
                        out=buf[:, :],
                        out_offset=bass.IndirectOffsetOnAxis(idxc[:, 0:1], 0),
                        in_=xrow[:, sl * D:(sl + 1) * D], in_offset=None) \
                        .then_inc(P.sem["ld"], 16)
                    g.indirect_dma_start(
                        out=buf[:, :],
                        out_offset=bass.IndirectOffsetOnAxis(idxc[:, 1:2], 0),
                        in_=xrow[:, sl * D:(sl + 1) * D], in_offset=None) \
                        .then_inc(P.sem["ld"], 16)
                    n += 32
                g.wait_ge(P.sem["ld"], n)
            blk.gpsimd(disp)
            P.n["ld"] += NT * 5 * 16

        NSG = CAP // 512
        NMC = MEXP // 128
        NIC = IEXP // 128
        for e in range(NE):
            estk = contextlib.ExitStack()
            wugs = estk.enter_context(nc.sbuf_tensor(f"mo_wu{e}", [128, NDC * MEXP], BF16))
            wdns = estk.enter_context(nc.sbuf_tensor(f"mo_wd{e}", [128, NIC * D], BF16))
            bfm = estk.enter_context(nc.sbuf_tensor(f"mo_bf{e}", [128, NDC * 512], BF16))
            hfm = estk.enter_context(nc.sbuf_tensor(f"mo_hf{e}", [128, NIC * 512], BF16))
            hup = estk.enter_context(nc.sbuf_tensor(f"mo_hu{e}", [128, NMC * 512], BF16))
            with estk:
                load_block(nc, P, [
                    (wugs[:, dc * MEXP:(dc + 1) * MEXP],
                     wug[e, dc * 128:(dc + 1) * 128, :]) for dc in range(NDC)
                ] + [
                    (wdns[:, ic * D:(ic + 1) * D],
                     wdn[e, ic * 128:(ic + 1) * 128, :]) for ic in range(NIC)])
                for sg in range(NSG):
                    with nc.Block() as blk:
                        b_st2 = P.n["st"]

                        def tload(sy, e=e, sg=sg, b_st2=b_st2):
                            for dc in range(NDC):
                                sy.dma_start(
                                    out=bfm[:, dc * 512:(dc + 1) * 512],
                                    in_=buf[e * ESTR + sg * 512:e * ESTR + (sg + 1) * 512,
                                            dc * 128:(dc + 1) * 128],
                                    transpose=True).then_inc(P.sem["st"], 16)
                            sy.wait_ge(P.sem["st"], b_st2 + NDC * 16)
                        blk.sync(tload)
                        P.n["st"] += NDC * 16

                    jobs = []
                    for mc in range(NMC):
                        def mk_ev(mc=mc):
                            def ev(eng, ps, o):
                                if mc < NIC:
                                    return eng.activation(
                                        hup[:, mc * 512:(mc + 1) * 512], ps, AF.Silu)
                                return eng.activation(
                                    hup[:, mc * 512:(mc + 1) * 512], ps, AF.Copy)
                            return ev
                        jobs.append(Job(
                            kchunks=[(wugs[:, dc * MEXP + mc * 128:
                                          dc * MEXP + (mc + 1) * 128],
                                      bfm[:, dc * 512:(dc + 1) * 512])
                                     for dc in range(NDC)],
                            evac=mk_ev(), store=None))
                    pipe_mm(nc, P, f"mo_ug{e}_{sg}", jobs, 512, out_dt=BF16, nps=3,
                            evac_engine="scalar", persistent_out=lambda j: None)
                    simple_block(nc, f"mo_cb{e}_{sg}", vector=lambda v: [
                        v.tensor_mul(hfm[:, ic * 512:(ic + 1) * 512],
                                     hup[:, ic * 512:(ic + 1) * 512],
                                     hup[:, (NIC + ic) * 512:(NIC + ic + 1) * 512])
                        for ic in range(NIC)])

                    jobs = []
                    for sc in range(4):
                        for ob in range(4):
                            jobs.append(Job(
                                kchunks=[(hfm[:, ic * 512 + sc * 128:
                                              ic * 512 + (sc + 1) * 128],
                                          wdns[:, ic * D + ob * 512:
                                               ic * D + (ob + 1) * 512])
                                         for ic in range(NIC)],
                                evac=lambda eng, ps, o: ecopy(eng, o, ps),
                                store=obuf[e * ESTR + sg * 512 + sc * 128:
                                           e * ESTR + sg * 512 + (sc + 1) * 128,
                                           ob * 512:(ob + 1) * 512]))
                    pipe_mm(nc, P, f"mo_dn{e}_{sg}", jobs, 512, out_dt=BF16,
                            nps=3, nout=4, evac_engine="scalar")

        if debug:
            with nc.Block() as blk:
                def dmp(g):
                    n0 = P.n["ld"]
                    for rr in range(0, ZROW + 1, 2048):
                        rw = min(2048, ZROW + 1 - rr)
                        g.wait_ge(P.sem["ld"], P.n["ld"])
                        g.dma_start(out=dbuf[rr:rr + rw, :], in_=buf[rr:rr + rw, :]) \
                            .then_inc(P.sem["ld"], 16)
                        P.n["ld"] += 16
                        g.wait_ge(P.sem["ld"], P.n["ld"])
                        g.dma_start(out=dobuf[rr:rr + rw, :], in_=obuf[rr:rr + rw, :]) \
                            .then_inc(P.sem["ld"], 16)
                        P.n["ld"] += 16
                    g.wait_ge(P.sem["ld"], P.n["ld"])
                blk.gpsimd(dmp)
        cstk = contextlib.ExitStack()
        gath = cstk.enter_context(nc.sbuf_tensor("mo_g", [128, 2 * D], BF16))
        ytm = cstk.enter_context(nc.sbuf_tensor("mo_y", [128, D], BF16))
        fstg = cstk.enter_context(nc.sbuf_tensor("mo_fs", [128, D], F32))
        xat = cstk.enter_context(nc.sbuf_tensor("mo_xa", [128, 2 * D], F32))
        cstk.__enter__() if False else None
        for tc in range(NT):
            load_block(nc, P, [
                (idxc[:, 2:3], gidx0[tc * 128:(tc + 1) * 128, :]),
                (idxc[:, 3:4], gidx1[tc * 128:(tc + 1) * 128, :]),
                (cwc[:, 0:1], cw0[tc * 128:(tc + 1) * 128, :]),
                (cwc[:, 1:2], cw1[tc * 128:(tc + 1) * 128, :]),
            ] + [(xat[:, 0:NDC * 128].rearrange("p (c t) -> p c t", c=NDC),
                  csrc(xa2, 0, NDC, tc * 128, 128))])
            with nc.Block() as blk:
                b_ld2 = P.n["ld"]

                def gat(g):
                    g.indirect_dma_start(
                        out=gath[:, 0:D], out_offset=None, in_=obuf[:, :],
                        in_offset=bass.IndirectOffsetOnAxis(idxc[:, 2:3], 0)) \
                        .then_inc(P.sem["ld"], 16)
                    g.wait_ge(P.sem["ld"], b_ld2 + 16)
                    g.indirect_dma_start(
                        out=gath[:, D:2 * D], out_offset=None, in_=obuf[:, :],
                        in_offset=bass.IndirectOffsetOnAxis(idxc[:, 3:4], 0)) \
                        .then_inc(P.sem["ld"], 16)
                    g.wait_ge(P.sem["ld"], b_ld2 + 32)
                blk.gpsimd(gat)
                P.n["ld"] = b_ld2 + 32
            def _cmb(v, tc=tc):
                v = Ch(v, P)
                v.tensor_scalar_mul(fstg[:, 0:D], gath[:, 0:D], cwc[:, 0:1])
                v.scalar_tensor_tensor(fstg[:, 0:D], gath[:, D:2 * D], cwc[:, 1:2],
                                       fstg[:, 0:D], ALU.mult, ALU.add)
                v.tensor_copy(ytm[:, :], fstg[:, 0:D])
            simple_block(nc, f"mo_y{tc}", vector=_cmb)
            with (
                nc.psum_tensor(f"mo_yp{tc}a", [128, 1024], BF16) as pta,
                nc.psum_tensor(f"mo_yp{tc}b", [128, 1024], BF16) as ptb,
                nc.Block() as blk,
            ):
                pss = [pta, ptb]
                b_mm, b_dve, b_st3 = P.n["mm"], P.n["dve"], P.n["st"]

                def tb2(t):
                    for dc in range(NDC):
                        if dc >= 2:
                            t.wait_ge(P.sem["dve"], b_dve + dc - 1)
                        t.transpose(pss[dc % 2][:, 0:128],
                                    ytm[:, dc * 128:(dc + 1) * 128],
                                    idn[:, :]).then_inc(P.sem["mm"], 1)
                blk.tensor(tb2)
                P.n["mm"] = b_mm + NDC

                def te2(v_):
                    for dc in range(NDC):
                        v_.wait_ge(P.sem["mm"], b_mm + dc + 1)
                        v_.tensor_tensor(fstg[:, dc * 128:(dc + 1) * 128],
                                         pss[dc % 2][:, 0:128],
                                         xat[:, dc * 128:(dc + 1) * 128],
                                         ALU.add).then_inc(P.sem["dve"], 1)
                blk.vector(te2)
                P.n["dve"] = b_dve + NDC

                def st2(sy):
                    for dc in range(NDC):
                        sy.wait_ge(P.sem["dve"], b_dve + dc + 1)
                        if dc > 0:
                            sy.wait_ge(P.sem["st"], b_st3 + 16 * dc)
                        sy.dma_start(
                            out=out[dc * 128:(dc + 1) * 128, tc * 128:(tc + 1) * 128],
                            in_=fstg[:, dc * 128:(dc + 1) * 128]) \
                            .then_inc(P.sem["st"], 16)
                    sy.wait_ge(P.sem["st"], b_st3 + 16 * NDC)
                blk.sync(st2)
                P.n["st"] = b_st3 + 16 * NDC
        cstk.close()




def _passthrough_dr(nc):
    def dr(name, shape, dt, kind=None):
        if kind is None:
            return nc.dram_tensor(name, shape, dt)
        return nc.dram_tensor(name, shape, dt, kind=kind)
    return dr


def _mk_build(emit):
    @prog
    def b(nc, P, **kw):
        emit(nc, P, _passthrough_dr(nc), **kw)
    return b


build_wo_res = _mk_build(emit_phase_wo_res)
build_ln0 = _mk_build(emit_phase_ln0)
build_qkvg = _mk_build(emit_phase_qkvg)
build_scan = _mk_build(emit_phase_scan)
build_wp_res = _mk_build(emit_phase_wp_res)
build_ffn0 = _mk_build(emit_phase_ffn0)
build_down0 = _mk_build(emit_phase_down0)
build_ln1 = _mk_build(emit_phase_ln1)
build_qkv1 = _mk_build(emit_phase_qkv1)
build_attn1 = _mk_build(emit_phase_attn1)
build_shared = _mk_build(emit_phase_shared)
build_shdown = _mk_build(emit_phase_shdown)
build_route = _mk_build(emit_phase_route)
build_moe = _mk_build(emit_phase_moe)



def emit_phase_halo(nc, P, dr):
    k1 = dr("k1", [TPC, NKV * HS], BF16, kind="ExternalInput")
    v1 = dr("v1", [TPC, NKV * HS], BF16, kind="ExternalInput")
    kidx = dr("kidx", [1152, 1], I32, kind="ExternalInput")
    vidx = dr("vidx", [1152, 1], I32, kind="ExternalInput")
    khalo = dr("khalo", [1152, NKV * HS], BF16, kind="ExternalOutput")
    vhalo = dr("vhalo", [1152, NKV * HS], BF16, kind="ExternalOutput")
    contrib = nc.dram_tensor("hx_contrib", [2064, 512], BF16)
    agh = nc.dram_tensor("hx_agh", [GRP * 2064, 512], BF16)

    stack = contextlib.ExitStack()
    zt = stack.enter_context(nc.sbuf_tensor("hx_z", [128, 512], BF16))
    idxs = stack.enter_context(nc.sbuf_tensor("hx_i", [128, 2], I32))
    row = stack.enter_context(nc.sbuf_tensor("hx_r", [128, 512], BF16))
    with stack:
        with nc.Block() as blk:
            def cb(g):
                g.memset(zt[:, :], 0.0)
                for dst0, src, rws in [(0, k1, (1024, 2048)), (1024, k1, (0, 4)),
                                       (1028, v1, (0, 4)), (1032, v1, (1024, 2048))]:
                    g.wait_ge(P.sem["ld"], P.n["ld"])
                    g.dma_start(out=contrib[dst0:dst0 + (rws[1] - rws[0]), :],
                                in_=src[rws[0]:rws[1], :]).then_inc(P.sem["ld"], 16)
                    P.n["ld"] += 16
                g.wait_ge(P.sem["ld"], P.n["ld"])
                g.dma_start(out=contrib[2056:2064, :], in_=zt[0:8, :]) \
                    .then_inc(P.sem["ld"], 16)
                P.n["ld"] += 16
                g.wait_ge(P.sem["ld"], P.n["ld"])
                g.collective_compute(
                    "AllGather", ALU.bypass,
                    replica_groups=[[0, 1, 2, 3], [4, 5, 6, 7]],
                    ins=[contrib.ap().opt()], outs=[agh.ap().opt()]) \
                    .then_inc(P.sem["cc"])
                P.n["cc"] += 1
                g.wait_ge(P.sem["cc"], P.n["cc"])
                for j in range(9):
                    for which, (idxd, outd, col) in enumerate(
                            [(kidx, khalo, 0), (vidx, vhalo, 1)]):
                        g.wait_ge(P.sem["ld"], P.n["ld"])
                        g.dma_start(out=idxs[:, col:col + 1],
                                    in_=idxd[j * 128:(j + 1) * 128, :]) \
                            .then_inc(P.sem["ld"], 16)
                        P.n["ld"] += 16
                        g.wait_ge(P.sem["ld"], P.n["ld"])
                        g.indirect_dma_start(
                            out=row[:, :], out_offset=None, in_=agh[:, :],
                            in_offset=bass.IndirectOffsetOnAxis(idxs[:, col:col + 1], 0)) \
                            .then_inc(P.sem["ld"], 16)
                        P.n["ld"] += 16
                        g.wait_ge(P.sem["ld"], P.n["ld"])
                        g.dma_start(out=outd[j * 128:(j + 1) * 128, :], in_=row[:, :]) \
                            .then_inc(P.sem["ld"], 16)
                        P.n["ld"] += 16
                g.wait_ge(P.sem["ld"], P.n["ld"])
            blk.gpsimd(cb)


_EI, _EO = "ExternalInput", "ExternalOutput"
FUSED_ALIASES = {
    "ln0": {"x": ("x", _EI), "xn": ("xn", None), "xnr": ("xnr", None)},
    "qkvg": {"xnr": ("xnr", None), "wq": ("l0wq", _EI), "wk": ("l0wk", _EI),
             "wv": ("l0wv", _EI), "wg": ("l0wg", _EI), "q": ("l0q", None),
             "k": ("l0k", None), "g": ("l0g", None), "v": ("l0v", None)},
    "scan": {"q": ("l0q", None), "k": ("l0k", None), "v": ("l0v", None),
             "g": ("l0g", None), "rankm": ("rankm", _EI), "ident": ("ident", _EI),
             "trilm": ("trilm", _EI), "og": ("og", None)},
    "wp_res": {"og": ("og", None), "wp": ("l0wp", _EI), "x": ("x", _EI),
               "xmid": ("xmid", None)},
    "ffn0": {"xn": ("xn", None), "wgate": ("l0wgate", _EI), "wup": ("l0wup", _EI),
             "gs": ("l0gs", _EI), "us": ("l0us", _EI), "h": ("h0", None)},
    "down0": {"h": ("h0", None), "wdown": ("l0wdown", _EI), "ds": ("l0ds", _EI),
              "xmid": ("xmid", None), "x1": ("x1", None)},
    "ln1": {"x1": ("x1", None), "xn1": ("xn1", None), "xn1t": ("xn1t", None)},
    "qkv1": {"xn1": ("xn1", None), "wq": ("l1wq", _EI), "wk": ("l1wk", _EI),
             "wv": ("l1wv", _EI), "q1": ("q1", None), "k1": ("k1", None),
             "v1": ("v1", None)},
    "halo": {"k1": ("k1", None), "v1": ("v1", None), "kidx": ("kidx", _EI),
             "vidx": ("vidx", _EI), "khalo": ("khalo", None), "vhalo": ("vhalo", None)},
    "attn1": {"q1": ("q1", None), "k1": ("k1", None), "v1": ("v1", None),
              "khalo": ("khalo", None), "vhalo": ("vhalo", None),
              "bandml": ("bandml", _EI), "bandmr": ("bandmr", _EI),
              "sinkm": ("sinkm", _EI), "ninv": ("ninv", _EI), "y": ("y1", None)},
    "wo_res": {"y": ("y1", None), "wo": ("l1wo", _EI), "x1": ("x1", None),
               "xa": ("xa", None)},
    "shared": {"xn1": ("xn1", None), "wug": ("l1wsug", _EI), "h": ("hsh", None)},
    "shdown": {"h": ("hsh", None), "wdown": ("l1wsdn", _EI), "xa": ("xa", None),
               "xa2": ("xa2", None)},
    "route": {"xn1": ("xn1", None), "wr": ("l1wr", _EI), "cmask": ("cmask", _EI),
              "sdst0": ("sdst0", None), "sdst1": ("sdst1", None),
              "gidx0": ("gidx0", None), "gidx1": ("gidx1", None),
              "cw0": ("cw0", None), "cw1": ("cw1", None)},
    "moe": {"xn1t": ("xn1t", None), "sdst0": ("sdst0", None), "sdst1": ("sdst1", None),
            "gidx0": ("gidx0", None), "gidx1": ("gidx1", None), "cw0": ("cw0", None),
            "cw1": ("cw1", None), "wug": ("l1wug", _EI), "wdn": ("l1wdn", _EI),
            "xa2": ("xa2", None), "out": ("out", _EO)},
}

FUSED_ORDER = ["ln0", "qkvg", "scan", "wp_res", "ffn0", "down0", "ln1", "qkv1",
               "halo", "attn1", "wo_res", "shared", "shdown", "route", "moe"]


@prog
def build_fused(nc, P):
    cache = {}

    def mk_dr(amap):
        def dr(name, shape, dt, kind=None):
            cname, ckind = amap[name]
            if cname in cache:
                return cache[cname]
            if ckind is None:
                t = nc.dram_tensor(cname, list(shape), dt)
            else:
                t = nc.dram_tensor(cname, list(shape), dt, kind=ckind)
            cache[cname] = t
            return t
        return dr

    for ph in FUSED_ORDER:
        emit = globals()[f"emit_phase_{ph}"]
        emit(nc, P, mk_dr(FUSED_ALIASES[ph]))


def make_halo_idx():
    kidxs, vidxs = [], []
    for c in range(NCORE):
        r = c % GRP
        Z = 2056
        ki = np.full(1152, Z, np.int32)
        vi = np.full(1152, Z, np.int32)
        if r > 0:
            ki[0:1024] = (r - 1) * 2064 + np.arange(1024)
            vi[0:1024] = (r - 1) * 2064 + 1032 + np.arange(1024)
        ki[1024:1028] = 1024 + np.arange(4)
        vi[1024:1028] = 1028 + np.arange(4)
        kidxs.append(np.ascontiguousarray(ki.reshape(-1, 1)))
        vidxs.append(np.ascontiguousarray(vi.reshape(-1, 1)))
    return kidxs, vidxs


_FUSED_NC = None


def kernel_fused(L0_attn_x, L0_attn_Wq, L0_attn_Wk, L0_attn_Wv, L0_attn_Wg,
                 L0_attn_Wp, L0_ffn_gate_w, L0_ffn_up_w, L0_ffn_down_w,
                 L0_ffn_gate_s, L0_ffn_up_s, L0_ffn_down_s, L1_attn_Wq,
                 L1_attn_Wk, L1_attn_Wv, L1_attn_Wo, L1_ffn_router,
                 L1_ffn_w_up_gate, L1_ffn_w_down, L1_ffn_shared_w_up_gate,
                 L1_ffn_shared_w_down):
    def T(w):
        return np.ascontiguousarray(np.asarray(w).T)

    def C(w):
        return np.ascontiguousarray(np.asarray(w))

    global _FUSED_NC
    xs = shard_x(np.asarray(L0_attn_x))
    rankm, ident, trilm = host_scan_inputs()
    bandml, bandmr, sinkms, ninvs = make_attn_inputs()
    kidxs, vidxs = make_halo_idx()
    shared_w = {
        "l0wq": T(L0_attn_Wq), "l0wk": T(L0_attn_Wk), "l0wv": T(L0_attn_Wv),
        "l0wg": T(L0_attn_Wg), "l0wp": T(L0_attn_Wp),
        "l0wgate": T(L0_ffn_gate_w), "l0wup": T(L0_ffn_up_w),
        "l0wdown": T(L0_ffn_down_w),
        "l0gs": C(np.asarray(L0_ffn_gate_s).reshape(-1, 1)),
        "l0us": C(np.asarray(L0_ffn_up_s).reshape(-1, 1)),
        "l0ds": C(np.asarray(L0_ffn_down_s).reshape(-1, 1)),
        "l1wq": T(L1_attn_Wq), "l1wk": T(L1_attn_Wk), "l1wv": T(L1_attn_Wv),
        "l1wo": T(L1_attn_Wo), "l1wr": C(L1_ffn_router),
        "l1wug": C(L1_ffn_w_up_gate), "l1wdn": C(L1_ffn_w_down),
        "l1wsug": C(L1_ffn_shared_w_up_gate), "l1wsdn": C(L1_ffn_shared_w_down),
        "ident": ident, "trilm": trilm, "bandml": bandml, "bandmr": bandmr,
    }
    in_maps = []
    for c in range(NCORE):
        cm = np.zeros((NCORE, 1), np.float32)
        cm[:c, 0] = 1.0
        in_maps.append({
            "x": xs[c], "rankm": rankm[c], "sinkm": sinkms[c], "ninv": ninvs[c],
            "cmask": cm, "kidx": kidxs[c], "vidx": vidxs[c], **shared_w})
    if _FUSED_NC is None:
        _FUSED_NC = build_fused()
    res = _run(_FUSED_NC, in_maps)
    out = np.empty((2, 8192, D), np.float32)
    for c in range(NCORE):
        b, qr = c // GRP, c % GRP
        out[b, qr * TPC:(qr + 1) * TPC, :] = res[c]["out"].T
    return out



def _run(nc, in_maps):
    return run_bass_kernel_spmd(nc, in_maps, core_ids=list(range(NCORE))).results


def shard_x(x):
    out = []
    for c in range(NCORE):
        b, qr = c // GRP, c % GRP
        out.append(np.ascontiguousarray(x[b, qr * TPC:(qr + 1) * TPC, :].T))
    return out



def make_attn_inputs():
    i = np.arange(QT)
    bandml = np.zeros((4, 128, QT), np.float32)
    bandmr = np.zeros((4, 128, QT), np.float32)
    for kc in range(4):
        for rr in range(128):
            bandml[kc, rr, :] = (kc * 128 + rr >= i + 1)
    for j in range(4):
        for rr in range(128):
            bandmr[j, rr, :] = ((8 + j) * 128 + rr <= 1024 + i)
    bandml = bandml.astype(BF)
    bandmr = bandmr.astype(BF)
    sinkms, ninvs = [], []
    for c in range(NCORE):
        first = (c % GRP == 0)
        sm = np.ones((NQT, 4, QT), np.float32)
        nv = np.zeros((NQT, QT), np.float32)
        if first:
            for qt in range(NQT):
                q0 = qt * QT
                qv = q0 + i
                for js in range(4):
                    sm[qt, js, :] = np.where(
                        qv < 1024, 0.0,
                        np.where(qv <= 1027, (qv - 1023 > js).astype(np.float32), 1.0))
                lim = 1023 - q0
                nv[qt, :] = np.maximum(0, np.minimum(1024 + i, lim) - i)
        sinkms.append(sm.astype(BF))
        ninvs.append(nv)
    return bandml, bandmr, sinkms, ninvs


def make_halo(k_all, v_all):
    kh, vh = [], []
    for c in range(NCORE):
        b, qr = c // GRP, c % GRP
        k = np.zeros((1152, NKV * HS), BF)
        v = np.zeros((1152, NKV * HS), BF)
        if qr > 0:
            k[0:1024] = k_all[c - 1][TPC - 1024:]
            v[0:1024] = v_all[c - 1][TPC - 1024:]
        k[1024:1028] = k_all[b * GRP][0:4]
        v[1024:1028] = v_all[b * GRP][0:4]
        kh.append(k)
        vh.append(v)
    return kh, vh


def host_scan_inputs():
    rankm = []
    for c in range(NCORE):
        row = np.zeros((128, GRP), np.float32)
        row[:, :c % GRP] = 1.0
        rankm.append(row)
    ident = np.eye(128, dtype=np.float32).astype(BF)
    trilm = np.zeros((4, 128, BS), np.float32)
    for kc in range(4):
        for rr in range(128):
            trilm[kc, rr, kc * 128 + rr:] = 1.0
    return rankm, ident, trilm.astype(BF)


def kernel_unfused(L0_attn_x, L0_attn_Wq, L0_attn_Wk, L0_attn_Wv, L0_attn_Wg, L0_attn_Wp,
           L0_ffn_gate_w, L0_ffn_up_w, L0_ffn_down_w, L0_ffn_gate_s, L0_ffn_up_s,
           L0_ffn_down_s, L1_attn_Wq, L1_attn_Wk, L1_attn_Wv, L1_attn_Wo,
           L1_ffn_router, L1_ffn_w_up_gate, L1_ffn_w_down,
           L1_ffn_shared_w_up_gate, L1_ffn_shared_w_down):
    def T(w):
        return np.ascontiguousarray(np.asarray(w).T)

    def C(w):
        return np.ascontiguousarray(np.asarray(w))

    R = range(NCORE)
    xs = shard_x(np.asarray(L0_attn_x))

    r1 = _run(build_ln0(), [{"x": xs[c]} for c in R])
    wq, wk, wv, wg = T(L0_attn_Wq), T(L0_attn_Wk), T(L0_attn_Wv), T(L0_attn_Wg)
    r2 = _run(build_qkvg(), [
        {"xnr": r1[c]["xnr"], "wq": wq, "wk": wk, "wv": wv, "wg": wg} for c in R])
    rankm, ident, trilm = host_scan_inputs()
    r3 = _run(build_scan(), [
        {"q": r2[c]["q"], "k": r2[c]["k"], "v": r2[c]["v"], "g": r2[c]["g"],
         "rankm": rankm[c], "ident": ident, "trilm": trilm} for c in R])
    wp = T(L0_attn_Wp)
    r4 = _run(build_wp_res(), [
        {"og": r3[c]["og"], "wp": wp, "x": xs[c]} for c in R])
    wgate, wup = T(L0_ffn_gate_w), T(L0_ffn_up_w)
    gs = C(np.asarray(L0_ffn_gate_s).reshape(-1, 1))
    us = C(np.asarray(L0_ffn_up_s).reshape(-1, 1))
    r5 = _run(build_ffn0(), [
        {"xn": r1[c]["xn"], "wgate": wgate, "wup": wup, "gs": gs, "us": us}
        for c in R])
    wdown0 = T(L0_ffn_down_w)
    ds = C(np.asarray(L0_ffn_down_s).reshape(-1, 1))
    r6 = _run(build_down0(), [
        {"h": r5[c]["h"], "wdown": wdown0, "ds": ds, "xmid": r4[c]["xmid"]}
        for c in R])
    r7 = _run(build_ln1(), [{"x1": r6[c]["x1"]} for c in R])
    wq1, wk1, wv1 = T(L1_attn_Wq), T(L1_attn_Wk), T(L1_attn_Wv)
    r8 = _run(build_qkv1(), [
        {"xn1": r7[c]["xn1"], "wq": wq1, "wk": wk1, "wv": wv1} for c in R])
    kh, vh = make_halo([r8[c]["k1"] for c in R], [r8[c]["v1"] for c in R])
    bandml, bandmr, sinkms, ninvs = make_attn_inputs()
    r9 = _run(build_attn1(), [
        {"q1": r8[c]["q1"], "k1": r8[c]["k1"], "v1": r8[c]["v1"],
         "khalo": kh[c], "vhalo": vh[c], "bandml": bandml, "bandmr": bandmr,
         "sinkm": sinkms[c], "ninv": ninvs[c]} for c in R])
    wo = T(L1_attn_Wo)
    r10 = _run(build_wo_res(), [
        {"y": r9[c]["y"], "wo": wo, "x1": r6[c]["x1"]} for c in R])
    wsug = C(L1_ffn_shared_w_up_gate)
    r11 = _run(build_shared(), [{"xn1": r7[c]["xn1"], "wug": wsug} for c in R])
    wsdn = C(L1_ffn_shared_w_down)
    r12 = _run(build_shdown(), [
        {"h": r11[c]["h"], "wdown": wsdn, "xa": r10[c]["xa"]} for c in R])
    wr = C(L1_ffn_router)
    cmasks = []
    for c in R:
        cm = np.zeros((NCORE, 1), np.float32)
        cm[:c, 0] = 1.0
        cmasks.append(cm)
    r13 = _run(build_route(), [
        {"xn1": r7[c]["xn1"], "wr": wr, "cmask": cmasks[c]} for c in R])
    wug_a = C(L1_ffn_w_up_gate)
    wdn_a = C(L1_ffn_w_down)
    r14 = _run(build_moe(), [
        {"xn1t": r7[c]["xn1t"], "sdst0": r13[c]["sdst0"], "sdst1": r13[c]["sdst1"],
         "gidx0": r13[c]["gidx0"], "gidx1": r13[c]["gidx1"],
         "cw0": r13[c]["cw0"], "cw1": r13[c]["cw1"],
         "wug": wug_a, "wdn": wdn_a, "xa2": r12[c]["xa2"]} for c in R])

    out = np.empty((2, 2 * GRP * 0 + 8192, D), np.float32)
    for c in R:
        b, qr = c // GRP, c % GRP
        out[b, qr * TPC:(qr + 1) * TPC, :] = r14[c]["out"].T
    return out


kernel = kernel_fused
